# version 15
# baseline (speedup 1.0000x reference)
import numpy as np
import ml_dtypes
import concourse.bass as bass
import concourse.bacc as bacc
import concourse.mybir as mybir
from concourse import library_config
from concourse.bass_utils import run_bass_kernel_spmd

F32 = mybir.dt.float32
BF16 = mybir.dt.bfloat16
I16 = mybir.dt.int16
U8 = mybir.dt.uint8
AF = mybir.ActivationFunctionType
ALU = mybir.AluOpType
AX = mybir.AxisListType

FULL_CFG = dict(NW=16, WFN=4, WB=[0] * 8 + [1] * 8, W=256, D=1024, H=16, HD=64,
                E=8, FF=4096, CAP=352, REG=384, EPS=1e-5)
SMALL_CFG = dict(NW=4, WFN=2, WB=[0, 0, 1, 1], W=256, D=1024, H=16, HD=64,
                 E=8, FF=1024, CAP=96, REG=128, EPS=1e-5)


def _derived(cfg):
    d = dict(cfg)
    d["KC"] = cfg["D"] // 128
    d["TPW"] = cfg["W"] // 128
    d["OC"] = cfg["FF"] // 512
    d["NT"] = cfg["REG"] // 128
    d["S"] = cfg["E"] * cfg["REG"]
    d["SC"] = d["S"] // 128
    d["NROW"] = cfg["WFN"] * cfg["W"]
    d["RC"] = d["NROW"] // 128
    d["NTOK"] = cfg["NW"] * cfg["W"]
    d["HG"] = cfg["H"] // 4
    sizes, left = [], cfg["CAP"]
    for _ in range(d["NT"]):
        sizes.append(min(128, left))
        left -= sizes[-1]
    d["TSZ"] = sizes
    return d


class Sem:
    REGISTRY = None

    def __init__(self, nc, name):
        self.h = nc.alloc_semaphore(name)
        self.n = 0
        self.name = name
        if Sem.REGISTRY is not None:
            Sem.REGISTRY.append(self)

    def tok(self):
        return (self, self.n)


class Rec:
    ENGS = ("sync", "gpsimd", "vector", "scalar", "tensor")

    def __init__(self, nc):
        self.nc = nc
        self.ops = {e: [] for e in self.ENGS}

    def op(self, eng, build, after=(), inc=None, dma=False):
        waits = [(s, v) for (s, v) in after if v > 0]
        if inc is not None:
            inc.n += 16 if dma else 1
        self.ops[eng].append((waits, build, inc, 16 if dma else 1))
        return inc.tok() if inc is not None else None

    def emit(self, block):
        for eng_name in self.ENGS:
            ops = self.ops[eng_name]

            def body(eng, ops=ops):
                seen = {}
                for waits, build, inc, k in ops:
                    for s, v in waits:
                        if seen.get(s.name, 0) < v:
                            eng.wait_ge(s.h, v)
                            seen[s.name] = v
                    ins = build(eng)
                    if inc is not None:
                        ins.then_inc(inc.h, k)
            getattr(block, eng_name)(body)


class Arena:

    def __init__(self, nc, name, dtype, nelem):
        self.t = nc.alloc_sbuf_tensor(name, [128, nelem], dtype)
        self.dtype = dtype
        self.nelem = nelem
        self.off = 0
        self.marks = []
        self.peak = 0

    def alloc(self, *shape):
        n = int(np.prod(shape))
        off = self.off
        self.off += (n + 15) & ~15
        self.peak = max(self.peak, self.off)
        assert self.off <= self.nelem, f"arena overflow {self.off} > {self.nelem}"
        ap = self.t[:, off:off + n]
        if len(shape) > 1:
            names = [chr(97 + i) for i in range(len(shape))]
            ap = ap.rearrange("p (" + " ".join(names) + ") -> p " + " ".join(names),
                              **{nm: s for nm, s in zip(names, shape)})
        return ap

    def mark(self):
        self.marks.append(self.off)

    def reset(self):
        self.off = self.marks.pop()


def build_program(cfg, stage=99):
    import os
    KDROP = set(os.environ.get('KDROP', '').split(','))
    c = _derived(cfg)
    NW, WFN, WB, W, D, H, HD, E, FF = (c["NW"], c["WFN"], c["WB"], c["W"], c["D"],
                                       c["H"], c["HD"], c["E"], c["FF"])
    KC, CAP, REG, NT, S, SC = c["KC"], c["CAP"], c["REG"], c["NT"], c["S"], c["SC"]
    NROW, RC, NTOK, TPW, OC, HG, TSZ, EPS = (c["NROW"], c["RC"], c["NTOK"], c["TPW"],
                                             c["OC"], c["HG"], c["TSZ"], c["EPS"])
    QKW = D + 2 * HD
    JQK = QKW // 128
    QKW2 = QKW + 128
    JQK2 = JQK + 1
    SCALE = float(HD) ** -0.5

    nc = bacc.Bacc("TRN2", debug=False)
    Sem.REGISTRY = []

    x_in = nc.dram_tensor("x_rows", [NTOK, D], F32, kind="ExternalInput").ap()
    qw = nc.dram_tensor("qw", [D, D], F32, kind="ExternalInput").ap()
    kvw = nc.dram_tensor("kvw", [2 * HD, D], F32, kind="ExternalInput").ap()
    pjw = nc.dram_tensor("pjw", [D, D], F32, kind="ExternalInput").ap()
    pjb = nc.dram_tensor("pjb", [1, D], F32, kind="ExternalInput").ap()
    c2 = nc.dram_tensor("c2", [2, D], F32, kind="ExternalInput").ap()
    wmod = nc.dram_tensor("wmod", [D, 6 * D], F32, kind="ExternalInput").ap()
    w1 = nc.dram_tensor("w1", [E, D, FF], F32, kind="ExternalInput").ap()
    b1t = nc.dram_tensor("b1t", [FF, E], F32, kind="ExternalInput").ap()
    w2 = nc.dram_tensor("w2", [E, FF, D], F32, kind="ExternalInput").ap()
    b2 = nc.dram_tensor("b2", [E, D], F32, kind="ExternalInput").ap()
    gidx = nc.dram_tensor("gidx", [128, E * (REG // 16)], I16, kind="ExternalInput").ap()
    stok_d = nc.dram_tensor("stok", [128, SC], F32, kind="ExternalInput").ap()
    swt_d = nc.dram_tensor("swt", [128, SC], F32, kind="ExternalInput").ap()
    out = nc.dram_tensor("out_rows", [NTOK, D], F32, kind="ExternalOutput").ap()
    h2hb = nc.dram_tensor("h2hb", [NROW, D], BF16, kind="Internal").ap()
    xmhb = nc.dram_tensor("xmhb", [NROW, D], F32, kind="Internal").ap()

    AB = Arena(nc, "ab", BF16, 71 * 1024)
    FA = Arena(nc, "af", F32, 14 * 1024)
    gix = nc.alloc_sbuf_tensor("gix", [128, E * (REG // 16)], I16).ap().rearrange("p (e r) -> p e r", e=E)
    wm32 = nc.alloc_sbuf_tensor("wm32", [128, 4, 512], F32).ap()

    iota_f = FA.alloc(D)
    iota_c = FA.alloc(1)
    idn_f = FA.alloc(128)
    gbr = FA.alloc(D)
    qsc = FA.alloc(2, KC)
    km2 = FA.alloc(4)
    stokc = FA.alloc(SC)
    swc = FA.alloc(SC)
    b1c = FA.alloc(FF // 128, E)
    mvt = FA.alloc(2, TPW, 16)
    stt = FA.alloc(2, TPW, 2)
    idn_b = AB.alloc(128)
    ones_b = AB.alloc(256)
    colA = AB.alloc(1)
    colB = AB.alloc(1)
    A64 = AB.alloc(128)
    B64 = AB.alloc(128)
    ones_c = AB.alloc(1)
    Bw = AB.alloc(2, QKW2)
    Rr = AB.alloc(2, D)
    A2r = AB.alloc(D)
    B2r = AB.alloc(D)
    sac = AB.alloc(2, KC)

    ovb = AB.off
    wqkvT = AB.alloc(2, KC, QKW2)
    projT = AB.alloc(2, KC, D)
    qb2 = AB.off
    AB.mark(); FA.mark()
    att = {}
    att["ln1T"] = AB.alloc(KC, W)
    att["qkvT"] = AB.alloc(JQK2, W)
    att["E"] = AB.alloc(2, TPW, W)
    att["sqb"] = AB.alloc(JQK, W)
    att["attnT"] = AB.alloc(KC, W)
    att["h2b"] = AB.alloc(TPW, D)
    att["ln1"] = AB.alloc(TPW, D)
    vbuf = AB.alloc(TPW, HD + 1)
    att["xb"] = FA.alloc(2, TPW, D)
    att["xmo"] = FA.alloc(2, TPW, D)
    ubias = FA.alloc(1)
    zrs32 = FA.alloc(2, 256)
    AB.reset(); FA.reset()
    AB.mark(); FA.mark()
    stp = {}
    stp["wstage"] = AB.alloc(2, D)
    stp["wmodb"] = AB.alloc(4, 512)
    stp["silT"] = AB.alloc(KC, 2)
    stp["mrow1"] = AB.alloc(2, 6 * D)
    stp["pjbs"] = AB.alloc(D)
    stp["grep"] = AB.alloc(D)
    stp["rbf2"] = AB.alloc(D)
    stp["sil"] = FA.alloc(D)
    stp["mrow2"] = FA.alloc(3 * D)
    stp["crow"] = FA.alloc(2 * D)
    stp["c2s"] = FA.alloc(D)
    stp["scols"] = FA.alloc(4, D)
    AB.reset(); FA.reset()
    AB.off = ovb
    moe = {}
    moe["gath"] = AB.alloc(NT, D)
    moe["gathT"] = AB.alloc(2, KC, REG)
    moe["w1s"] = AB.alloc(2, KC, 512)
    moe["w2s"] = AB.alloc(2, 4, D)
    moe["hb"] = AB.alloc(2, 4, CAP)
    moe["out2"] = AB.alloc(E * NT, D)
    moe["b2e"] = AB.alloc(2, D)
    moe["Mt"] = AB.alloc(2, D)
    moe["xo"] = FA.alloc(2, D)
    moe["xmi"] = FA.alloc(2, D)

    PS = nc.alloc_psum_tensor("PS", [128, 4096], F32)

    def psf(lo, *shape):
        n = int(np.prod(shape))
        ap = PS[:, lo:lo + n]
        if len(shape) > 1:
            names = [chr(97 + i) for i in range(len(shape))]
            ap = ap.rearrange("p (" + " ".join(names) + ") -> p " + " ".join(names),
                              **{nm: s for nm, s in zip(names, shape)})
        return ap

    R = Rec(nc)
    sv = Sem(nc, "sv")
    ss = Sem(nc, "ss")
    st_ = Sem(nc, "st")
    sg = Sem(nc, "sg")
    d_set = Sem(nc, "d_set")
    d_g = Sem(nc, "d_g")
    d_x0 = Sem(nc, "d_x0")
    d_x1 = Sem(nc, "d_x1")
    d_o0 = Sem(nc, "d_o0")
    d_o1 = Sem(nc, "d_o1")
    d_h2 = Sem(nc, "d_h2")
    d_w1 = Sem(nc, "d_w1")
    d_w2 = Sem(nc, "d_w2")
    d_gt = Sem(nc, "d_gt")
    d_b2 = Sem(nc, "d_b2")
    d_sep = Sem(nc, "d_sep")
    d_c2 = Sem(nc, "d_c2")
    d_xmp = [Sem(nc, "d_xm0"), Sem(nc, "d_xm1")]
    d_sc = Sem(nc, "d_sc")
    d_xi = [Sem(nc, "d_xia"), Sem(nc, "d_xib")]
    d_sm = Sem(nc, "d_sm")
    d_cast = [Sem(nc, "d_cast0"), Sem(nc, "d_cast1"), Sem(nc, "d_cast2")]
    d_stg = [Sem(nc, "d_stg0"), Sem(nc, "d_stg1"), Sem(nc, "d_stg2")]
    d_tt = [Sem(nc, "d_tt0"), Sem(nc, "d_tt1")]
    d_gw = [Sem(nc, "d_gw%d" % i) for i in range(6)]
    d_w1p = [Sem(nc, "d_w1a"), Sem(nc, "d_w1b")]
    d_w2p = [Sem(nc, "d_w2a"), Sem(nc, "d_w2b")]
    d_gtp = [Sem(nc, "d_gta"), Sem(nc, "d_gtb")]
    d_b2p = [Sem(nc, "d_b2a"), Sem(nc, "d_b2b")]


    def _early_exit(after_toks):
        R.op("sync", lambda s: s.dma_start(out=out[0:2, 0:D], in_=stp["c2s"][0:2, :]),
             after=list(after_toks), inc=d_o0, dma=True)
        if os.environ.get("KEXIT") == "min":
            R.op("sync", lambda s: s.nop(), after=[d_o0.tok()])
        else:
            R.op("sync", lambda s: s.nop(), after=[s_.tok() for s_ in Sem.REGISTRY if s_.n > 0])
        with nc.Block() as block:
            R.emit(block)
        nc.finalize()
        return nc, c

    if stage == -9:
        R.op("vector", lambda v: v.memset(gbr, 3.0), inc=sv)
        R.op("sync", lambda s: s.dma_start(out=out[0:128, 0:D], in_=gbr), after=[sv.tok()], inc=d_o0, dma=True)
        R.op("sync", lambda s: s.nop(), after=[d_o0.tok()])
        with nc.Block() as block:
            R.emit(block)
        nc.finalize()
        return nc, c
    t_iof = R.op("gpsimd", lambda g: g.iota(iota_f, pattern=[[1, D]], channel_multiplier=0,
                                            allow_small_or_imprecise_dtypes=True), inc=sg)
    t_ioc = R.op("gpsimd", lambda g: g.iota(iota_c, pattern=[[0, 1]], channel_multiplier=1,
                                            allow_small_or_imprecise_dtypes=True), after=[t_iof], inc=sg)
    R.op("gpsimd", lambda g: g.load_library(library_config.mlp), after=[t_ioc], inc=None)
    t_lib = t_ioc
    t_idb = R.op("vector", lambda v: v.tensor_scalar(idn_b, iota_f[:, 0:128], iota_c, None, ALU.is_equal),
                 after=[t_iof, t_ioc], inc=sv)
    t_idf = R.op("vector", lambda v: v.tensor_copy(idn_f, idn_b), after=[t_idb], inc=sv)
    t_on = R.op("vector", lambda v: v.memset(ones_b, 1.0), inc=sv)
    t_onc = R.op("vector", lambda v: v.memset(ones_c, 1.0), inc=sv)
    t_cA = R.op("vector", lambda v: v.tensor_scalar(colA, iota_c, 64.0, None, ALU.is_lt),
                after=[t_ioc], inc=sv)
    t_cB = R.op("vector", lambda v: v.tensor_scalar(colB, iota_c, 64.0, None, ALU.is_ge),
                after=[t_ioc], inc=sv)
    t_A64 = R.op("vector", lambda v: v.tensor_scalar(A64, iota_f[:, 0:128], 64.0, None, ALU.is_lt),
                 after=[t_iof], inc=sv)
    t_B64 = R.op("vector", lambda v: v.tensor_scalar(B64, iota_f[:, 0:128], 64.0, None, ALU.is_ge),
                 after=[t_iof], inc=sv)

    if stage == -8:
        R.op("sync", lambda s: s.dma_start(out=out[0:128, 0:128], in_=idn_f), after=[t_idf], inc=d_o0, dma=True)
        R.op("sync", lambda s: s.nop(), after=[d_o0.tok()])
        with nc.Block() as block:
            R.emit(block)
        nc.finalize()
        return nc, c
    t_c2 = R.op("sync", lambda s: s.dma_start(out=stp["c2s"][0:2, :], in_=c2), inc=d_c2, dma=True)
    if stage == -7:
        R.op("sync", lambda s: s.dma_start(out=out[0:2, 0:D], in_=stp["c2s"][0:2, :]),
             after=[t_c2, t_idf, t_cB], inc=d_o0, dma=True)
        R.op("sync", lambda s: s.nop(), after=[d_o0.tok()])
        with nc.Block() as block:
            R.emit(block)
        nc.finalize()
        return nc, c
    if "pjb" in KDROP:
        t_pjb = t_lib
    else:
        t_pjb = R.op("gpsimd", lambda g: g.dma_start(out=stp["pjbs"][0:1, :], in_=pjb),
                     after=[t_lib], inc=d_g, dma=True)
    if "smloads" not in KDROP:
        R.op("sync", lambda s: s.dma_start(out=b1c, in_=b1t.rearrange("(m p) e -> p m e", p=128)),
             inc=d_sm, dma=True)
        R.op("sync", lambda s: s.dma_start(out=gix, in_=gidx.rearrange("p (e r) -> p e r", e=E)),
             inc=d_sm, dma=True)
        R.op("sync", lambda s: s.dma_start(out=stokc, in_=stok_d), inc=d_sm, dma=True)
        R.op("sync", lambda s: s.dma_start(out=swc, in_=swt_d), inc=d_sm, dma=True)
    t_b1 = t_gi = t_stk = t_sw = d_sm.tok()

    if stage == -6:
        return _early_exit([t_c2, t_sm := d_sm.tok()])
    chunks = ([("q", j) for j in range(KC)] + [("kv", 0)] + [("pj", j) for j in range(KC)])
    asm_state = dict(last=None, tr=None, ld_cons={}, lds={})

    def emit_asm_load(ci):
        kind, j = chunks[ci]
        srcw = qw if kind == "q" else (kvw if kind == "kv" else pjw)
        asm_state["lds"][ci] = R.op("gpsimd", lambda g, ci=ci, srcw=srcw, j=j: g.dma_start(
            out=stp["wstage"][:, ci % 2, :], in_=srcw[128 * j:128 * (j + 1), :]),
            after=[t_lib] + ([asm_state["ld_cons"][ci - 2]] if ci - 2 in asm_state["ld_cons"] else []),
            inc=d_cast[ci % 2], dma=True)

    def emit_asm_chunk(ci, extra):
        kind, j = chunks[ci]
        t_ld = asm_state["lds"][ci]
        for k in range(KC):
            slot = 3072 if k % 2 == 0 else 3584
            asm_state["tr"] = R.op("tensor", lambda t, ci=ci, k=k, slot=slot: t.matmul(
                psf(slot, 128), stp["wstage"][:, ci % 2, 128 * k:128 * (k + 1)], idn_b,
                start=True, stop=True),
                after=[t_ld, t_idb] + extra + ([asm_state["tr"]] if asm_state["tr"] else []) +
                      ([asm_state["last"]] if asm_state["last"] else []), inc=st_)
            if kind == "q":
                dst = wqkvT[:, 0, k, 128 * j:128 * (j + 1)]
            elif kind == "kv":
                dst = wqkvT[:, 0, k, D:D + 128]
            else:
                dst = projT[:, 0, k, 128 * j:128 * (j + 1)]
            asm_state["last"] = R.op("vector", lambda v, dst=dst, slot=slot: v.tensor_copy(dst, psf(slot, 128)),
                                     after=[asm_state["tr"]] + ([asm_state["last"]] if asm_state["last"] else []), inc=sv)
            if kind == "kv":
                asm_state["last"] = R.op("vector", lambda v, k=k: v.tensor_copy(
                    wqkvT[:, 0, k, QKW:QKW + HD], wqkvT[:, 0, k, D:D + HD]),
                    after=[asm_state["last"]], inc=sv)
                asm_state["last"] = R.op("vector", lambda v, k=k: v.tensor_copy(
                    wqkvT[:, 0, k, QKW + HD:QKW2], wqkvT[:, 0, k, D:D + HD]),
                    after=[asm_state["last"]], inc=sv)
        asm_state["ld_cons"][ci] = asm_state["tr"]

    emit_asm_load(0)
    emit_asm_load(1)

    t_sil = R.op("scalar", lambda s: s.activation(stp["sil"][0:2, :], stp["c2s"][0:2, :], AF.Silu),
                 after=[t_c2], inc=ss)
    t_slT = []
    for k in range(KC):
        tt = R.op("tensor", lambda t, k=k: t.transpose(
            psf(3584, 2), stp["sil"][0:2, 128 * k:128 * (k + 1)], idn_f[0:2, 0:2]),
            after=[t_sil, t_idf] + ([t_slT[-1]] if t_slT else []), inc=st_)
        t_slT.append(R.op("vector", lambda v, k=k: v.tensor_copy(stp["silT"][:, k, :], psf(3584, 2)),
                          after=[tt], inc=sv))
    t_silT = t_slT[-1]

    t_mrow = []
    asm_next = [0]
    WQE = ("sync", "scalar", "gpsimd")
    mm_of_chunk = {}
    cast_of_chunk = {}
    NCW = 3 * D // 512
    for half in range(2):
        last_mm = None
        for k in range(KC):
            for ch in range(NCW):
                ci = (half * KC + k) * NCW + ch
                tl = R.op(WQE[ci % 3], lambda g, k=k, half=half, ch=ch, ci=ci: g.dma_start(
                    out=wm32[:, ci % 4, :],
                    in_=wmod[128 * k:128 * (k + 1),
                             half * 3 * D + ch * 512:half * 3 * D + (ch + 1) * 512]),
                    after=([cast_of_chunk[ci - 4]] if ci - 4 in cast_of_chunk else []),
                    inc=d_gw[(ci % 3) * 2 + (ci // 3) % 2], dma=True)
                tc = R.op("vector", lambda v, ci=ci: v.tensor_copy(
                    stp["wmodb"][:, ci % 4, :], wm32[:, ci % 4, :]),
                    after=[tl] + ([mm_of_chunk[ci - 4]] if ci - 4 in mm_of_chunk else []), inc=sv)
                cast_of_chunk[ci] = tc
                last_mm = R.op("tensor", lambda t, k=k, ch=ch, ci=ci: t.matmul(
                    psf(512 * ch, 512)[0:2, :], stp["silT"][:, k, :],
                    stp["wmodb"][:, ci % 4, :],
                    start=(k == 0), stop=(k == KC - 1)),
                    after=[tc, t_silT] + (t_mrow if (half == 1 and k == 0 and ch == 0) else []), inc=st_)
                mm_of_chunk[ci] = last_mm
                if ci % 6 == 5 and asm_next[0] < len(chunks):
                    ai = asm_next[0]
                    emit_asm_chunk(ai, [t_silT])
                    if ai + 2 < len(chunks):
                        emit_asm_load(ai + 2)
                    asm_next[0] += 1
        tev = R.op("scalar", lambda s, half=half: s.copy(
            stp["mrow2"][0:2, :], psf(0, 3 * D)[0:2, :]),
            after=[last_mm] + t_mrow, inc=ss)
        tsp = R.op("gpsimd", lambda g, half=half: g.dma_start(
            out=stp["mrow1"][0:1, :, half * 3 * D:(half + 1) * 3 * D],
            in_=stp["mrow2"][0:2, :]), after=[tev, t_lib], inc=d_sep, dma=True)
        if half == 0:
            tsp = R.op("sync", lambda s: s.dma_start(
                out=stp["scols"][0:1, :, :].rearrange("p a d -> p (a d)"),
                in_=stp["mrow2"][0:2, 0:2 * D]),
                after=[tev, tsp], inc=d_sc, dma=True)
        t_mrow.append(tsp)
    while asm_next[0] < len(chunks):
        ai = asm_next[0]
        emit_asm_chunk(ai, [t_silT])
        if ai + 2 < len(chunks):
            emit_asm_load(ai + 2)
        asm_next[0] += 1
    t_asm = asm_state["last"]
    t_sep = t_mrow[-1]

    if stage == -4:
        return _early_exit([t_sep, t_mrow[-1]])

    def mr(b, i):
        return stp["mrow1"][0:1, b, i * D:(i + 1) * D]

    t_clamp = None
    for b in range(2):
        for gi_ in (2, 5):
            tsg = R.op("vector", lambda v, b=b, gi_=gi_: v.tensor_scalar(
                stp["crow"][0:1, 0:D], mr(b, gi_), 0.0, 2.0, ALU.is_ge, ALU.mult),
                after=[t_sep] + ([t_clamp] if t_clamp else []), inc=sv)
            tsa0 = R.op("scalar", lambda s, b=b, gi_=gi_: s.activation(
                stp["crow"][0:1, D:2 * D], mr(b, gi_), AF.Abs),
                after=[t_sep, tsg], inc=ss)
            tsa = R.op("vector", lambda v: v.tensor_scalar(
                stp["crow"][0:1, D:2 * D], stp["crow"][0:1, D:2 * D], 0.25, None, ALU.max),
                after=[tsa0], inc=sv)
            t_clamp = R.op("vector", lambda v, b=b, gi_=gi_: v.scalar_tensor_tensor(
                mr(b, gi_), stp["crow"][0:1, 0:D], -1.0, stp["crow"][0:1, D:2 * D],
                ALU.add, ALU.mult),
                after=[tsg, tsa], inc=sv)

    t_a2 = R.op("vector", lambda v: v.tensor_scalar(stp["rbf2"][0:1, :], mr(0, 4), 1.0, None, ALU.add),
                after=[t_sep], inc=sv)

    t_cols = None
    tt = None
    for b in range(2):
        for k in range(KC):
            tt = R.op("tensor", lambda t, b=b, k=k: t.transpose(
                psf(3840 + 8 * b + k, 1)[:, :], stp["scols"][0:1, 2 * b + 1, 128 * k:128 * (k + 1)],
                idn_f[0:1, 0:1]),
                after=[t_sep, t_asm] + ([tt] if tt else []), inc=st_)
            tt = R.op("tensor", lambda t, b=b, k=k: t.transpose(
                psf(3856 + 8 * b + k, 1)[:, :], stp["scols"][0:1, 2 * b, 128 * k:128 * (k + 1)],
                idn_f[0:1, 0:1]),
                after=[tt], inc=st_)
    t_qsc = R.op("vector", lambda v: v.tensor_scalar(
        qsc.rearrange("p a b -> p (a b)"), psf(3840, 16), 1.0, None, ALU.add),
        after=[tt], inc=sv)
    t_cols = R.op("vector", lambda v: v.tensor_copy(
        sac.rearrange("p a b -> p (a b)"), psf(3856, 16)),
        after=[tt, t_qsc], inc=sv)

    def rep(row_ap, dst_ap, after):
        tm = None
        for hf in range(D // 512):
            tm = R.op("tensor", lambda t, hf=hf: t.matmul(
                psf(3072 + 512 * hf, 512), ones_b[0:1, 0:128], row_ap[:, 512 * hf:512 * (hf + 1)],
                start=True, stop=True),
                after=list(after) + [t_on, t_cols, t_asm], inc=st_)
        return R.op("scalar", lambda s: s.copy(dst_ap, psf(3072, D)), after=[tm], inc=ss)

    t_A2 = rep(stp["rbf2"][0:1, :], A2r, [t_a2])
    t_B2 = rep(mr(0, 3), B2r, [t_sep, t_A2])
    t_gb = rep(mr(0, 5), gbr, [t_clamp, t_B2])

    if stage == -3:
        return _early_exit([t_cols, t_clamp, t_a2])
    if stage == -2:
        return _early_exit([t_A2, t_B2, t_gb, t_cols, t_clamp])
    BWOF = (0, 1536)
    t_bw_last = None
    for b in range(2):
        for jj in range(QKW // 128):
            for k in range(KC):
                t_bw_last = R.op("tensor", lambda t, b=b, jj=jj, k=k: t.matmul(
                    psf(BWOF[b], QKW)[0:1, 128 * jj:128 * (jj + 1)], sac[:, b, k:k + 1],
                    wqkvT[:, 0, k, 128 * jj:128 * (jj + 1)],
                    start=(k == 0), stop=(k == KC - 1)),
                    after=[t_asm, t_cols] + ([t_bw_last] if t_bw_last else []), inc=st_)
    t_wq = []
    for k in range(KC):
        t_wq.append(R.op("vector", lambda v, k=k: v.tensor_scalar(
            wqkvT[:, 1, k, :], wqkvT[:, 0, k, :], qsc[:, 1, k:k + 1], None, ALU.mult),
            after=[t_asm, t_cols, t_bw_last], inc=sv))
        t_wq.append(R.op("vector", lambda v, k=k: v.tensor_scalar(
            wqkvT[:, 0, k, :], wqkvT[:, 0, k, :], qsc[:, 0, k:k + 1], None, ALU.mult),
            after=[t_wq[-1]], inc=sv))
    t_wqkv = t_wq[-1]
    t_Bw = []
    for b in range(2):
        te = R.op("scalar", lambda s, b=b: s.copy(Bw[0:1, b, 0:QKW], psf(BWOF[b], QKW)[0:1, :]),
                  after=[t_bw_last], inc=ss)
        te = R.op("vector", lambda v, b=b: v.tensor_copy(Bw[0:1, b, QKW:QKW + HD], Bw[0:1, b, D:D + HD]),
                  after=[te], inc=sv)
        te = R.op("vector", lambda v, b=b: v.tensor_copy(Bw[0:1, b, QKW + HD:QKW2], Bw[0:1, b, D:D + HD]),
                  after=[te], inc=sv)
        t_Bw.append(te)

    if stage == -1:
        return _early_exit([t_wqkv, t_Bw[1]])
    t_pj = []
    tge = None
    for b in (1, 0):
        tg = None
        for hf in range(D // 512):
            tg = R.op("tensor", lambda t, b=b, hf=hf: t.matmul(
                psf(3072 + 512 * hf, 512), ones_b[0:1, 0:128], mr(b, 2)[:, 512 * hf:512 * (hf + 1)],
                start=True, stop=True),
                after=[t_clamp, t_on, t_asm] + ([t_pj[-1]] if t_pj else [t_gb]), inc=st_)
        tge = R.op("scalar", lambda s: s.copy(stp["grep"], psf(3072, D)), after=[tg], inc=ss)
        for k in range(KC):
            t_pj.append(R.op("vector", lambda v, k=k, b=b: v.tensor_tensor(
                projT[:, b, k, :], projT[:, 0, k, :], stp["grep"], ALU.mult),
                after=[tge, t_asm] + ([t_pj[-1]] if t_pj else []), inc=sv))
    t_projd = t_pj[-1]
    t_R = [R.op("vector", lambda v, b=b: v.tensor_tensor(
        Rr[0:1, b, :], mr(b, 2), stp["pjbs"][0:1, :], ALU.mult),
        after=[t_clamp, t_pjb], inc=sv) for b in range(2)]

    t_setup = [t_wqkv, t_projd, t_Bw[1], t_R[1], t_A2, t_B2, t_gb, t_on,
               t_onc, t_cA, t_cB, t_b1, t_gi, t_stk, t_sw]

    d_xp = (d_x0, d_x1)
    d_op = (d_o0, d_o1)
    t_xload = {}
    for w in range(min(2, NW)):
        t_xload[w] = R.op("sync", lambda s, w=w: s.dma_start(
            out=att["xb"][:, w % 2, :, :],
            in_=x_in[W * w:W * (w + 1), :].rearrange("(t p) d -> p t d", p=128)),
            after=t_setup, inc=d_xp[w % 2], dma=True)

    def emit_ln_dve(xt, dst, ti, sel, aft, aft_sq, aft_aff):
        tb1a = R.op("vector", lambda v, xt=xt, ti=ti, sel=sel: v.bn_stats(
            mvt[:, sel, ti, 0:6], xt[:, 0:D // 2]), after=aft, inc=sv)
        tb1 = R.op("vector", lambda v, xt=xt, ti=ti, sel=sel: v.bn_stats(
            mvt[:, sel, ti, 6:12], xt[:, D // 2:D]), after=[tb1a], inc=sv)
        tb2 = R.op("vector", lambda v, ti=ti, sel=sel: v.bn_aggr(
            stt[:, sel, ti, :], mvt[:, sel, ti, 0:12]), after=[tb1], inc=sv)
        tv1 = R.op("vector", lambda v, ti=ti, sel=sel: v.tensor_scalar(
            mvt[:, sel, ti, 12:13], stt[:, sel, ti, 1:2], EPS, None, ALU.add),
            after=[tb2] + aft_sq, inc=sv)
        ts1 = R.op("scalar", lambda s, ti=ti, sel=sel: s.activation(
            mvt[:, sel, ti, 13:14], mvt[:, sel, ti, 12:13], AF.Sqrt), after=[tv1], inc=ss)
        tv2 = R.op("vector", lambda v, ti=ti, sel=sel: v.reciprocal(
            mvt[:, sel, ti, 12:13], mvt[:, sel, ti, 13:14]), after=[ts1], inc=sv)
        tv3 = R.op("vector", lambda v, ti=ti, sel=sel: v.tensor_scalar(
            mvt[:, sel, ti, 13:14], stt[:, sel, ti, 0:1], mvt[:, sel, ti, 12:13], -1.0,
            ALU.mult, ALU.mult), after=[tv2], inc=sv)
        ta = R.op("vector", lambda v, ti=ti, sel=sel, xt=xt, dst=dst: v.tensor_scalar(
            dst, xt, mvt[:, sel, ti, 12:13], mvt[:, sel, ti, 13:14], ALU.mult, ALU.add),
            after=[tv3] + aft_aff, inc=sv)
        return ta, ts1

    t_comb_prev = None
    t_ate_prev = None
    t_free_xmo = {0: [], 1: []}
    t_xmf = []
    t_h2_store = None
    t_win_last = None
    P = None
    h2_pend = None
    h2_sq, h2_rc, h2_sq2, h2_rc2 = {}, {}, {}, {}
    t_sqtok, t_rctok = {}, {}

    t_aff_n = []
    if NW > 0 and stage >= 1:
        for ti in range(TPW):
            xt0 = att["xb"][:, 0, ti, :]
            ta0, tq0 = emit_ln_dve(xt0, att["ln1"][:, ti, :], ti, 0,
                                   [t_xload[0]] + t_setup, [], t_setup)
            t_aff_n.append(ta0)
            t_sqtok[ti] = tq0

    for w in range(NW if stage >= 1 else 0):
        b = WB[w]
        ffn = w < WFN
        t_aff = t_aff_n
        t_lt = None
        for ti in range(TPW):
            for k in range(KC):
                slot = 0 if (ti * KC + k) % 2 == 0 else 512
                tt = R.op("tensor", lambda t, ti=ti, k=k, slot=slot: t.matmul(
                    psf(slot, 128), att["ln1"][:, ti, 128 * k:128 * (k + 1)], idn_b,
                    start=True, stop=True),
                    after=[t_aff[ti]] +
                          (([P["expB"]] if P else t_setup) if (ti == 0 and k == 0) else []) +
                          ([t_lt] if t_lt else []), inc=st_)
                t_lt = R.op("scalar", lambda s, ti=ti, k=k, slot=slot: s.copy(
                    att["ln1T"][:, k, 128 * ti:128 * (ti + 1)], psf(slot, 128)),
                    after=[tt], inc=ss)
        t_lnT = t_lt
        t_qkv_j = {}
        for j in range(JQK2):
            if P is None:
                jg = []
            elif j < 4:
                jg = [P["expB"]]
            elif j < 6:
                jg = [P["tze7"]]
            else:
                jg = [P["ate7"]]
            for k in range(KC):
                R.op("tensor", lambda t, j=j, k=k, b=b: t.matmul(
                    psf(j * W, W), wqkvT[:, b, k, 128 * j:128 * (j + 1)], att["ln1T"][:, k, :],
                    start=(k == 0), stop=False),
                    after=[t_lnT] + jg, inc=st_)
            t_qkv_j[j] = R.op("tensor", lambda t, j=j, b=b: t.matmul(
                psf(j * W, W), Bw[0:1, b, 128 * j:128 * (j + 1)], ones_b[0:1, 0:W],
                start=False, stop=True), after=[t_lnT], inc=st_)
        t_sqk = R.op("scalar", lambda s: s.activation(att["sqb"][0:HD, JQK - 1, :],
                                                      psf((JQK - 1) * W, W)[0:HD, :], AF.Square),
                     after=[t_qkv_j[JQK - 1], t_qkv_j[JQK]], inc=ss)
        t_ev1 = R.op("scalar", lambda s: s.copy(
            att["qkvT"][:, JQK - 1:JQK + 1, :].rearrange("p a b -> p (a b)"),
            psf((JQK - 1) * W, 2 * W)), after=[t_qkv_j[JQK - 1], t_qkv_j[JQK]], inc=ss)
        t_ev0 = R.op("scalar", lambda s: s.copy(
            att["qkvT"][:, 0:4, :].rearrange("p a b -> p (a b)"),
            psf(0, 4 * W)), after=[t_qkv_j[j] for j in range(4)], inc=ss)
        t_sqA = R.op("scalar", lambda s: s.activation(
            att["sqb"][:, 0:4, :].rearrange("p a b -> p (a b)"),
            att["qkvT"][:, 0:4, :].rearrange("p a b -> p (a b)"), AF.Square),
            after=[t_ev0], inc=ss)
        t_ev2 = R.op("scalar", lambda s: s.copy(
            att["qkvT"][:, 4:8, :].rearrange("p a b -> p (a b)"),
            psf(4 * W, 4 * W)), after=[t_qkv_j[j] for j in range(4, 8)], inc=ss)
        t_sqB = R.op("scalar", lambda s: s.activation(
            att["sqb"][:, 4:8, :].rearrange("p a b -> p (a b)"),
            att["qkvT"][:, 4:8, :].rearrange("p a b -> p (a b)"), AF.Square),
            after=[t_ev2], inc=ss)
        t_k2 = R.op("tensor", lambda t: t.matmul(psf(3072, W)[0:1, :], colA[0:HD, :],
                                                 att["sqb"][0:HD, JQK - 1, :], start=True, stop=True),
                    after=[t_sqk, t_cA] + ([P["comb1"]] if P else []), inc=st_)
        t_k2e = R.op("vector", lambda v: v.tensor_reduce(km2[0:1, 0:1], psf(3072, W)[0:1, :], AX.X, ALU.max),
                     after=[t_k2], inc=sv)

        H2 = H // 2
        t_sc = {}
        t_exp = {}
        t_zs = {}
        t_pv = {}
        tze = {}
        t_ate_l = {}
        ZSLOTS = (1024, 1280)
        PVS = (1536, 1792, 2048, 2304)

        def emit_scores(h):
            evg = t_ev0 if h < 8 else t_ev2
            gate = [t_ev1, evg] + ([t_exp[h - 2]] if h >= 2 else [])
            qj, i = h // 2, h % 2
            qp = i * HD
            kcj = JQK - 1 if i == 0 else JQK
            last = None
            for kc2 in range(TPW):
                last = R.op("tensor", lambda t, h=h, qj=qj, qp=qp, kc2=kc2, kcj=kcj: t.matmul(
                    psf(512 * (h % 2), TPW, W)[:, kc2, :],
                    att["qkvT"][qp:qp + HD, kcj, 128 * kc2:128 * (kc2 + 1)],
                    att["qkvT"][qp:qp + HD, qj, :],
                    start=True, stop=True),
                    after=gate, inc=st_)
            t_sc[h] = last

        def emit_exp(h):
            gate = [t_sc[h], t_uB]
            if h >= 2:
                gate += [t_pv[h - 2], t_zs[(h - 2) // 2]]
            t_exp[h] = R.op("scalar", lambda s, h=h: s.activation(
                att["E"][:, h % 2, :, :].rearrange("p a b -> p (a b)"),
                psf(512 * (h % 2), TPW * W), AF.Exp, scale=SCALE, bias=ubias),
                after=gate, inc=ss)

        def emit_pv(h):
            cj, i = h // 2, h % 2
            cp = i * HD
            gate = [t_exp[h], t_vones] + ([t_ate_l[cj - 1]] if cj >= 1 else [])
            last = None
            for kc2 in range(TPW):
                last = R.op("tensor", lambda t, h=h, cj=cj, cp=cp, kc2=kc2: t.matmul(
                    PS[cp:cp + HD, PVS[cj % 4]:PVS[cj % 4] + W],
                    vbuf[:, kc2, 0:HD],
                    att["E"][:, h % 2, kc2, :],
                    start=(kc2 == 0), stop=(kc2 == TPW - 1)),
                    after=gate, inc=st_)
            t_pv[h] = last

        def emit_zrep(cj):
            gate = [t_exp[2 * cj + 1], t_A64, t_B64] + ([tze[cj - 1]] if cj >= 1 else [])
            last = None
            for i in range(2):
                lt = A64 if i == 0 else B64
                for kc2 in range(TPW):
                    last = R.op("tensor", lambda t, cj=cj, i=i, kc2=kc2, lt=lt: t.matmul(
                        psf(ZSLOTS[cj % 2], W), lt, att["E"][:, i, kc2, :],
                        start=(i == 0 and kc2 == 0), stop=(i == 1 and kc2 == TPW - 1)),
                        after=gate, inc=st_)
            t_zs[cj] = last

        def emit_tze(cj):
            tze[cj] = R.op("vector", lambda v, cj=cj: v.reciprocal_approx_fast(
                zrs32[:, cj % 2, :], psf(ZSLOTS[cj % 2], W)), after=[t_zs[cj]], inc=sv)

        def emit_ate(cj):
            gate = [tze[cj], t_pv[2 * cj], t_pv[2 * cj + 1]]
            t_ate_l[cj] = R.op("vector", lambda v, cj=cj: v.tensor_tensor(
                att["attnT"][:, cj, :], PS[:, PVS[cj % 4]:PVS[cj % 4] + W],
                zrs32[:, cj % 2, :], ALU.mult),
                after=gate, inc=sv)

        emit_scores(0)
        t_q2 = None
        for k in range(KC):
            t_q2 = R.op("tensor", lambda t, k=k: t.matmul(
                psf(3328, W)[0:1, :], ones_c, att["sqb"][:, k, :],
                start=(k == 0), stop=(k == KC - 1)),
                after=[(t_sqA if k < 4 else t_sqB), t_k2e, t_onc], inc=st_)
        t_q2e = R.op("vector", lambda v: v.tensor_reduce(km2[0:1, 1:2], psf(3328, W)[0:1, :], AX.X, ALU.max),
                     after=[t_q2], inc=sv)
        tq2 = R.op("vector", lambda v: v.tensor_tensor(km2[0:1, 1:2], km2[0:1, 1:2], km2[0:1, 0:1], ALU.mult),
                   after=[t_q2e, t_k2e], inc=sv)
        tq3 = R.op("scalar", lambda s: s.activation(km2[0:1, 3:4], km2[0:1, 1:2], AF.Sqrt),
                   after=[tq2], inc=ss)
        tq4 = R.op("vector", lambda v: v.tensor_scalar(km2[0:1, 3:4], km2[0:1, 3:4], -SCALE, None, ALU.mult),
                   after=[tq3], inc=sv)
        t_uB = R.op("gpsimd", lambda g: g.partition_broadcast(ubias, km2[0:1, 3:4]),
                    after=[tq4], inc=sg)
        t_vb = None
        for kc2 in range(TPW):
            tt = R.op("tensor", lambda t, kc2=kc2: t.matmul(
                psf(2304, HD), att["qkvT"][HD:2 * HD, JQK - 1, 128 * kc2:128 * (kc2 + 1)],
                idn_b[HD:2 * HD, HD:2 * HD], start=True, stop=True),
                after=[t_ev1] + ([t_vb] if t_vb else []), inc=st_)
            t_vb = R.op("scalar", lambda s, kc2=kc2: s.copy(vbuf[:, kc2, 0:HD], psf(2304, HD)),
                        after=[tt], inc=ss)
        t_vones = R.op("vector", lambda v: v.memset(vbuf[:, :, HD:HD + 1], 1.0),
                       after=[t_vb], inc=sv)
        def emit_h2_tile(wp, combp, ti, t_h2w):
            xo = att["xmo"][:, wp % 2, ti, :]
            tah, tqh = emit_ln_dve(xo, att["h2b"][:, ti, :], ti, 1,
                                   [combp[ti]] + ([t_h2_store] if t_h2_store else []),
                                   [h2_sq[ti]] if h2_sq else [], [])
            tm1 = R.op("vector", lambda v, ti=ti: v.tensor_tensor(
                att["h2b"][:, ti, :], att["h2b"][:, ti, :], A2r, ALU.mult), after=[tah], inc=sv)
            tm2 = R.op("vector", lambda v, ti=ti: v.tensor_tensor(
                att["h2b"][:, ti, :], att["h2b"][:, ti, :], B2r, ALU.add), after=[tm1], inc=sv)
            t_h2w.append(tm2)
            h2_sq2[ti] = tqh
        t_h2w_acc = []
        emit_exp(0)
        t_prj = {}

        def emit_proj(k, ti, extra):
            for hf in range(2):
                t_prj[(ti, hf)] = R.op("tensor", lambda t, k=k, ti=ti, hf=hf, b=b: t.matmul(
                    psf(2560 + 512 * hf, 512),
                    att["attnT"][:, k, 128 * ti:128 * (ti + 1)],
                    projT[:, b, k, 512 * hf:512 * (hf + 1)],
                    start=(k == 0), stop=False),
                    after=[t_ate_l[k]] + extra + ([t_prj[(ti, hf)]] if (ti, hf) in t_prj else []),
                    inc=st_)

        for h in range(1, 2 * H2):
            emit_scores(h)
            if h % 2 == 0 and h >= 2:
                emit_zrep(h // 2 - 1)
                emit_tze(h // 2 - 1)
            emit_exp(h)
            emit_pv(h - 1)
            if h % 2 == 0 and h >= 2:
                emit_ate(h // 2 - 1)
            if h % 2 == 1 and h >= 3 and h // 2 - 1 < KC - 1:
                emit_proj(h // 2 - 1, 0, ([P["comb1"]] if (P and h == 3) else []))
            if h2_pend is not None and h in (3, 6):
                wp, combp = h2_pend
                emit_h2_tile(wp, combp, 0 if h == 3 else 1, t_h2w_acc)
                if h == 6:
                    h2_sq, h2_sq2 = dict(h2_sq2), {}
                    t_h2_store = R.op("sync", lambda s, wp=wp: s.dma_start(
                        out=h2hb[W * wp:W * (wp + 1), :].rearrange("(t p) d -> p t d", p=128),
                        in_=att["h2b"][:, :, :]), after=t_h2w_acc, inc=d_h2, dma=True)
                    t_free_xmo[wp % 2] = t_free_xmo[wp % 2] + t_h2w_acc
                    h2_pend = None
                    t_h2w_acc = []
            if h in (9, 12) and w + 1 < NW:
                if h == 9:
                    t_aff_n = []
                    sq_n = {}
                ti = 0 if h == 9 else 1
                xt1 = att["xb"][:, (w + 1) % 2, ti, :]
                tan, tqn = emit_ln_dve(xt1, att["ln1"][:, ti, :], ti, 0,
                                       [t_xload[w + 1]],
                                       [t_sqtok[ti]] if t_sqtok else [], [t_lnT])
                t_aff_n.append(tan)
                sq_n[ti] = tqn
                if h == 12:
                    t_sqtok = {ti2: sq_n[ti2] for ti2 in range(TPW)}
        emit_pv(2 * H2 - 1)
        emit_zrep(H2 - 1)
        emit_tze(H2 - 1)
        emit_ate(H2 - 1)
        t_comb = []
        for ti in range(TPW):
            if ti == 0:
                emit_proj(KC - 1, 0, [])
            else:
                for k in range(KC):
                    emit_proj(k, 1, ([t_comb[0]] if k == 0 else []))
            for hf in range(2):
                t_prj[(ti, hf)] = R.op("tensor", lambda t, ti=ti, hf=hf, b=b: t.matmul(
                    psf(2560 + 512 * hf, 512), ones_b[0:1, 0:128],
                    Rr[0:1, b, 512 * hf:512 * (hf + 1)],
                    start=False, stop=True), after=[t_prj[(ti, hf)], t_on], inc=st_)
            t_comb.append(R.op("vector", lambda v, ti=ti, w=w: v.tensor_tensor(
                att["xmo"][:, w % 2, ti, :], psf(2560, 1024),
                att["xb"][:, w % 2, ti, :], ALU.add),
                after=[t_prj[(ti, 0)], t_prj[(ti, 1)]] + t_free_xmo[w % 2], inc=sv))
        t_pj_last = t_prj[(1, 1)]
        t_ate = t_ate_l[H2 - 1]
        free_xmo = []
        if not ffn:
            tso = R.op("sync", lambda s, w=w: s.dma_start(
                out=out[W * w:W * (w + 1), :].rearrange("(t p) d -> p t d", p=128),
                in_=att["xmo"][:, w % 2, :, :]), after=t_comb, inc=d_op[w % 2], dma=True)
            free_xmo = [tso]
        else:
            for ti in range(TPW):
                t_xmf.append(R.op("sync", lambda s, ti=ti, w=w: s.dma_start(
                    out=xmhb[W * w + 128 * ti:W * w + 128 * (ti + 1), :],
                    in_=att["xmo"][:, w % 2, ti, :]), after=[t_comb[ti]], inc=d_xmp[w % 2], dma=True))
            h2_pend = (w, list(t_comb))
            free_xmo = [t_xmf[-1], t_xmf[-2]]
        t_free_xmo[w % 2] = free_xmo
        if w + 2 < NW:
            t_xload[w + 2] = R.op("sync", lambda s, w=w: s.dma_start(
                out=att["xb"][:, w % 2, :, :],
                in_=x_in[W * (w + 2):W * (w + 3), :].rearrange("(t p) d -> p t d", p=128)),
                after=[t_comb[-1]], inc=d_xp[w % 2], dma=True)
        P = dict(expB=t_exp[2 * H2 - 1], tze7=tze[H2 - 1], ate7=t_ate_l[H2 - 1],
                 comb1=t_comb[1])
        t_comb_prev = t_comb[-1]
        t_ate_prev = t_ate
        t_win_last = t_comb[-1]

    att_done = ([t_win_last, t_ate_prev, t_h2_store, d_o0.tok(), d_o1.tok(),
                 d_h2.tok(), d_xmp[0].tok(), d_xmp[1].tok()] if stage >= 1 else
                [x for x in t_setup])
    aw_w1 = [t_qkv_j[JQK]] if stage >= 1 else att_done
    aw_w2 = ([t_qkv_j[JQK], t_pj_last] if stage >= 1 else att_done)
    aw_gath = ([t_qkv_j[JQK], t_h2_store, d_h2.tok()] if stage >= 1 else att_done)
    if stage < 2:
        R.op("sync", lambda s: s.dma_start(out=out[0:128, :], in_=gbr),
             after=att_done + [t_gb], inc=d_o0, dma=True)
        R.op("sync", lambda s: s.nop(), after=[s_.tok() for s_ in Sem.REGISTRY if s_.n > 0])
        with nc.Block() as block:
            R.emit(block)
        nc.finalize()
        return nc, c

    octs = [(e, oc) for e in range(E) for oc in range(OC)]
    t_w1l, t_w2l = {}, {}
    t_w1_cons = {}
    t_w2_cons = {}
    t_gath = {}
    t_gath_cons = {}
    t_b2l = {}
    t_b2_cons = {}

    def load_oct(i):
        e, oc = octs[i]
        aft_w1 = aw_w1 if i < 2 else [t_w1_cons[i - 2]]
        aft_w2 = aw_w2 if i < 2 else [t_w2_cons[i - 2]]
        t_w1l[i] = R.op("gpsimd", lambda g, e=e, oc=oc: g.dma_start(
            out=moe["w1s"][:, oc % 2, :, :],
            in_=w1[e, :, 512 * oc:512 * (oc + 1)].rearrange("(k p) f -> p k f", p=128)),
            after=[t_lib] + aft_w1, inc=d_w1p[oc % 2], dma=True)
        t_w2l[i] = R.op("gpsimd", lambda g, e=e, oc=oc: g.dma_start(
            out=moe["w2s"][:, oc % 2, :, :],
            in_=w2[e, 512 * oc:512 * (oc + 1), :].rearrange("(m p) d -> p m d", p=128)),
            after=[t_lib] + aft_w2, inc=d_w2p[oc % 2], dma=True)

    def load_gath(e):
        aft = aw_gath if e < 1 else [t_gath_cons[e - 1]]
        t_gath[e] = R.op("gpsimd", lambda g, e=e: g.dma_gather(
            moe["gath"][:, :, :], h2hb, gix[:, e, :], REG, REG, D,
            elem_step=D, transpose=False),
            after=[t_lib, t_gi] + aft, inc=d_gtp[e % 2], dma=True)

    def load_b2(e):
        aft = att_done if e < 2 else [t_b2_cons[e - 2]]
        t_b2l[e] = R.op("gpsimd", lambda g, e=e: g.dma_start(
            out=moe["b2e"][0:1, e % 2, :], in_=b2[e:e + 1, :]),
            after=[t_lib] + aft, inc=d_b2p[e % 2], dma=True)

    NE_PRE = E if stage >= 3 else 1
    for i in range(min(2, NE_PRE * OC)):
        load_oct(i)
    load_gath(0)
    for e in range(min(2, NE_PRE)):
        load_b2(e)
    t_memz = R.op("vector", lambda v: v.memset(moe["out2"].rearrange("p a b -> p (a b)"), 0.0),
                  after=att_done, inc=sv)

    t_o2e = {}
    t_hb_cons = {}
    t_gtr_done = {}
    t_last_gel = None
    NE_RUN = E if stage >= 3 else 1
    for e in range(NE_RUN):
        t_gtr = None
        for nt in range(NT):
            for k in range(KC):
                bank = 6 + (k % 2)
                ttr = R.op("tensor", lambda t, nt=nt, k=k, bank=bank: t.matmul(
                    psf(bank * 512, 128), moe["gath"][:, nt, 128 * k:128 * (k + 1)], idn_b,
                    start=True, stop=True),
                    after=[t_gath[e]] + ([t_gtr] if t_gtr else []) +
                          ([t_win_last] if (e == 0 and nt == 0 and k == 0 and stage >= 1) else []) +
                          ([t_gtr_done[e - 2]] if (e - 2 in t_gtr_done and nt == 0 and k == 0) else []) +
                          ([t_last_gel] if (t_last_gel and nt == 0 and k == 0) else []), inc=st_)
                t_gtr = R.op("vector", lambda v, e=e, nt=nt, k=k, bank=bank: v.tensor_copy(
                    moe["gathT"][:, e % 2, k, 128 * nt:128 * (nt + 1)], psf(bank * 512, 128)),
                    after=[ttr], inc=sv)
        t_gtr_done[e] = t_gtr
        if e + 1 < NE_RUN:
            t_gath_cons[e] = t_gtr
            load_gath(e + 1)
        t_w2mm = None
        gel_of_oct = {}
        for oc in range(OC):
            i = e * OC + oc
            tw1, tw2 = t_w1l[i], t_w2l[i]
            t_gel = []
            for m in range(4):
                bank = 6 + (m % 2)
                mm = None
                gate = [tw1, t_gtr_done[e]]
                if len(t_gel) >= 2:
                    gate.append(t_gel[-2])
                elif oc >= 1 and len(t_gel) < 2:
                    gate.append(gel_of_oct[oc - 1][-2 + len(t_gel)])
                if (i % 2) in t_hb_cons and m == 0:
                    gate.append(t_hb_cons[i % 2])
                for k in range(KC):
                    mm = R.op("tensor", lambda t, e=e, oc=oc, m=m, k=k, bank=bank: t.matmul(
                        psf(bank * 512, CAP), moe["w1s"][:, oc % 2, k, 128 * m:128 * (m + 1)],
                        moe["gathT"][:, e % 2, k, 0:CAP],
                        start=(k == 0), stop=(k == KC - 1)),
                        after=gate, inc=st_)
                ffg = oc * 512 + m * 128
                t_gel.append(R.op("scalar", lambda s, e=e, oc=oc, m=m, bank=bank, ffg=ffg: s.activation(
                    moe["hb"][:, oc % 2, m, :], psf(bank * 512, CAP),
                    AF.Gelu_apprx_tanh, bias=b1c[:, ffg // 128, e:e + 1]),
                    after=[mm, t_b1], inc=ss))
            gel_of_oct[oc] = t_gel
            t_w1_cons[i] = t_gel[-1]
            for tix in range(NT):
                tsz = TSZ[tix]
                for m in range(4):
                    for hf in range(2):
                        t_w2mm = R.op("tensor", lambda t, oc=oc, m=m, hf=hf, tix=tix, tsz=tsz: t.matmul(
                            psf(tix * 1024, 2, 512)[0:tsz, hf, :],
                            moe["hb"][:, oc % 2, m, 128 * tix:128 * tix + tsz],
                            moe["w2s"][:, oc % 2, m, 512 * hf:512 * (hf + 1)],
                            start=(oc == 0 and m == 0), stop=False),
                            after=[tw2, t_gel[m]], inc=st_)
            t_w2_cons[i] = t_w2mm
            t_hb_cons[i % 2] = t_w2mm
            t_last_gel = t_gel[-1]
            if i + 2 < NE_RUN * OC:
                load_oct(i + 2)
        for tix in range(NT):
            tsz = TSZ[tix]
            for hf in range(2):
                t_w2mm = R.op("tensor", lambda t, e=e, tix=tix, hf=hf, tsz=tsz: t.matmul(
                    psf(tix * 1024, 2, 512)[0:tsz, hf, :], ones_b[0:1, 0:tsz],
                    moe["b2e"][0:1, e % 2, 512 * hf:512 * (hf + 1)],
                    start=False, stop=True),
                    after=[t_b2l[e], t_on], inc=st_)
        t_b2_cons[e] = t_w2mm
        evs = []
        for tix in range(NT):
            tsz = TSZ[tix]
            evs.append(R.op("scalar", lambda s, e=e, tix=tix, tsz=tsz: s.copy(
                moe["out2"][0:tsz, e * NT + tix, :],
                psf(tix * 1024, 1024)[0:tsz, :]),
                after=[t_w2mm, t_memz], inc=ss))
        t_o2e[e] = evs[-1]
        if e + 2 < NE_RUN:
            load_b2(e + 2)

    if stage < 4:
        R.op("sync", lambda s: s.dma_start(out=out[0:128, :], in_=gbr),
             after=[t_o2e[NE_RUN - 1], t_gb], inc=d_o0, dma=True)
        R.op("sync", lambda s: s.nop(), after=[s_.tok() for s_ in Sem.REGISTRY if s_.n > 0])
        with nc.Block() as block:
            R.emit(block)
        nc.finalize()
        return nc, c
    t_st_final = []
    t_mt_cons = {}
    prev_pass_last = None
    for pas in range(2):
        rcs = list(range(pas * (RC - RC // 2) if False else pas * (RC // 2), (pas + 1) * (RC // 2)))
        if RC == 1:
            rcs = [0] if pas == 0 else []
        t_acc = {}
        for sc in range(SC if rcs else 0):
            e_of = sc // NT
            gate = [t_stk, t_sw, t_iof, t_o2e[e_of]]
            if (sc % 2) in t_mt_cons:
                gate.append(t_mt_cons[sc % 2])
            if sc == 0 and pas == 0:
                gate.append(t_o2e[E - 1])
            if sc == 0 and prev_pass_last is not None:
                gate.append(prev_pass_last)
            tb = R.op("vector", lambda v, sc=sc: v.tensor_scalar(
                moe["Mt"][:, sc % 2, :], iota_f, stokc[:, sc:sc + 1], swc[:, sc:sc + 1],
                ALU.is_equal, ALU.mult), after=gate, inc=sv)
            last = None
            for r in rcs:
                for hf in range(2):
                    last = R.op("tensor", lambda t, sc=sc, r=r, hf=hf: t.matmul(
                        psf((r % max(1, RC // 2)) * 1024, 2, 512)[:, hf, :],
                        moe["Mt"][:, sc % 2, 128 * r:128 * (r + 1)],
                        moe["out2"][:, sc, 512 * hf:512 * (hf + 1)],
                        start=(sc == 0), stop=(sc == SC - 1)),
                        after=[tb], inc=st_)
                    t_acc[r] = last
            t_mt_cons[sc % 2] = last
        for r in rcs:
            txi = R.op("sync", lambda s, r=r: s.dma_start(
                out=moe["xmi"][:, r % 2, :], in_=xmhb[128 * r:128 * (r + 1), :]),
                after=[d_xmp[0].tok(), d_xmp[1].tok()] + ([t_st_final[-2][0]] if len(t_st_final) >= 2 else []),
                inc=d_xi[r % 2], dma=True)
            tv = R.op("vector", lambda v, r=r: v.tensor_tensor(
                moe["xo"][:, r % 2, :], psf((r % max(1, RC // 2)) * 1024, 1024), gbr, ALU.mult),
                after=[t_acc[r], t_gb] + ([t_st_final[-2][1]] if len(t_st_final) >= 2 else []),
                inc=sv)
            tv2 = R.op("vector", lambda v, r=r: v.tensor_tensor(
                moe["xo"][:, r % 2, :], moe["xo"][:, r % 2, :], moe["xmi"][:, r % 2, :], ALU.add),
                after=[tv, txi], inc=sv)
            tso = R.op("sync", lambda s, r=r: s.dma_start(
                out=out[128 * r:128 * (r + 1), :], in_=moe["xo"][:, r % 2, :]),
                after=[tv2], inc=d_op[r % 2], dma=True)
            t_st_final.append((tv2, tso))
        if rcs:
            prev_pass_last = t_acc[rcs[-1]]

    R.op("sync", lambda s: s.nop(), after=[s_.tok() for s_ in Sem.REGISTRY if s_.n > 0])

    with nc.Block() as block:
        R.emit(block)
    nc.finalize()
    return nc, c


_PROG_CACHE = {}


def _get_program(key, cfg):
    if key not in _PROG_CACHE:
        _PROG_CACHE[key] = build_program(cfg)
    return _PROG_CACHE[key]


def host_prep_core(cfg, top_idx, combine, row_base):
    c = _derived(cfg)
    E, REG, CAP, SC, S, NROW = c["E"], c["REG"], c["CAP"], c["SC"], c["S"], c["NROW"]
    gidx = np.zeros((16, E * (REG // 16)), np.int16)
    stok = np.full(S, -1.0, np.float32)
    sw = np.zeros(S, np.float32)
    for e in range(E):
        col = np.asarray(top_idx[:, e])
        locs = np.flatnonzero((col >= row_base) & (col < row_base + NROW))
        toks = (col[locs] - row_base).astype(np.int64)
        n = len(toks)
        if n > CAP:
            raise RuntimeError(f"slot capacity overflow e={e}: {n} > {CAP}")
        base = e * REG
        for i in range(n):
            j = base + i
            gidx[j % 16, j // 16] = np.int16(toks[i])
            stok[j] = np.float32(toks[i])
            sw[j] = np.float32(combine[int(col[locs[i]]), e])
    return (np.tile(gidx, (8, 1)),
            np.ascontiguousarray(stok.reshape(SC, 128).T),
            np.ascontiguousarray(sw.reshape(SC, 128).T))


def kernel(L0_attn_x, L0_attn_q_w, L0_attn_kv_w, L0_attn_proj_w, L0_attn_proj_b,
           L0_ffn_combine_weights, L0_ffn_top_idx, L0_ffn_W1, L0_ffn_b1,
           L0_ffn_W2, L0_ffn_b2, L0_c, L0_wmod):
    cfg = FULL_CFG
    c = _derived(cfg)
    ncores = 8
    L = 16384
    x = np.ascontiguousarray(np.asarray(L0_attn_x, dtype=np.float32)).reshape(-1, c["D"])
    top_idx = np.asarray(L0_ffn_top_idx)
    combine = np.asarray(L0_ffn_combine_weights, dtype=np.float32)
    nc, _ = _get_program("full", cfg)

    shared = dict(
        qw=np.ascontiguousarray(np.asarray(L0_attn_q_w, np.float32)),
        kvw=np.ascontiguousarray(np.asarray(L0_attn_kv_w, np.float32)),
        pjw=np.ascontiguousarray(np.asarray(L0_attn_proj_w, np.float32)),
        pjb=np.ascontiguousarray(np.asarray(L0_attn_proj_b, np.float32)).reshape(1, -1),
        c2=np.ascontiguousarray(np.asarray(L0_c, np.float32)),
        wmod=np.ascontiguousarray(np.asarray(L0_wmod, np.float32)),
        w1=np.ascontiguousarray(np.asarray(L0_ffn_W1, np.float32)),
        b1t=np.ascontiguousarray(np.asarray(L0_ffn_b1, np.float32).T),
        w2=np.ascontiguousarray(np.asarray(L0_ffn_W2, np.float32)),
        b2=np.ascontiguousarray(np.asarray(L0_ffn_b2, np.float32)),
    )
    in_maps, row_maps = [], []
    for core in range(ncores):
        rows = np.concatenate([
            np.arange(1024 * core, 1024 * core + 1024),
            np.arange(8192 + 1024 * core, 8192 + 1024 * core + 1024),
            np.arange(L + 2048 * core, L + 2048 * core + 2048)])
        row_maps.append(rows)
        gi, stk, sw = host_prep_core(cfg, top_idx, combine, 1024 * core)
        m = dict(shared)
        m["x_rows"] = np.ascontiguousarray(x[rows])
        m["gidx"] = gi
        m["stok"] = stk
        m["swt"] = sw
        in_maps.append(m)

    res = run_bass_kernel_spmd(nc, in_maps, core_ids=list(range(ncores)))
    outp = np.empty((2 * L, c["D"]), np.float32)
    for core in range(ncores):
        outp[row_maps[core]] = res.results[core]["out_rows"]
    return outp.reshape(2, L, c["D"])


# version 17
# speedup vs baseline: 1.0306x; 1.0306x over previous
import numpy as np
import ml_dtypes
import concourse.bass as bass
import concourse.bacc as bacc
import concourse.mybir as mybir
from concourse import library_config
from concourse.bass_utils import run_bass_kernel_spmd

F32 = mybir.dt.float32
BF16 = mybir.dt.bfloat16
I16 = mybir.dt.int16
U8 = mybir.dt.uint8
AF = mybir.ActivationFunctionType
ALU = mybir.AluOpType
AX = mybir.AxisListType

FULL_CFG = dict(NW=16, WFN=4, WB=[0] * 8 + [1] * 8, W=256, D=1024, H=16, HD=64,
                E=8, FF=4096, CAP=352, REG=384, EPS=1e-5)
SMALL_CFG = dict(NW=4, WFN=2, WB=[0, 0, 1, 1], W=256, D=1024, H=16, HD=64,
                 E=8, FF=1024, CAP=96, REG=128, EPS=1e-5)


def _derived(cfg):
    d = dict(cfg)
    d["KC"] = cfg["D"] // 128
    d["TPW"] = cfg["W"] // 128
    d["OC"] = cfg["FF"] // 512
    d["NT"] = cfg["REG"] // 128
    d["S"] = cfg["E"] * cfg["REG"]
    d["SC"] = d["S"] // 128
    d["NROW"] = cfg["WFN"] * cfg["W"]
    d["RC"] = d["NROW"] // 128
    d["NTOK"] = cfg["NW"] * cfg["W"]
    d["HG"] = cfg["H"] // 4
    sizes, left = [], cfg["CAP"]
    for _ in range(d["NT"]):
        sizes.append(min(128, left))
        left -= sizes[-1]
    d["TSZ"] = sizes
    return d


class Sem:
    REGISTRY = None

    def __init__(self, nc, name):
        self.h = nc.alloc_semaphore(name)
        self.n = 0
        self.name = name
        if Sem.REGISTRY is not None:
            Sem.REGISTRY.append(self)

    def tok(self):
        return (self, self.n)


class Rec:
    ENGS = ("sync", "gpsimd", "vector", "scalar", "tensor")

    def __init__(self, nc):
        self.nc = nc
        self.ops = {e: [] for e in self.ENGS}

    def op(self, eng, build, after=(), inc=None, dma=False):
        waits = [(s, v) for (s, v) in after if v > 0]
        if inc is not None:
            inc.n += 16 if dma else 1
        self.ops[eng].append((waits, build, inc, 16 if dma else 1))
        return inc.tok() if inc is not None else None

    def emit(self, block):
        for eng_name in self.ENGS:
            ops = self.ops[eng_name]

            def body(eng, ops=ops):
                seen = {}
                for waits, build, inc, k in ops:
                    for s, v in waits:
                        if seen.get(s.name, 0) < v:
                            eng.wait_ge(s.h, v)
                            seen[s.name] = v
                    ins = build(eng)
                    if inc is not None:
                        ins.then_inc(inc.h, k)
            getattr(block, eng_name)(body)


class Arena:

    def __init__(self, nc, name, dtype, nelem):
        self.t = nc.alloc_sbuf_tensor(name, [128, nelem], dtype)
        self.dtype = dtype
        self.nelem = nelem
        self.off = 0
        self.marks = []
        self.peak = 0

    def alloc(self, *shape):
        n = int(np.prod(shape))
        off = self.off
        self.off += (n + 15) & ~15
        self.peak = max(self.peak, self.off)
        assert self.off <= self.nelem, f"arena overflow {self.off} > {self.nelem}"
        ap = self.t[:, off:off + n]
        if len(shape) > 1:
            names = [chr(97 + i) for i in range(len(shape))]
            ap = ap.rearrange("p (" + " ".join(names) + ") -> p " + " ".join(names),
                              **{nm: s for nm, s in zip(names, shape)})
        return ap

    def mark(self):
        self.marks.append(self.off)

    def reset(self):
        self.off = self.marks.pop()


def build_program(cfg, stage=99):
    import os
    KDROP = set(os.environ.get('KDROP', '').split(','))
    c = _derived(cfg)
    NW, WFN, WB, W, D, H, HD, E, FF = (c["NW"], c["WFN"], c["WB"], c["W"], c["D"],
                                       c["H"], c["HD"], c["E"], c["FF"])
    KC, CAP, REG, NT, S, SC = c["KC"], c["CAP"], c["REG"], c["NT"], c["S"], c["SC"]
    NROW, RC, NTOK, TPW, OC, HG, TSZ, EPS = (c["NROW"], c["RC"], c["NTOK"], c["TPW"],
                                             c["OC"], c["HG"], c["TSZ"], c["EPS"])
    QKW = D + 2 * HD
    JQK = QKW // 128
    QKW2 = QKW + 128
    JQK2 = JQK + 1
    SCALE = float(HD) ** -0.5

    nc = bacc.Bacc("TRN2", debug=False)
    Sem.REGISTRY = []

    x_in = nc.dram_tensor("x_rows", [NTOK, D], F32, kind="ExternalInput").ap()
    qw = nc.dram_tensor("qw", [D, D], F32, kind="ExternalInput").ap()
    kvw = nc.dram_tensor("kvw", [2 * HD, D], F32, kind="ExternalInput").ap()
    pjw = nc.dram_tensor("pjw", [D, D], F32, kind="ExternalInput").ap()
    pjb = nc.dram_tensor("pjb", [1, D], F32, kind="ExternalInput").ap()
    c2 = nc.dram_tensor("c2", [2, D], F32, kind="ExternalInput").ap()
    wmod = nc.dram_tensor("wmod", [D, 6 * D], F32, kind="ExternalInput").ap()
    w1 = nc.dram_tensor("w1", [E, D, FF], F32, kind="ExternalInput").ap()
    b1t = nc.dram_tensor("b1t", [FF, E], F32, kind="ExternalInput").ap()
    w2 = nc.dram_tensor("w2", [E, FF, D], F32, kind="ExternalInput").ap()
    b2 = nc.dram_tensor("b2", [E, D], F32, kind="ExternalInput").ap()
    gidx = nc.dram_tensor("gidx", [128, E * (REG // 16)], I16, kind="ExternalInput").ap()
    stok_d = nc.dram_tensor("stok", [128, SC], F32, kind="ExternalInput").ap()
    swt_d = nc.dram_tensor("swt", [128, SC], F32, kind="ExternalInput").ap()
    out = nc.dram_tensor("out_rows", [NTOK, D], F32, kind="ExternalOutput").ap()
    h2hb = nc.dram_tensor("h2hb", [NROW, D], BF16, kind="Internal").ap()
    xmhb = nc.dram_tensor("xmhb", [NROW, D], F32, kind="Internal").ap()

    AB = Arena(nc, "ab", BF16, 71 * 1024)
    FA = Arena(nc, "af", F32, 14 * 1024)
    gix = nc.alloc_sbuf_tensor("gix", [128, E * (REG // 16)], I16).ap().rearrange("p (e r) -> p e r", e=E)
    wm32 = nc.alloc_sbuf_tensor("wm32", [128, 4, 512], F32).ap()

    iota_f = FA.alloc(D)
    iota_c = FA.alloc(1)
    idn_f = FA.alloc(128)
    gbr = FA.alloc(D)
    qsc = FA.alloc(2, KC)
    km2 = FA.alloc(4)
    stokc = FA.alloc(SC)
    swc = FA.alloc(SC)
    b1c = FA.alloc(FF // 128, E)
    mvt = FA.alloc(2, TPW, 16)
    stt = FA.alloc(2, TPW, 2)
    idn_b = AB.alloc(128)
    ones_b = AB.alloc(256)
    colA = AB.alloc(1)
    colB = AB.alloc(1)
    A64 = AB.alloc(128)
    B64 = AB.alloc(128)
    ones_c = AB.alloc(1)
    Bw = AB.alloc(2, QKW2)
    Rr = AB.alloc(2, D)
    A2r = AB.alloc(D)
    B2r = AB.alloc(D)
    sac = AB.alloc(2, KC)

    ovb = AB.off
    wqkvT = AB.alloc(2, KC, QKW2)
    projT = AB.alloc(2, KC, D)
    qb2 = AB.off
    AB.mark(); FA.mark()
    att = {}
    att["ln1T"] = AB.alloc(KC, W)
    att["qkvT"] = AB.alloc(JQK2, W)
    att["E"] = AB.alloc(2, TPW, W)
    att["sqb"] = AB.alloc(JQK, W)
    att["attnT"] = AB.alloc(KC, W)
    att["h2b"] = AB.alloc(TPW, D)
    att["ln1"] = AB.alloc(TPW, D)
    vbuf = AB.alloc(TPW, HD + 1)
    att["xb"] = FA.alloc(2, TPW, D)
    att["xmo"] = FA.alloc(2, TPW, D)
    ubias = FA.alloc(1)
    zrs32 = FA.alloc(4, 256)
    AB.reset(); FA.reset()
    AB.mark(); FA.mark()
    stp = {}
    stp["wstage"] = AB.alloc(2, D)
    stp["wmodb"] = AB.alloc(4, 512)
    stp["silT"] = AB.alloc(KC, 2)
    stp["mrow1"] = AB.alloc(2, 6 * D)
    stp["pjbs"] = AB.alloc(D)
    stp["grep"] = AB.alloc(D)
    stp["rbf2"] = AB.alloc(D)
    stp["sil"] = FA.alloc(D)
    stp["mrow2"] = FA.alloc(3 * D)
    stp["crow"] = FA.alloc(2 * D)
    stp["c2s"] = FA.alloc(D)
    stp["scols"] = FA.alloc(4, D)
    AB.reset(); FA.reset()
    AB.off = ovb
    moe = {}
    moe["gath"] = AB.alloc(NT, D)
    moe["gathT"] = AB.alloc(2, KC, REG)
    moe["w1s"] = AB.alloc(2, KC, 512)
    moe["w2s"] = AB.alloc(2, 4, D)
    moe["hb"] = AB.alloc(2, 4, CAP)
    moe["out2"] = AB.alloc(E * NT, D)
    moe["b2e"] = AB.alloc(2, D)
    moe["Mt"] = AB.alloc(2, D)
    moe["xo"] = FA.alloc(2, D)
    moe["xmi"] = FA.alloc(2, D)

    PS = nc.alloc_psum_tensor("PS", [128, 4096], F32)

    def psf(lo, *shape):
        n = int(np.prod(shape))
        ap = PS[:, lo:lo + n]
        if len(shape) > 1:
            names = [chr(97 + i) for i in range(len(shape))]
            ap = ap.rearrange("p (" + " ".join(names) + ") -> p " + " ".join(names),
                              **{nm: s for nm, s in zip(names, shape)})
        return ap

    R = Rec(nc)
    sv = Sem(nc, "sv")
    ss = Sem(nc, "ss")
    st_ = Sem(nc, "st")
    sg = Sem(nc, "sg")
    d_set = Sem(nc, "d_set")
    d_g = Sem(nc, "d_g")
    d_x0 = Sem(nc, "d_x0")
    d_x1 = Sem(nc, "d_x1")
    d_o0 = Sem(nc, "d_o0")
    d_o1 = Sem(nc, "d_o1")
    d_h2 = Sem(nc, "d_h2")
    d_w1 = Sem(nc, "d_w1")
    d_w2 = Sem(nc, "d_w2")
    d_gt = Sem(nc, "d_gt")
    d_b2 = Sem(nc, "d_b2")
    d_sep = Sem(nc, "d_sep")
    d_c2 = Sem(nc, "d_c2")
    d_xmp = [Sem(nc, "d_xm0"), Sem(nc, "d_xm1")]
    d_sc = Sem(nc, "d_sc")
    d_xi = [Sem(nc, "d_xia"), Sem(nc, "d_xib")]
    d_sm = Sem(nc, "d_sm")
    d_cast = [Sem(nc, "d_cast0"), Sem(nc, "d_cast1"), Sem(nc, "d_cast2")]
    d_stg = [Sem(nc, "d_stg0"), Sem(nc, "d_stg1"), Sem(nc, "d_stg2")]
    d_tt = [Sem(nc, "d_tt0"), Sem(nc, "d_tt1")]
    d_gw = [Sem(nc, "d_gw%d" % i) for i in range(6)]
    d_w1p = [Sem(nc, "d_w1a"), Sem(nc, "d_w1b")]
    d_w2p = [Sem(nc, "d_w2a"), Sem(nc, "d_w2b")]
    d_gtp = [Sem(nc, "d_gta"), Sem(nc, "d_gtb")]
    d_b2p = [Sem(nc, "d_b2a"), Sem(nc, "d_b2b")]


    def _early_exit(after_toks):
        R.op("sync", lambda s: s.dma_start(out=out[0:2, 0:D], in_=stp["c2s"][0:2, :]),
             after=list(after_toks), inc=d_o0, dma=True)
        if os.environ.get("KEXIT") == "min":
            R.op("sync", lambda s: s.nop(), after=[d_o0.tok()])
        else:
            R.op("sync", lambda s: s.nop(), after=[s_.tok() for s_ in Sem.REGISTRY if s_.n > 0])
        with nc.Block() as block:
            R.emit(block)
        nc.finalize()
        return nc, c

    if stage == -9:
        R.op("vector", lambda v: v.memset(gbr, 3.0), inc=sv)
        R.op("sync", lambda s: s.dma_start(out=out[0:128, 0:D], in_=gbr), after=[sv.tok()], inc=d_o0, dma=True)
        R.op("sync", lambda s: s.nop(), after=[d_o0.tok()])
        with nc.Block() as block:
            R.emit(block)
        nc.finalize()
        return nc, c
    t_iof = R.op("gpsimd", lambda g: g.iota(iota_f, pattern=[[1, D]], channel_multiplier=0,
                                            allow_small_or_imprecise_dtypes=True), inc=sg)
    t_ioc = R.op("gpsimd", lambda g: g.iota(iota_c, pattern=[[0, 1]], channel_multiplier=1,
                                            allow_small_or_imprecise_dtypes=True), after=[t_iof], inc=sg)
    R.op("gpsimd", lambda g: g.load_library(library_config.mlp), after=[t_ioc], inc=None)
    t_lib = t_ioc
    t_idb = R.op("vector", lambda v: v.tensor_scalar(idn_b, iota_f[:, 0:128], iota_c, None, ALU.is_equal),
                 after=[t_iof, t_ioc], inc=sv)
    t_idf = R.op("vector", lambda v: v.tensor_copy(idn_f, idn_b), after=[t_idb], inc=sv)
    t_on = R.op("vector", lambda v: v.memset(ones_b, 1.0), inc=sv)
    t_onc = R.op("vector", lambda v: v.memset(ones_c, 1.0), inc=sv)
    t_cA = R.op("vector", lambda v: v.tensor_scalar(colA, iota_c, 64.0, None, ALU.is_lt),
                after=[t_ioc], inc=sv)
    t_cB = R.op("vector", lambda v: v.tensor_scalar(colB, iota_c, 64.0, None, ALU.is_ge),
                after=[t_ioc], inc=sv)
    t_A64 = R.op("vector", lambda v: v.tensor_scalar(A64, iota_f[:, 0:128], 64.0, None, ALU.is_lt),
                 after=[t_iof], inc=sv)
    t_B64 = R.op("vector", lambda v: v.tensor_scalar(B64, iota_f[:, 0:128], 64.0, None, ALU.is_ge),
                 after=[t_iof], inc=sv)

    if stage == -8:
        R.op("sync", lambda s: s.dma_start(out=out[0:128, 0:128], in_=idn_f), after=[t_idf], inc=d_o0, dma=True)
        R.op("sync", lambda s: s.nop(), after=[d_o0.tok()])
        with nc.Block() as block:
            R.emit(block)
        nc.finalize()
        return nc, c
    t_c2 = R.op("sync", lambda s: s.dma_start(out=stp["c2s"][0:2, :], in_=c2), inc=d_c2, dma=True)
    if stage == -7:
        R.op("sync", lambda s: s.dma_start(out=out[0:2, 0:D], in_=stp["c2s"][0:2, :]),
             after=[t_c2, t_idf, t_cB], inc=d_o0, dma=True)
        R.op("sync", lambda s: s.nop(), after=[d_o0.tok()])
        with nc.Block() as block:
            R.emit(block)
        nc.finalize()
        return nc, c
    if "pjb" in KDROP:
        t_pjb = t_lib
    else:
        t_pjb = R.op("gpsimd", lambda g: g.dma_start(out=stp["pjbs"][0:1, :], in_=pjb),
                     after=[t_lib], inc=d_g, dma=True)
    if "smloads" not in KDROP:
        R.op("sync", lambda s: s.dma_start(out=b1c, in_=b1t.rearrange("(m p) e -> p m e", p=128)),
             inc=d_sm, dma=True)
        R.op("sync", lambda s: s.dma_start(out=gix, in_=gidx.rearrange("p (e r) -> p e r", e=E)),
             inc=d_sm, dma=True)
        R.op("sync", lambda s: s.dma_start(out=stokc, in_=stok_d), inc=d_sm, dma=True)
        R.op("sync", lambda s: s.dma_start(out=swc, in_=swt_d), inc=d_sm, dma=True)
    t_b1 = t_gi = t_stk = t_sw = d_sm.tok()

    if stage == -6:
        return _early_exit([t_c2, t_sm := d_sm.tok()])
    chunks = ([("q", j) for j in range(KC)] + [("kv", 0)] + [("pj", j) for j in range(KC)])
    asm_state = dict(last=None, tr=None, ld_cons={}, lds={})

    def emit_asm_load(ci):
        kind, j = chunks[ci]
        srcw = qw if kind == "q" else (kvw if kind == "kv" else pjw)
        asm_state["lds"][ci] = R.op("gpsimd", lambda g, ci=ci, srcw=srcw, j=j: g.dma_start(
            out=stp["wstage"][:, ci % 2, :], in_=srcw[128 * j:128 * (j + 1), :]),
            after=[t_lib] + ([asm_state["ld_cons"][ci - 2]] if ci - 2 in asm_state["ld_cons"] else []),
            inc=d_cast[ci % 2], dma=True)

    def emit_asm_chunk(ci, extra):
        kind, j = chunks[ci]
        t_ld = asm_state["lds"][ci]
        for k in range(KC):
            slot = 3072 if k % 2 == 0 else 3584
            asm_state["tr"] = R.op("tensor", lambda t, ci=ci, k=k, slot=slot: t.matmul(
                psf(slot, 128), stp["wstage"][:, ci % 2, 128 * k:128 * (k + 1)], idn_b,
                start=True, stop=True),
                after=[t_ld, t_idb] + extra + ([asm_state["tr"]] if asm_state["tr"] else []) +
                      ([asm_state["last"]] if asm_state["last"] else []), inc=st_)
            if kind == "q":
                dst = wqkvT[:, 0, k, 128 * j:128 * (j + 1)]
            elif kind == "kv":
                dst = wqkvT[:, 0, k, D:D + 128]
            else:
                dst = projT[:, 0, k, 128 * j:128 * (j + 1)]
            asm_state["last"] = R.op("vector", lambda v, dst=dst, slot=slot: v.tensor_copy(dst, psf(slot, 128)),
                                     after=[asm_state["tr"]] + ([asm_state["last"]] if asm_state["last"] else []), inc=sv)
            if kind == "kv":
                asm_state["last"] = R.op("vector", lambda v, k=k: v.tensor_copy(
                    wqkvT[:, 0, k, QKW:QKW + HD], wqkvT[:, 0, k, D:D + HD]),
                    after=[asm_state["last"]], inc=sv)
                asm_state["last"] = R.op("vector", lambda v, k=k: v.tensor_copy(
                    wqkvT[:, 0, k, QKW + HD:QKW2], wqkvT[:, 0, k, D:D + HD]),
                    after=[asm_state["last"]], inc=sv)
        asm_state["ld_cons"][ci] = asm_state["tr"]

    emit_asm_load(0)
    emit_asm_load(1)

    t_sil = R.op("scalar", lambda s: s.activation(stp["sil"][0:2, :], stp["c2s"][0:2, :], AF.Silu),
                 after=[t_c2], inc=ss)
    t_slT = []
    for k in range(KC):
        tt = R.op("tensor", lambda t, k=k: t.transpose(
            psf(3584, 2), stp["sil"][0:2, 128 * k:128 * (k + 1)], idn_f[0:2, 0:2]),
            after=[t_sil, t_idf] + ([t_slT[-1]] if t_slT else []), inc=st_)
        t_slT.append(R.op("vector", lambda v, k=k: v.tensor_copy(stp["silT"][:, k, :], psf(3584, 2)),
                          after=[tt], inc=sv))
    t_silT = t_slT[-1]

    t_mrow = []
    asm_next = [0]
    WQE = ("sync", "scalar", "gpsimd")
    mm_of_chunk = {}
    cast_of_chunk = {}
    NCW = 3 * D // 512
    for half in range(2):
        last_mm = None
        for k in range(KC):
            for ch in range(NCW):
                ci = (half * KC + k) * NCW + ch
                tl = R.op(WQE[ci % 3], lambda g, k=k, half=half, ch=ch, ci=ci: g.dma_start(
                    out=wm32[:, ci % 4, :],
                    in_=wmod[128 * k:128 * (k + 1),
                             half * 3 * D + ch * 512:half * 3 * D + (ch + 1) * 512]),
                    after=([cast_of_chunk[ci - 4]] if ci - 4 in cast_of_chunk else []),
                    inc=d_gw[(ci % 3) * 2 + (ci // 3) % 2], dma=True)
                tc = R.op("vector", lambda v, ci=ci: v.tensor_copy(
                    stp["wmodb"][:, ci % 4, :], wm32[:, ci % 4, :]),
                    after=[tl] + ([mm_of_chunk[ci - 4]] if ci - 4 in mm_of_chunk else []), inc=sv)
                cast_of_chunk[ci] = tc
                last_mm = R.op("tensor", lambda t, k=k, ch=ch, ci=ci: t.matmul(
                    psf(512 * ch, 512)[0:2, :], stp["silT"][:, k, :],
                    stp["wmodb"][:, ci % 4, :],
                    start=(k == 0), stop=(k == KC - 1)),
                    after=[tc, t_silT] + (t_mrow if (half == 1 and k == 0 and ch == 0) else []), inc=st_)
                mm_of_chunk[ci] = last_mm
                if ci % 6 == 5 and asm_next[0] < len(chunks):
                    ai = asm_next[0]
                    emit_asm_chunk(ai, [t_silT])
                    if ai + 2 < len(chunks):
                        emit_asm_load(ai + 2)
                    asm_next[0] += 1
        tev = R.op("scalar", lambda s, half=half: s.copy(
            stp["mrow2"][0:2, :], psf(0, 3 * D)[0:2, :]),
            after=[last_mm] + t_mrow, inc=ss)
        tsp = R.op("gpsimd", lambda g, half=half: g.dma_start(
            out=stp["mrow1"][0:1, :, half * 3 * D:(half + 1) * 3 * D],
            in_=stp["mrow2"][0:2, :]), after=[tev, t_lib], inc=d_sep, dma=True)
        if half == 0:
            tsp = R.op("sync", lambda s: s.dma_start(
                out=stp["scols"][0:1, :, :].rearrange("p a d -> p (a d)"),
                in_=stp["mrow2"][0:2, 0:2 * D]),
                after=[tev, tsp], inc=d_sc, dma=True)
        t_mrow.append(tsp)
    while asm_next[0] < len(chunks):
        ai = asm_next[0]
        emit_asm_chunk(ai, [t_silT])
        if ai + 2 < len(chunks):
            emit_asm_load(ai + 2)
        asm_next[0] += 1
    t_asm = asm_state["last"]
    t_sep = t_mrow[-1]

    if stage == -4:
        return _early_exit([t_sep, t_mrow[-1]])

    def mr(b, i):
        return stp["mrow1"][0:1, b, i * D:(i + 1) * D]

    t_clamp = None
    for b in range(2):
        for gi_ in (2, 5):
            tsg = R.op("vector", lambda v, b=b, gi_=gi_: v.tensor_scalar(
                stp["crow"][0:1, 0:D], mr(b, gi_), 0.0, 2.0, ALU.is_ge, ALU.mult),
                after=[t_sep] + ([t_clamp] if t_clamp else []), inc=sv)
            tsa0 = R.op("scalar", lambda s, b=b, gi_=gi_: s.activation(
                stp["crow"][0:1, D:2 * D], mr(b, gi_), AF.Abs),
                after=[t_sep, tsg], inc=ss)
            tsa = R.op("vector", lambda v: v.tensor_scalar(
                stp["crow"][0:1, D:2 * D], stp["crow"][0:1, D:2 * D], 0.25, None, ALU.max),
                after=[tsa0], inc=sv)
            t_clamp = R.op("vector", lambda v, b=b, gi_=gi_: v.scalar_tensor_tensor(
                mr(b, gi_), stp["crow"][0:1, 0:D], -1.0, stp["crow"][0:1, D:2 * D],
                ALU.add, ALU.mult),
                after=[tsg, tsa], inc=sv)

    t_a2 = R.op("vector", lambda v: v.tensor_scalar(stp["rbf2"][0:1, :], mr(0, 4), 1.0, None, ALU.add),
                after=[t_sep], inc=sv)

    t_cols = None
    tt = None
    for b in range(2):
        for k in range(KC):
            tt = R.op("tensor", lambda t, b=b, k=k: t.transpose(
                psf(3840 + 8 * b + k, 1)[:, :], stp["scols"][0:1, 2 * b + 1, 128 * k:128 * (k + 1)],
                idn_f[0:1, 0:1]),
                after=[t_sep, t_asm] + ([tt] if tt else []), inc=st_)
            tt = R.op("tensor", lambda t, b=b, k=k: t.transpose(
                psf(3856 + 8 * b + k, 1)[:, :], stp["scols"][0:1, 2 * b, 128 * k:128 * (k + 1)],
                idn_f[0:1, 0:1]),
                after=[tt], inc=st_)
    t_qsc = R.op("vector", lambda v: v.tensor_scalar(
        qsc.rearrange("p a b -> p (a b)"), psf(3840, 16), 1.0, None, ALU.add),
        after=[tt], inc=sv)
    t_cols = R.op("vector", lambda v: v.tensor_copy(
        sac.rearrange("p a b -> p (a b)"), psf(3856, 16)),
        after=[tt, t_qsc], inc=sv)

    def rep(row_ap, dst_ap, after):
        tm = None
        for hf in range(D // 512):
            tm = R.op("tensor", lambda t, hf=hf: t.matmul(
                psf(3072 + 512 * hf, 512), ones_b[0:1, 0:128], row_ap[:, 512 * hf:512 * (hf + 1)],
                start=True, stop=True),
                after=list(after) + [t_on, t_cols, t_asm], inc=st_)
        return R.op("scalar", lambda s: s.copy(dst_ap, psf(3072, D)), after=[tm], inc=ss)

    t_A2 = rep(stp["rbf2"][0:1, :], A2r, [t_a2])
    t_B2 = rep(mr(0, 3), B2r, [t_sep, t_A2])
    t_gb = rep(mr(0, 5), gbr, [t_clamp, t_B2])

    if stage == -3:
        return _early_exit([t_cols, t_clamp, t_a2])
    if stage == -2:
        return _early_exit([t_A2, t_B2, t_gb, t_cols, t_clamp])
    BWOF = (0, 1536)
    t_bw_last = None
    for b in range(2):
        for jj in range(QKW // 128):
            for k in range(KC):
                t_bw_last = R.op("tensor", lambda t, b=b, jj=jj, k=k: t.matmul(
                    psf(BWOF[b], QKW)[0:1, 128 * jj:128 * (jj + 1)], sac[:, b, k:k + 1],
                    wqkvT[:, 0, k, 128 * jj:128 * (jj + 1)],
                    start=(k == 0), stop=(k == KC - 1)),
                    after=[t_asm, t_cols] + ([t_bw_last] if t_bw_last else []), inc=st_)
    t_wq = []
    for k in range(KC):
        t_wq.append(R.op("vector", lambda v, k=k: v.tensor_scalar(
            wqkvT[:, 1, k, :], wqkvT[:, 0, k, :], qsc[:, 1, k:k + 1], None, ALU.mult),
            after=[t_asm, t_cols, t_bw_last], inc=sv))
        t_wq.append(R.op("vector", lambda v, k=k: v.tensor_scalar(
            wqkvT[:, 0, k, :], wqkvT[:, 0, k, :], qsc[:, 0, k:k + 1], None, ALU.mult),
            after=[t_wq[-1]], inc=sv))
    t_wqkv = t_wq[-1]
    t_Bw = []
    for b in range(2):
        te = R.op("scalar", lambda s, b=b: s.copy(Bw[0:1, b, 0:QKW], psf(BWOF[b], QKW)[0:1, :]),
                  after=[t_bw_last], inc=ss)
        te = R.op("vector", lambda v, b=b: v.tensor_copy(Bw[0:1, b, QKW:QKW + HD], Bw[0:1, b, D:D + HD]),
                  after=[te], inc=sv)
        te = R.op("vector", lambda v, b=b: v.tensor_copy(Bw[0:1, b, QKW + HD:QKW2], Bw[0:1, b, D:D + HD]),
                  after=[te], inc=sv)
        t_Bw.append(te)

    if stage == -1:
        return _early_exit([t_wqkv, t_Bw[1]])
    t_pj = []
    tge = None
    for b in (1, 0):
        tg = None
        for hf in range(D // 512):
            tg = R.op("tensor", lambda t, b=b, hf=hf: t.matmul(
                psf(3072 + 512 * hf, 512), ones_b[0:1, 0:128], mr(b, 2)[:, 512 * hf:512 * (hf + 1)],
                start=True, stop=True),
                after=[t_clamp, t_on, t_asm] + ([t_pj[-1]] if t_pj else [t_gb]), inc=st_)
        tge = R.op("scalar", lambda s: s.copy(stp["grep"], psf(3072, D)), after=[tg], inc=ss)
        for k in range(KC):
            t_pj.append(R.op("vector", lambda v, k=k, b=b: v.tensor_tensor(
                projT[:, b, k, :], projT[:, 0, k, :], stp["grep"], ALU.mult),
                after=[tge, t_asm] + ([t_pj[-1]] if t_pj else []), inc=sv))
    t_projd = t_pj[-1]
    t_R = [R.op("vector", lambda v, b=b: v.tensor_tensor(
        Rr[0:1, b, :], mr(b, 2), stp["pjbs"][0:1, :], ALU.mult),
        after=[t_clamp, t_pjb], inc=sv) for b in range(2)]

    t_setup = [t_wqkv, t_projd, t_Bw[1], t_R[1], t_A2, t_B2, t_gb, t_on,
               t_onc, t_cA, t_cB, t_b1, t_gi, t_stk, t_sw]

    d_xp = (d_x0, d_x1)
    d_op = (d_o0, d_o1)
    t_xload = {}
    for w in range(min(2, NW)):
        t_xload[w] = R.op("sync", lambda s, w=w: s.dma_start(
            out=att["xb"][:, w % 2, :, :],
            in_=x_in[W * w:W * (w + 1), :].rearrange("(t p) d -> p t d", p=128)),
            after=t_setup, inc=d_xp[w % 2], dma=True)

    def emit_ln_dve(xt, dst, ti, sel, aft, aft_sq, aft_aff):
        tb1a = R.op("vector", lambda v, xt=xt, ti=ti, sel=sel: v.bn_stats(
            mvt[:, sel, ti, 0:6], xt[:, 0:D // 2]), after=aft, inc=sv)
        tb1 = R.op("vector", lambda v, xt=xt, ti=ti, sel=sel: v.bn_stats(
            mvt[:, sel, ti, 6:12], xt[:, D // 2:D]), after=[tb1a], inc=sv)
        tb2 = R.op("vector", lambda v, ti=ti, sel=sel: v.bn_aggr(
            stt[:, sel, ti, :], mvt[:, sel, ti, 0:12]), after=[tb1], inc=sv)
        tv1 = R.op("vector", lambda v, ti=ti, sel=sel: v.tensor_scalar(
            mvt[:, sel, ti, 12:13], stt[:, sel, ti, 1:2], EPS, None, ALU.add),
            after=[tb2] + aft_sq, inc=sv)
        ts1 = R.op("scalar", lambda s, ti=ti, sel=sel: s.activation(
            mvt[:, sel, ti, 13:14], mvt[:, sel, ti, 12:13], AF.Sqrt), after=[tv1], inc=ss)
        tv2 = R.op("vector", lambda v, ti=ti, sel=sel: v.reciprocal(
            mvt[:, sel, ti, 12:13], mvt[:, sel, ti, 13:14]), after=[ts1], inc=sv)
        tv3 = R.op("vector", lambda v, ti=ti, sel=sel: v.tensor_scalar(
            mvt[:, sel, ti, 13:14], stt[:, sel, ti, 0:1], mvt[:, sel, ti, 12:13], -1.0,
            ALU.mult, ALU.mult), after=[tv2], inc=sv)
        ta = R.op("vector", lambda v, ti=ti, sel=sel, xt=xt, dst=dst: v.tensor_scalar(
            dst, xt, mvt[:, sel, ti, 12:13], mvt[:, sel, ti, 13:14], ALU.mult, ALU.add),
            after=[tv3] + aft_aff, inc=sv)
        return ta, ts1

    t_comb_prev = None
    t_ate_prev = None
    t_free_xmo = {0: [], 1: []}
    t_xmf = []
    t_h2_store = None
    t_win_last = None
    P = None
    h2_pend = None
    h2_sq, h2_rc, h2_sq2, h2_rc2 = {}, {}, {}, {}
    t_sqtok, t_rctok = {}, {}

    t_aff_n = []
    if NW > 0 and stage >= 1:
        for ti in range(TPW):
            xt0 = att["xb"][:, 0, ti, :]
            ta0, tq0 = emit_ln_dve(xt0, att["ln1"][:, ti, :], ti, 0,
                                   [t_xload[0]] + t_setup, [], t_setup)
            t_aff_n.append(ta0)
            t_sqtok[ti] = tq0

    for w in range(NW if stage >= 1 else 0):
        b = WB[w]
        ffn = w < WFN
        t_aff = t_aff_n
        t_lt = None
        for ti in range(TPW):
            for k in range(KC):
                slot = 0 if (ti * KC + k) % 2 == 0 else 512
                tt = R.op("tensor", lambda t, ti=ti, k=k, slot=slot: t.matmul(
                    psf(slot, 128), att["ln1"][:, ti, 128 * k:128 * (k + 1)], idn_b,
                    start=True, stop=True),
                    after=[t_aff[ti]] +
                          (([P["expB"]] if P else t_setup) if (ti == 0 and k == 0) else []) +
                          ([t_lt] if t_lt else []), inc=st_)
                t_lt = R.op("scalar", lambda s, ti=ti, k=k, slot=slot: s.copy(
                    att["ln1T"][:, k, 128 * ti:128 * (ti + 1)], psf(slot, 128)),
                    after=[tt], inc=ss)
        t_lnT = t_lt
        t_qkv_j = {}
        for j in range(JQK2):
            if P is None:
                jg = []
            elif j < 4:
                jg = [P["expB"]]
            elif j < 8:
                jg = [P["tze7"]]
            else:
                jg = [P["comb0"]]
            for k in range(KC):
                R.op("tensor", lambda t, j=j, k=k, b=b: t.matmul(
                    psf(j * W, W), wqkvT[:, b, k, 128 * j:128 * (j + 1)], att["ln1T"][:, k, :],
                    start=(k == 0), stop=False),
                    after=[t_lnT] + jg, inc=st_)
            t_qkv_j[j] = R.op("tensor", lambda t, j=j, b=b: t.matmul(
                psf(j * W, W), Bw[0:1, b, 128 * j:128 * (j + 1)], ones_b[0:1, 0:W],
                start=False, stop=True), after=[t_lnT], inc=st_)
        t_sqk = R.op("scalar", lambda s: s.activation(att["sqb"][0:HD, JQK - 1, :],
                                                      psf((JQK - 1) * W, W)[0:HD, :], AF.Square),
                     after=[t_qkv_j[JQK - 1], t_qkv_j[JQK]], inc=ss)
        t_ev1 = R.op("scalar", lambda s: s.copy(
            att["qkvT"][:, JQK - 1:JQK + 1, :].rearrange("p a b -> p (a b)"),
            psf((JQK - 1) * W, 2 * W)), after=[t_qkv_j[JQK - 1], t_qkv_j[JQK]], inc=ss)
        t_ev0 = R.op("scalar", lambda s: s.copy(
            att["qkvT"][:, 0:4, :].rearrange("p a b -> p (a b)"),
            psf(0, 4 * W)), after=[t_qkv_j[j] for j in range(4)], inc=ss)
        t_sqA = R.op("scalar", lambda s: s.activation(
            att["sqb"][:, 0:4, :].rearrange("p a b -> p (a b)"),
            att["qkvT"][:, 0:4, :].rearrange("p a b -> p (a b)"), AF.Square),
            after=[t_ev0], inc=ss)
        t_ev2 = R.op("scalar", lambda s: s.copy(
            att["qkvT"][:, 4:8, :].rearrange("p a b -> p (a b)"),
            psf(4 * W, 4 * W)), after=[t_qkv_j[j] for j in range(4, 8)], inc=ss)
        t_sqB = R.op("scalar", lambda s: s.activation(
            att["sqb"][:, 4:8, :].rearrange("p a b -> p (a b)"),
            att["qkvT"][:, 4:8, :].rearrange("p a b -> p (a b)"), AF.Square),
            after=[t_ev2], inc=ss)
        t_k2 = R.op("tensor", lambda t: t.matmul(psf(3072, W)[0:1, :], colA[0:HD, :],
                                                 att["sqb"][0:HD, JQK - 1, :], start=True, stop=True),
                    after=[t_sqk, t_cA] + ([P["comb1"]] if P else []), inc=st_)
        t_k2e = R.op("vector", lambda v: v.tensor_reduce(km2[0:1, 0:1], psf(3072, W)[0:1, :], AX.X, ALU.max),
                     after=[t_k2], inc=sv)

        H2 = H // 2
        t_sc = {}
        t_exp = {}
        t_zs = {}
        t_pv = {}
        tze = {}
        t_ate_l = {}
        ZSLOTS = (1024, 1536, 1280, 1792)

        def emit_scores(h):
            evg = t_ev0 if h < 8 else t_ev2
            gate = [t_ev1, evg] + ([t_exp[h - 2]] if h >= 2 else [])
            qj, i = h // 2, h % 2
            qp = i * HD
            kcj = JQK - 1 if i == 0 else JQK
            last = None
            for kc2 in range(TPW):
                last = R.op("tensor", lambda t, h=h, qj=qj, qp=qp, kc2=kc2, kcj=kcj: t.matmul(
                    psf(512 * (h % 2), TPW, W)[:, kc2, :],
                    att["qkvT"][qp:qp + HD, kcj, 128 * kc2:128 * (kc2 + 1)],
                    att["qkvT"][qp:qp + HD, qj, :],
                    start=True, stop=True),
                    after=gate, inc=st_)
            t_sc[h] = last

        def emit_exp(h):
            gate = [t_sc[h], t_uB]
            if h >= 2:
                gate += [t_pv[h - 2], t_zs[(h - 2) // 2]]
            t_exp[h] = R.op("scalar", lambda s, h=h: s.activation(
                att["E"][:, h % 2, :, :].rearrange("p a b -> p (a b)"),
                psf(512 * (h % 2), TPW * W), AF.Exp, scale=SCALE, bias=ubias),
                after=gate, inc=ss)

        def emit_pv(h):
            cj, i = h // 2, h % 2
            cp = i * HD
            gate = [t_exp[h], t_vones]
            last = None
            for kc2 in range(TPW):
                last = R.op("tensor", lambda t, h=h, cj=cj, cp=cp, kc2=kc2: t.matmul(
                    PS[cp:cp + HD, 2048 + cj * W:2048 + (cj + 1) * W],
                    vbuf[:, kc2, 0:HD],
                    att["E"][:, h % 2, kc2, :],
                    start=(kc2 == 0), stop=(kc2 == TPW - 1)),
                    after=gate, inc=st_)
            t_pv[h] = last

        def emit_zrep(cj):
            gate = [t_exp[2 * cj + 1], t_A64, t_B64] + ([tze[cj - 2]] if cj >= 2 else [])
            last = None
            for i in range(2):
                lt = A64 if i == 0 else B64
                for kc2 in range(TPW):
                    last = R.op("tensor", lambda t, cj=cj, i=i, kc2=kc2, lt=lt: t.matmul(
                        psf(ZSLOTS[cj % 4], W), lt, att["E"][:, i, kc2, :],
                        start=(i == 0 and kc2 == 0), stop=(i == 1 and kc2 == TPW - 1)),
                        after=gate, inc=st_)
            t_zs[cj] = last

        def emit_tze(cj):
            tze[cj] = R.op("vector", lambda v, cj=cj: v.reciprocal_approx_fast(
                zrs32[:, cj % 4, :], psf(ZSLOTS[cj % 4], W)), after=[t_zs[cj]], inc=sv)

        def emit_ate(cj):
            gate = [tze[cj], t_pv[2 * cj], t_pv[2 * cj + 1]]
            if cj % 2 == 0:
                gate.append(t_pv[2 * cj + 3])
            t_ate_l[cj] = R.op("vector", lambda v, cj=cj: v.tensor_tensor(
                att["attnT"][:, cj, :], PS[:, 2048 + cj * W:2048 + (cj + 1) * W],
                zrs32[:, cj % 4, :], ALU.mult),
                after=gate, inc=sv)

        emit_scores(0)
        t_q2 = None
        for k in range(KC):
            t_q2 = R.op("tensor", lambda t, k=k: t.matmul(
                psf(3328, W)[0:1, :], ones_c, att["sqb"][:, k, :],
                start=(k == 0), stop=(k == KC - 1)),
                after=[(t_sqA if k < 4 else t_sqB), t_k2e, t_onc], inc=st_)
        t_q2e = R.op("vector", lambda v: v.tensor_reduce(km2[0:1, 1:2], psf(3328, W)[0:1, :], AX.X, ALU.max),
                     after=[t_q2], inc=sv)
        tq2 = R.op("vector", lambda v: v.tensor_tensor(km2[0:1, 1:2], km2[0:1, 1:2], km2[0:1, 0:1], ALU.mult),
                   after=[t_q2e, t_k2e], inc=sv)
        tq3 = R.op("scalar", lambda s: s.activation(km2[0:1, 3:4], km2[0:1, 1:2], AF.Sqrt),
                   after=[tq2], inc=ss)
        tq4 = R.op("vector", lambda v: v.tensor_scalar(km2[0:1, 3:4], km2[0:1, 3:4], -SCALE, None, ALU.mult),
                   after=[tq3], inc=sv)
        t_uB = R.op("gpsimd", lambda g: g.partition_broadcast(ubias, km2[0:1, 3:4]),
                    after=[tq4], inc=sg)
        t_vb = None
        for kc2 in range(TPW):
            tt = R.op("tensor", lambda t, kc2=kc2: t.matmul(
                psf(2816, HD), att["qkvT"][HD:2 * HD, JQK - 1, 128 * kc2:128 * (kc2 + 1)],
                idn_b[HD:2 * HD, HD:2 * HD], start=True, stop=True),
                after=[t_ev1] + ([t_vb] if t_vb else []), inc=st_)
            t_vb = R.op("scalar", lambda s, kc2=kc2: s.copy(vbuf[:, kc2, 0:HD], psf(2816, HD)),
                        after=[tt], inc=ss)
        t_vones = R.op("vector", lambda v: v.memset(vbuf[:, :, HD:HD + 1], 1.0),
                       after=[t_vb], inc=sv)
        def emit_h2_tile(wp, combp, ti, t_h2w):
            xo = att["xmo"][:, wp % 2, ti, :]
            tah, tqh = emit_ln_dve(xo, att["h2b"][:, ti, :], ti, 1,
                                   [combp[ti]] + ([t_h2_store] if t_h2_store else []),
                                   [h2_sq[ti]] if h2_sq else [], [])
            tm1 = R.op("vector", lambda v, ti=ti: v.tensor_tensor(
                att["h2b"][:, ti, :], att["h2b"][:, ti, :], A2r, ALU.mult), after=[tah], inc=sv)
            tm2 = R.op("vector", lambda v, ti=ti: v.tensor_tensor(
                att["h2b"][:, ti, :], att["h2b"][:, ti, :], B2r, ALU.add), after=[tm1], inc=sv)
            t_h2w.append(tm2)
            h2_sq2[ti] = tqh
        t_h2w_acc = []
        emit_exp(0)
        for h in range(1, 2 * H2):
            emit_scores(h)
            if h % 2 == 0 and h >= 2:
                emit_zrep(h // 2 - 1)
                emit_tze(h // 2 - 1)
            emit_exp(h)
            emit_pv(h - 1)
            if h % 2 == 0 and h >= 4:
                emit_ate(h // 2 - 2)
            if h2_pend is not None and h in (3, 6):
                wp, combp = h2_pend
                emit_h2_tile(wp, combp, 0 if h == 3 else 1, t_h2w_acc)
                if h == 6:
                    h2_sq, h2_sq2 = dict(h2_sq2), {}
                    t_h2_store = R.op("sync", lambda s, wp=wp: s.dma_start(
                        out=h2hb[W * wp:W * (wp + 1), :].rearrange("(t p) d -> p t d", p=128),
                        in_=att["h2b"][:, :, :]), after=t_h2w_acc, inc=d_h2, dma=True)
                    t_free_xmo[wp % 2] = t_free_xmo[wp % 2] + t_h2w_acc
                    h2_pend = None
                    t_h2w_acc = []
            if h in (9, 12) and w + 1 < NW:
                if h == 9:
                    t_aff_n = []
                    sq_n = {}
                ti = 0 if h == 9 else 1
                xt1 = att["xb"][:, (w + 1) % 2, ti, :]
                tan, tqn = emit_ln_dve(xt1, att["ln1"][:, ti, :], ti, 0,
                                       [t_xload[w + 1]],
                                       [t_sqtok[ti]] if t_sqtok else [], [t_lnT])
                t_aff_n.append(tan)
                sq_n[ti] = tqn
                if h == 12:
                    t_sqtok = {ti2: sq_n[ti2] for ti2 in range(TPW)}
        emit_pv(2 * H2 - 1)
        emit_zrep(H2 - 1)
        emit_tze(H2 - 1)
        emit_ate(H2 - 2)
        emit_ate(H2 - 1)
        t_pm = {}
        for k in range(KC):
            for q in range(4):
                ti, hf = q // 2, q % 2
                gate = [t_ate_l[k]]
                if k == 0:
                    gate.append(t_ate_l[2 * q + 1])
                else:
                    gate.append(t_pm[q])
                t_pm[q] = R.op("tensor", lambda t, k=k, q=q, ti=ti, hf=hf, b=b: t.matmul(
                    psf(2048 + 512 * q, 512),
                    att["attnT"][:, k, 128 * ti:128 * (ti + 1)],
                    projT[:, b, k, 512 * hf:512 * (hf + 1)],
                    start=(k == 0), stop=False),
                    after=gate, inc=st_)
        for q in range(4):
            hf = q % 2
            t_pm[q] = R.op("tensor", lambda t, q=q, hf=hf, b=b: t.matmul(
                psf(2048 + 512 * q, 512), ones_b[0:1, 0:128],
                Rr[0:1, b, 512 * hf:512 * (hf + 1)],
                start=False, stop=True), after=[t_pm[q], t_on], inc=st_)
        t_comb = []
        for ti in range(TPW):
            t_comb.append(R.op("vector", lambda v, ti=ti, w=w: v.tensor_tensor(
                att["xmo"][:, w % 2, ti, :], psf(2048 + ti * 1024, 1024),
                att["xb"][:, w % 2, ti, :], ALU.add),
                after=[t_pm[2 * ti], t_pm[2 * ti + 1]] + t_free_xmo[w % 2], inc=sv))
        t_ate = t_ate_l[H2 - 1]
        free_xmo = []
        if not ffn:
            tso = R.op("sync", lambda s, w=w: s.dma_start(
                out=out[W * w:W * (w + 1), :].rearrange("(t p) d -> p t d", p=128),
                in_=att["xmo"][:, w % 2, :, :]), after=t_comb, inc=d_op[w % 2], dma=True)
            free_xmo = [tso]
        else:
            for ti in range(TPW):
                t_xmf.append(R.op("sync", lambda s, ti=ti, w=w: s.dma_start(
                    out=xmhb[W * w + 128 * ti:W * w + 128 * (ti + 1), :],
                    in_=att["xmo"][:, w % 2, ti, :]), after=[t_comb[ti]], inc=d_xmp[w % 2], dma=True))
            h2_pend = (w, list(t_comb))
            free_xmo = [t_xmf[-1], t_xmf[-2]]
        t_free_xmo[w % 2] = free_xmo
        if w + 2 < NW:
            t_xload[w + 2] = R.op("sync", lambda s, w=w: s.dma_start(
                out=att["xb"][:, w % 2, :, :],
                in_=x_in[W * (w + 2):W * (w + 3), :].rearrange("(t p) d -> p t d", p=128)),
                after=[t_comb[-1]], inc=d_xp[w % 2], dma=True)
        P = dict(expA=t_exp[2 * H2 - 2], expB=t_exp[2 * H2 - 1], tze7=tze[H2 - 1],
                 comb0=t_comb[0], comb1=t_comb[1])
        t_comb_prev = t_comb[-1]
        t_ate_prev = t_ate
        t_win_last = t_comb[-1]

    att_done = ([t_win_last, t_ate_prev, t_h2_store, d_o0.tok(), d_o1.tok(),
                 d_h2.tok(), d_xmp[0].tok(), d_xmp[1].tok()] if stage >= 1 else
                [x for x in t_setup])
    aw_w1 = [t_qkv_j[JQK]] if stage >= 1 else att_done
    aw_w2 = ([t_qkv_j[JQK], t_pm[3]] if stage >= 1 else att_done)
    aw_gath = ([t_qkv_j[JQK], t_h2_store, d_h2.tok()] if stage >= 1 else att_done)
    if stage < 2:
        R.op("sync", lambda s: s.dma_start(out=out[0:128, :], in_=gbr),
             after=att_done + [t_gb], inc=d_o0, dma=True)
        R.op("sync", lambda s: s.nop(), after=[s_.tok() for s_ in Sem.REGISTRY if s_.n > 0])
        with nc.Block() as block:
            R.emit(block)
        nc.finalize()
        return nc, c

    octs = [(e, oc) for e in range(E) for oc in range(OC)]
    t_w1l, t_w2l = {}, {}
    t_w1_cons = {}
    t_w2_cons = {}
    t_gath = {}
    t_gath_cons = {}
    t_b2l = {}
    t_b2_cons = {}

    def load_oct(i):
        e, oc = octs[i]
        aft_w1 = aw_w1 if i < 2 else [t_w1_cons[i - 2]]
        aft_w2 = aw_w2 if i < 2 else [t_w2_cons[i - 2]]
        t_w1l[i] = R.op("gpsimd", lambda g, e=e, oc=oc: g.dma_start(
            out=moe["w1s"][:, oc % 2, :, :],
            in_=w1[e, :, 512 * oc:512 * (oc + 1)].rearrange("(k p) f -> p k f", p=128)),
            after=[t_lib] + aft_w1, inc=d_w1p[oc % 2], dma=True)
        t_w2l[i] = R.op("gpsimd", lambda g, e=e, oc=oc: g.dma_start(
            out=moe["w2s"][:, oc % 2, :, :],
            in_=w2[e, 512 * oc:512 * (oc + 1), :].rearrange("(m p) d -> p m d", p=128)),
            after=[t_lib] + aft_w2, inc=d_w2p[oc % 2], dma=True)

    def load_gath(e):
        aft = aw_gath if e < 1 else [t_gath_cons[e - 1]]
        t_gath[e] = R.op("gpsimd", lambda g, e=e: g.dma_gather(
            moe["gath"][:, :, :], h2hb, gix[:, e, :], REG, REG, D,
            elem_step=D, transpose=False),
            after=[t_lib, t_gi] + aft, inc=d_gtp[e % 2], dma=True)

    def load_b2(e):
        aft = att_done if e < 2 else [t_b2_cons[e - 2]]
        t_b2l[e] = R.op("gpsimd", lambda g, e=e: g.dma_start(
            out=moe["b2e"][0:1, e % 2, :], in_=b2[e:e + 1, :]),
            after=[t_lib] + aft, inc=d_b2p[e % 2], dma=True)

    NE_PRE = E if stage >= 3 else 1
    for i in range(min(2, NE_PRE * OC)):
        load_oct(i)
    load_gath(0)
    for e in range(min(2, NE_PRE)):
        load_b2(e)
    t_memz = R.op("vector", lambda v: v.memset(moe["out2"].rearrange("p a b -> p (a b)"), 0.0),
                  after=att_done, inc=sv)

    t_o2e = {}
    t_hb_cons = {}
    t_gtr_done = {}
    t_last_gel = None
    NE_RUN = E if stage >= 3 else 1
    for e in range(NE_RUN):
        t_gtr = None
        for nt in range(NT):
            for k in range(KC):
                bank = 6 + (k % 2)
                ttr = R.op("tensor", lambda t, nt=nt, k=k, bank=bank: t.matmul(
                    psf(bank * 512, 128), moe["gath"][:, nt, 128 * k:128 * (k + 1)], idn_b,
                    start=True, stop=True),
                    after=[t_gath[e]] + ([t_gtr] if t_gtr else []) +
                          ([t_win_last] if (e == 0 and nt == 0 and k == 0 and stage >= 1) else []) +
                          ([t_gtr_done[e - 2]] if (e - 2 in t_gtr_done and nt == 0 and k == 0) else []) +
                          ([t_last_gel] if (t_last_gel and nt == 0 and k == 0) else []), inc=st_)
                t_gtr = R.op("vector", lambda v, e=e, nt=nt, k=k, bank=bank: v.tensor_copy(
                    moe["gathT"][:, e % 2, k, 128 * nt:128 * (nt + 1)], psf(bank * 512, 128)),
                    after=[ttr], inc=sv)
        t_gtr_done[e] = t_gtr
        if e + 1 < NE_RUN:
            t_gath_cons[e] = t_gtr
            load_gath(e + 1)
        t_w2mm = None
        gel_of_oct = {}
        for oc in range(OC):
            i = e * OC + oc
            tw1, tw2 = t_w1l[i], t_w2l[i]
            t_gel = []
            for m in range(4):
                bank = 6 + (m % 2)
                mm = None
                gate = [tw1, t_gtr_done[e]]
                if len(t_gel) >= 2:
                    gate.append(t_gel[-2])
                elif oc >= 1 and len(t_gel) < 2:
                    gate.append(gel_of_oct[oc - 1][-2 + len(t_gel)])
                if (i % 2) in t_hb_cons and m == 0:
                    gate.append(t_hb_cons[i % 2])
                for k in range(KC):
                    mm = R.op("tensor", lambda t, e=e, oc=oc, m=m, k=k, bank=bank: t.matmul(
                        psf(bank * 512, CAP), moe["w1s"][:, oc % 2, k, 128 * m:128 * (m + 1)],
                        moe["gathT"][:, e % 2, k, 0:CAP],
                        start=(k == 0), stop=(k == KC - 1)),
                        after=gate, inc=st_)
                ffg = oc * 512 + m * 128
                t_gel.append(R.op("scalar", lambda s, e=e, oc=oc, m=m, bank=bank, ffg=ffg: s.activation(
                    moe["hb"][:, oc % 2, m, :], psf(bank * 512, CAP),
                    AF.Gelu_apprx_tanh, bias=b1c[:, ffg // 128, e:e + 1]),
                    after=[mm, t_b1], inc=ss))
            gel_of_oct[oc] = t_gel
            t_w1_cons[i] = t_gel[-1]
            for tix in range(NT):
                tsz = TSZ[tix]
                for m in range(4):
                    for hf in range(2):
                        t_w2mm = R.op("tensor", lambda t, oc=oc, m=m, hf=hf, tix=tix, tsz=tsz: t.matmul(
                            psf(tix * 1024, 2, 512)[0:tsz, hf, :],
                            moe["hb"][:, oc % 2, m, 128 * tix:128 * tix + tsz],
                            moe["w2s"][:, oc % 2, m, 512 * hf:512 * (hf + 1)],
                            start=(oc == 0 and m == 0), stop=False),
                            after=[tw2, t_gel[m]], inc=st_)
            t_w2_cons[i] = t_w2mm
            t_hb_cons[i % 2] = t_w2mm
            t_last_gel = t_gel[-1]
            if i + 2 < NE_RUN * OC:
                load_oct(i + 2)
        for tix in range(NT):
            tsz = TSZ[tix]
            for hf in range(2):
                t_w2mm = R.op("tensor", lambda t, e=e, tix=tix, hf=hf, tsz=tsz: t.matmul(
                    psf(tix * 1024, 2, 512)[0:tsz, hf, :], ones_b[0:1, 0:tsz],
                    moe["b2e"][0:1, e % 2, 512 * hf:512 * (hf + 1)],
                    start=False, stop=True),
                    after=[t_b2l[e], t_on], inc=st_)
        t_b2_cons[e] = t_w2mm
        evs = []
        for tix in range(NT):
            tsz = TSZ[tix]
            evs.append(R.op("scalar", lambda s, e=e, tix=tix, tsz=tsz: s.copy(
                moe["out2"][0:tsz, e * NT + tix, :],
                psf(tix * 1024, 1024)[0:tsz, :]),
                after=[t_w2mm, t_memz], inc=ss))
        t_o2e[e] = evs[-1]
        if e + 2 < NE_RUN:
            load_b2(e + 2)

    if stage < 4:
        R.op("sync", lambda s: s.dma_start(out=out[0:128, :], in_=gbr),
             after=[t_o2e[NE_RUN - 1], t_gb], inc=d_o0, dma=True)
        R.op("sync", lambda s: s.nop(), after=[s_.tok() for s_ in Sem.REGISTRY if s_.n > 0])
        with nc.Block() as block:
            R.emit(block)
        nc.finalize()
        return nc, c
    t_st_final = []
    t_mt_cons = {}
    prev_pass_last = None
    t_txi = {}
    xmi_cons = {}
    for pas in range(2):
        rcs = list(range(pas * (RC - RC // 2) if False else pas * (RC // 2), (pas + 1) * (RC // 2)))
        if RC == 1:
            rcs = [0] if pas == 0 else []
        for r in rcs[:2]:
            t_txi[r] = R.op("sync", lambda s, r=r: s.dma_start(
                out=moe["xmi"][:, r % 2, :], in_=xmhb[128 * r:128 * (r + 1), :]),
                after=[d_xmp[0].tok(), d_xmp[1].tok(), t_win_last] +
                      ([xmi_cons[r % 2]] if (r % 2) in xmi_cons else []),
                inc=d_xi[r % 2], dma=True)
        t_acc = {}
        for sc in range(SC if rcs else 0):
            e_of = sc // NT
            gate = [t_stk, t_sw, t_iof, t_o2e[e_of]]
            if (sc % 2) in t_mt_cons:
                gate.append(t_mt_cons[sc % 2])
            if sc == 0 and pas == 0:
                gate.append(t_o2e[E - 1])
            if sc == 0 and prev_pass_last is not None:
                gate.append(prev_pass_last)
            tb = R.op("vector", lambda v, sc=sc: v.tensor_scalar(
                moe["Mt"][:, sc % 2, :], iota_f, stokc[:, sc:sc + 1], swc[:, sc:sc + 1],
                ALU.is_equal, ALU.mult), after=gate, inc=sv)
            last = None
            for r in rcs:
                for hf in range(2):
                    last = R.op("tensor", lambda t, sc=sc, r=r, hf=hf: t.matmul(
                        psf((r % max(1, RC // 2)) * 1024, 2, 512)[:, hf, :],
                        moe["Mt"][:, sc % 2, 128 * r:128 * (r + 1)],
                        moe["out2"][:, sc, 512 * hf:512 * (hf + 1)],
                        start=(sc == 0), stop=(sc == SC - 1)),
                        after=[tb], inc=st_)
                    t_acc[r] = last
            t_mt_cons[sc % 2] = last
        for r in rcs:
            tv = R.op("vector", lambda v, r=r: v.tensor_tensor(
                moe["xo"][:, r % 2, :], psf((r % max(1, RC // 2)) * 1024, 1024), gbr, ALU.mult),
                after=[t_acc[r], t_gb] + ([t_st_final[-2][1]] if len(t_st_final) >= 2 else []),
                inc=sv)
            tv2 = R.op("vector", lambda v, r=r: v.tensor_tensor(
                moe["xo"][:, r % 2, :], moe["xo"][:, r % 2, :], moe["xmi"][:, r % 2, :], ALU.add),
                after=[tv, t_txi[r]], inc=sv)
            xmi_cons[r % 2] = tv2
            if r + 2 in rcs:
                t_txi[r + 2] = R.op("sync", lambda s, r=r: s.dma_start(
                    out=moe["xmi"][:, r % 2, :], in_=xmhb[128 * (r + 2):128 * (r + 3), :]),
                    after=[tv2], inc=d_xi[r % 2], dma=True)
            tso = R.op("sync", lambda s, r=r: s.dma_start(
                out=out[128 * r:128 * (r + 1), :], in_=moe["xo"][:, r % 2, :]),
                after=[tv2], inc=d_op[r % 2], dma=True)
            t_st_final.append((tv2, tso))
        if rcs:
            prev_pass_last = t_acc[rcs[-1]]

    R.op("sync", lambda s: s.nop(), after=[s_.tok() for s_ in Sem.REGISTRY if s_.n > 0])

    with nc.Block() as block:
        R.emit(block)
    nc.finalize()
    return nc, c


_PROG_CACHE = {}


def _get_program(key, cfg):
    if key not in _PROG_CACHE:
        _PROG_CACHE[key] = build_program(cfg)
    return _PROG_CACHE[key]


def host_prep_core(cfg, top_idx, combine, row_base):
    c = _derived(cfg)
    E, REG, CAP, SC, S, NROW = c["E"], c["REG"], c["CAP"], c["SC"], c["S"], c["NROW"]
    gidx = np.zeros((16, E * (REG // 16)), np.int16)
    stok = np.full(S, -1.0, np.float32)
    sw = np.zeros(S, np.float32)
    for e in range(E):
        col = np.asarray(top_idx[:, e])
        locs = np.flatnonzero((col >= row_base) & (col < row_base + NROW))
        toks = (col[locs] - row_base).astype(np.int64)
        n = len(toks)
        if n > CAP:
            raise RuntimeError(f"slot capacity overflow e={e}: {n} > {CAP}")
        base = e * REG
        for i in range(n):
            j = base + i
            gidx[j % 16, j // 16] = np.int16(toks[i])
            stok[j] = np.float32(toks[i])
            sw[j] = np.float32(combine[int(col[locs[i]]), e])
    return (np.tile(gidx, (8, 1)),
            np.ascontiguousarray(stok.reshape(SC, 128).T),
            np.ascontiguousarray(sw.reshape(SC, 128).T))


def kernel(L0_attn_x, L0_attn_q_w, L0_attn_kv_w, L0_attn_proj_w, L0_attn_proj_b,
           L0_ffn_combine_weights, L0_ffn_top_idx, L0_ffn_W1, L0_ffn_b1,
           L0_ffn_W2, L0_ffn_b2, L0_c, L0_wmod):
    cfg = FULL_CFG
    c = _derived(cfg)
    ncores = 8
    L = 16384
    x = np.ascontiguousarray(np.asarray(L0_attn_x, dtype=np.float32)).reshape(-1, c["D"])
    top_idx = np.asarray(L0_ffn_top_idx)
    combine = np.asarray(L0_ffn_combine_weights, dtype=np.float32)
    nc, _ = _get_program("full", cfg)

    shared = dict(
        qw=np.ascontiguousarray(np.asarray(L0_attn_q_w, np.float32)),
        kvw=np.ascontiguousarray(np.asarray(L0_attn_kv_w, np.float32)),
        pjw=np.ascontiguousarray(np.asarray(L0_attn_proj_w, np.float32)),
        pjb=np.ascontiguousarray(np.asarray(L0_attn_proj_b, np.float32)).reshape(1, -1),
        c2=np.ascontiguousarray(np.asarray(L0_c, np.float32)),
        wmod=np.ascontiguousarray(np.asarray(L0_wmod, np.float32)),
        w1=np.ascontiguousarray(np.asarray(L0_ffn_W1, np.float32)),
        b1t=np.ascontiguousarray(np.asarray(L0_ffn_b1, np.float32).T),
        w2=np.ascontiguousarray(np.asarray(L0_ffn_W2, np.float32)),
        b2=np.ascontiguousarray(np.asarray(L0_ffn_b2, np.float32)),
    )
    in_maps, row_maps = [], []
    for core in range(ncores):
        rows = np.concatenate([
            np.arange(1024 * core, 1024 * core + 1024),
            np.arange(8192 + 1024 * core, 8192 + 1024 * core + 1024),
            np.arange(L + 2048 * core, L + 2048 * core + 2048)])
        row_maps.append(rows)
        gi, stk, sw = host_prep_core(cfg, top_idx, combine, 1024 * core)
        m = dict(shared)
        m["x_rows"] = np.ascontiguousarray(x[rows])
        m["gidx"] = gi
        m["stok"] = stk
        m["swt"] = sw
        in_maps.append(m)

    res = run_bass_kernel_spmd(nc, in_maps, core_ids=list(range(ncores)))
    outp = np.empty((2 * L, c["D"]), np.float32)
    for core in range(ncores):
        outp[row_maps[core]] = res.results[core]["out_rows"]
    return outp.reshape(2, L, c["D"])


# version 18
# speedup vs baseline: 1.1520x; 1.1178x over previous
import numpy as np
import ml_dtypes
import concourse.bass as bass
import concourse.bacc as bacc
import concourse.mybir as mybir
from concourse import library_config
from concourse.bass_utils import run_bass_kernel_spmd

F32 = mybir.dt.float32
BF16 = mybir.dt.bfloat16
I16 = mybir.dt.int16
U8 = mybir.dt.uint8
AF = mybir.ActivationFunctionType
ALU = mybir.AluOpType
AX = mybir.AxisListType

FULL_CFG = dict(NW=16, WFN=4, WB=[0] * 8 + [1] * 8, W=256, D=1024, H=16, HD=64,
                E=8, FF=4096, CAP=352, REG=384, EPS=1e-5)
SMALL_CFG = dict(NW=4, WFN=2, WB=[0, 0, 1, 1], W=256, D=1024, H=16, HD=64,
                 E=8, FF=1024, CAP=96, REG=128, EPS=1e-5)


def _derived(cfg):
    d = dict(cfg)
    d["KC"] = cfg["D"] // 128
    d["TPW"] = cfg["W"] // 128
    d["OC"] = cfg["FF"] // 512
    d["NT"] = cfg["REG"] // 128
    d["S"] = cfg["E"] * cfg["REG"]
    d["SC"] = d["S"] // 128
    d["NROW"] = cfg["WFN"] * cfg["W"]
    d["RC"] = d["NROW"] // 128
    d["NTOK"] = cfg["NW"] * cfg["W"]
    d["HG"] = cfg["H"] // 4
    sizes, left = [], cfg["CAP"]
    for _ in range(d["NT"]):
        sizes.append(min(128, left))
        left -= sizes[-1]
    d["TSZ"] = sizes
    return d


class Sem:
    REGISTRY = None

    def __init__(self, nc, name):
        self.h = nc.alloc_semaphore(name)
        self.n = 0
        self.name = name
        if Sem.REGISTRY is not None:
            Sem.REGISTRY.append(self)

    def tok(self):
        return (self, self.n)


class Rec:
    ENGS = ("sync", "gpsimd", "vector", "scalar", "tensor")

    def __init__(self, nc):
        self.nc = nc
        self.ops = {e: [] for e in self.ENGS}

    def op(self, eng, build, after=(), inc=None, dma=False):
        waits = [(s, v) for (s, v) in after if v > 0]
        if inc is not None:
            inc.n += 16 if dma else 1
        self.ops[eng].append((waits, build, inc, 16 if dma else 1))
        return inc.tok() if inc is not None else None

    def emit(self, block):
        for eng_name in self.ENGS:
            ops = self.ops[eng_name]

            def body(eng, ops=ops):
                seen = {}
                for waits, build, inc, k in ops:
                    for s, v in waits:
                        if seen.get(s.name, 0) < v:
                            eng.wait_ge(s.h, v)
                            seen[s.name] = v
                    ins = build(eng)
                    if inc is not None:
                        ins.then_inc(inc.h, k)
            getattr(block, eng_name)(body)


class Arena:

    def __init__(self, nc, name, dtype, nelem):
        self.t = nc.alloc_sbuf_tensor(name, [128, nelem], dtype)
        self.dtype = dtype
        self.nelem = nelem
        self.off = 0
        self.marks = []
        self.peak = 0

    def alloc(self, *shape):
        n = int(np.prod(shape))
        off = self.off
        self.off += (n + 15) & ~15
        self.peak = max(self.peak, self.off)
        assert self.off <= self.nelem, f"arena overflow {self.off} > {self.nelem}"
        ap = self.t[:, off:off + n]
        if len(shape) > 1:
            names = [chr(97 + i) for i in range(len(shape))]
            ap = ap.rearrange("p (" + " ".join(names) + ") -> p " + " ".join(names),
                              **{nm: s for nm, s in zip(names, shape)})
        return ap

    def mark(self):
        self.marks.append(self.off)

    def reset(self):
        self.off = self.marks.pop()


def build_program(cfg, stage=99):
    import os
    KDROP = set(os.environ.get('KDROP', '').split(','))
    c = _derived(cfg)
    NW, WFN, WB, W, D, H, HD, E, FF = (c["NW"], c["WFN"], c["WB"], c["W"], c["D"],
                                       c["H"], c["HD"], c["E"], c["FF"])
    KC, CAP, REG, NT, S, SC = c["KC"], c["CAP"], c["REG"], c["NT"], c["S"], c["SC"]
    NROW, RC, NTOK, TPW, OC, HG, TSZ, EPS = (c["NROW"], c["RC"], c["NTOK"], c["TPW"],
                                             c["OC"], c["HG"], c["TSZ"], c["EPS"])
    QKW = D + 2 * HD
    JQK = QKW // 128
    QKW2 = QKW + 128
    JQK2 = JQK + 1
    SCALE = float(HD) ** -0.5

    nc = bacc.Bacc("TRN2", debug=False)
    Sem.REGISTRY = []

    x_in = nc.dram_tensor("x_rows", [NTOK, D], F32, kind="ExternalInput").ap()
    qw = nc.dram_tensor("qw", [D, D], F32, kind="ExternalInput").ap()
    kvw = nc.dram_tensor("kvw", [2 * HD, D], F32, kind="ExternalInput").ap()
    pjw = nc.dram_tensor("pjw", [D, D], F32, kind="ExternalInput").ap()
    pjb = nc.dram_tensor("pjb", [1, D], F32, kind="ExternalInput").ap()
    c2 = nc.dram_tensor("c2", [2, D], F32, kind="ExternalInput").ap()
    wmod = nc.dram_tensor("wmod", [D, 6 * D], F32, kind="ExternalInput").ap()
    w1 = nc.dram_tensor("w1", [E, D, FF], F32, kind="ExternalInput").ap()
    b1t = nc.dram_tensor("b1t", [FF, E], F32, kind="ExternalInput").ap()
    w2 = nc.dram_tensor("w2", [E, FF, D], F32, kind="ExternalInput").ap()
    b2 = nc.dram_tensor("b2", [E, D], F32, kind="ExternalInput").ap()
    gidx = nc.dram_tensor("gidx", [128, E * (REG // 16)], I16, kind="ExternalInput").ap()
    stok_d = nc.dram_tensor("stok", [128, SC], F32, kind="ExternalInput").ap()
    swt_d = nc.dram_tensor("swt", [128, SC], F32, kind="ExternalInput").ap()
    out = nc.dram_tensor("out_rows", [NTOK, D], F32, kind="ExternalOutput").ap()
    h2hb = nc.dram_tensor("h2hb", [NROW, D], BF16, kind="Internal").ap()
    xmhb = nc.dram_tensor("xmhb", [NROW, D], F32, kind="Internal").ap()

    AB = Arena(nc, "ab", BF16, 71 * 1024)
    FA = Arena(nc, "af", F32, 14 * 1024)
    gix = nc.alloc_sbuf_tensor("gix", [128, E * (REG // 16)], I16).ap().rearrange("p (e r) -> p e r", e=E)
    wm32 = nc.alloc_sbuf_tensor("wm32", [128, 4, 512], F32).ap()

    iota_f = FA.alloc(D)
    iota_c = FA.alloc(1)
    idn_f = FA.alloc(128)
    gbr = FA.alloc(D)
    qsc = FA.alloc(2, KC)
    km2 = FA.alloc(4)
    stokc = FA.alloc(SC)
    swc = FA.alloc(SC)
    b1c = FA.alloc(FF // 128, E)
    mvt = FA.alloc(2, TPW, 16)
    stt = FA.alloc(2, TPW, 2)
    idn_b = AB.alloc(128)
    ones_b = AB.alloc(256)
    colA = AB.alloc(1)
    colB = AB.alloc(1)
    A64 = AB.alloc(128)
    B64 = AB.alloc(128)
    ones_c = AB.alloc(1)
    Bw = AB.alloc(2, QKW2)
    Rr = AB.alloc(2, D)
    A2r = AB.alloc(D)
    B2r = AB.alloc(D)
    sac = AB.alloc(2, KC)

    ovb = AB.off
    wqkvT = AB.alloc(2, KC, QKW2)
    projT = AB.alloc(2, KC, D)
    qb2 = AB.off
    AB.mark(); FA.mark()
    att = {}
    att["ln1T"] = AB.alloc(KC, W)
    att["qkvT"] = AB.alloc(JQK2, W)
    att["E"] = AB.alloc(2, TPW, W)
    att["sqb"] = AB.alloc(JQK, W)
    att["attnT"] = AB.alloc(KC, W)
    att["h2b"] = AB.alloc(TPW, D)
    att["ln1"] = AB.alloc(TPW, D)
    vbuf = AB.alloc(TPW, HD + 1)
    att["xb"] = FA.alloc(2, TPW, D)
    att["xmo"] = FA.alloc(2, TPW, D)
    ubias = FA.alloc(1)
    zrs32 = FA.alloc(4, 256)
    AB.reset(); FA.reset()
    AB.mark(); FA.mark()
    stp = {}
    stp["wstage"] = AB.alloc(2, D)
    stp["wmodb"] = AB.alloc(4, 512)
    stp["silT"] = AB.alloc(KC, 2)
    stp["mrow1"] = AB.alloc(2, 6 * D)
    stp["pjbs"] = AB.alloc(D)
    stp["grep"] = AB.alloc(D)
    stp["rbf2"] = AB.alloc(D)
    stp["sil"] = FA.alloc(D)
    stp["mrow2"] = FA.alloc(3 * D)
    stp["crow"] = FA.alloc(2 * D)
    stp["c2s"] = FA.alloc(D)
    stp["scols"] = FA.alloc(4, D)
    AB.reset(); FA.reset()
    AB.off = ovb
    moe = {}
    moe["gath"] = AB.alloc(NT, D)
    moe["gathT"] = AB.alloc(2, KC, REG)
    moe["w1s"] = AB.alloc(2, KC, 512)
    moe["w2s"] = AB.alloc(2, 4, D)
    moe["hb"] = AB.alloc(2, 4, CAP)
    moe["out2"] = AB.alloc(E * NT, D)
    moe["b2e"] = AB.alloc(2, D)
    moe["Mt"] = AB.alloc(2, D)
    moe["xo"] = FA.alloc(2, D)
    moe["xmi"] = FA.alloc(2, D)

    PS = nc.alloc_psum_tensor("PS", [128, 4096], F32)

    def psf(lo, *shape):
        n = int(np.prod(shape))
        ap = PS[:, lo:lo + n]
        if len(shape) > 1:
            names = [chr(97 + i) for i in range(len(shape))]
            ap = ap.rearrange("p (" + " ".join(names) + ") -> p " + " ".join(names),
                              **{nm: s for nm, s in zip(names, shape)})
        return ap

    R = Rec(nc)
    sv = Sem(nc, "sv")
    ss = Sem(nc, "ss")
    st_ = Sem(nc, "st")
    sg = Sem(nc, "sg")
    d_set = Sem(nc, "d_set")
    d_g = Sem(nc, "d_g")
    d_x0 = Sem(nc, "d_x0")
    d_x1 = Sem(nc, "d_x1")
    d_o0 = Sem(nc, "d_o0")
    d_o1 = Sem(nc, "d_o1")
    d_h2 = Sem(nc, "d_h2")
    d_w1 = Sem(nc, "d_w1")
    d_w2 = Sem(nc, "d_w2")
    d_gt = Sem(nc, "d_gt")
    d_b2 = Sem(nc, "d_b2")
    d_sep = Sem(nc, "d_sep")
    d_c2 = Sem(nc, "d_c2")
    d_xmp = [Sem(nc, "d_xm0"), Sem(nc, "d_xm1")]
    d_sc = Sem(nc, "d_sc")
    d_xi = [Sem(nc, "d_xia"), Sem(nc, "d_xib")]
    d_sm = Sem(nc, "d_sm")
    d_cast = [Sem(nc, "d_cast0"), Sem(nc, "d_cast1"), Sem(nc, "d_cast2")]
    d_stg = [Sem(nc, "d_stg0"), Sem(nc, "d_stg1"), Sem(nc, "d_stg2")]
    d_tt = [Sem(nc, "d_tt0"), Sem(nc, "d_tt1")]
    d_gw = [Sem(nc, "d_gw%d" % i) for i in range(6)]
    d_w1p = [Sem(nc, "d_w1a"), Sem(nc, "d_w1b")]
    d_w2p = [Sem(nc, "d_w2a"), Sem(nc, "d_w2b")]
    d_gtp = [Sem(nc, "d_gta"), Sem(nc, "d_gtb")]
    d_b2p = [Sem(nc, "d_b2a"), Sem(nc, "d_b2b")]


    def _early_exit(after_toks):
        R.op("sync", lambda s: s.dma_start(out=out[0:2, 0:D], in_=stp["c2s"][0:2, :]),
             after=list(after_toks), inc=d_o0, dma=True)
        if os.environ.get("KEXIT") == "min":
            R.op("sync", lambda s: s.nop(), after=[d_o0.tok()])
        else:
            R.op("sync", lambda s: s.nop(), after=[s_.tok() for s_ in Sem.REGISTRY if s_.n > 0])
        with nc.Block() as block:
            R.emit(block)
        nc.finalize()
        return nc, c

    if stage == -9:
        R.op("vector", lambda v: v.memset(gbr, 3.0), inc=sv)
        R.op("sync", lambda s: s.dma_start(out=out[0:128, 0:D], in_=gbr), after=[sv.tok()], inc=d_o0, dma=True)
        R.op("sync", lambda s: s.nop(), after=[d_o0.tok()])
        with nc.Block() as block:
            R.emit(block)
        nc.finalize()
        return nc, c
    t_iof = R.op("gpsimd", lambda g: g.iota(iota_f, pattern=[[1, D]], channel_multiplier=0,
                                            allow_small_or_imprecise_dtypes=True), inc=sg)
    t_ioc = R.op("gpsimd", lambda g: g.iota(iota_c, pattern=[[0, 1]], channel_multiplier=1,
                                            allow_small_or_imprecise_dtypes=True), after=[t_iof], inc=sg)
    R.op("gpsimd", lambda g: g.load_library(library_config.mlp), after=[t_ioc], inc=None)
    t_lib = t_ioc
    t_idb = R.op("vector", lambda v: v.tensor_scalar(idn_b, iota_f[:, 0:128], iota_c, None, ALU.is_equal),
                 after=[t_iof, t_ioc], inc=sv)
    t_idf = R.op("vector", lambda v: v.tensor_copy(idn_f, idn_b), after=[t_idb], inc=sv)
    t_on = R.op("vector", lambda v: v.memset(ones_b, 1.0), inc=sv)
    t_onc = R.op("vector", lambda v: v.memset(ones_c, 1.0), inc=sv)
    t_cA = R.op("vector", lambda v: v.tensor_scalar(colA, iota_c, 64.0, None, ALU.is_lt),
                after=[t_ioc], inc=sv)
    t_cB = R.op("vector", lambda v: v.tensor_scalar(colB, iota_c, 64.0, None, ALU.is_ge),
                after=[t_ioc], inc=sv)
    t_A64 = R.op("vector", lambda v: v.tensor_scalar(A64, iota_f[:, 0:128], 64.0, None, ALU.is_lt),
                 after=[t_iof], inc=sv)
    t_B64 = R.op("vector", lambda v: v.tensor_scalar(B64, iota_f[:, 0:128], 64.0, None, ALU.is_ge),
                 after=[t_iof], inc=sv)

    if stage == -8:
        R.op("sync", lambda s: s.dma_start(out=out[0:128, 0:128], in_=idn_f), after=[t_idf], inc=d_o0, dma=True)
        R.op("sync", lambda s: s.nop(), after=[d_o0.tok()])
        with nc.Block() as block:
            R.emit(block)
        nc.finalize()
        return nc, c
    t_c2 = R.op("sync", lambda s: s.dma_start(out=stp["c2s"][0:2, :], in_=c2), inc=d_c2, dma=True)
    if stage == -7:
        R.op("sync", lambda s: s.dma_start(out=out[0:2, 0:D], in_=stp["c2s"][0:2, :]),
             after=[t_c2, t_idf, t_cB], inc=d_o0, dma=True)
        R.op("sync", lambda s: s.nop(), after=[d_o0.tok()])
        with nc.Block() as block:
            R.emit(block)
        nc.finalize()
        return nc, c
    if "pjb" in KDROP:
        t_pjb = t_lib
    else:
        t_pjb = R.op("gpsimd", lambda g: g.dma_start(out=stp["pjbs"][0:1, :], in_=pjb),
                     after=[t_lib], inc=d_g, dma=True)
    if "smloads" not in KDROP:
        R.op("sync", lambda s: s.dma_start(out=b1c, in_=b1t.rearrange("(m p) e -> p m e", p=128)),
             inc=d_sm, dma=True)
        R.op("sync", lambda s: s.dma_start(out=gix, in_=gidx.rearrange("p (e r) -> p e r", e=E)),
             inc=d_sm, dma=True)
        R.op("sync", lambda s: s.dma_start(out=stokc, in_=stok_d), inc=d_sm, dma=True)
        R.op("sync", lambda s: s.dma_start(out=swc, in_=swt_d), inc=d_sm, dma=True)
    t_b1 = t_gi = t_stk = t_sw = d_sm.tok()

    if stage == -6:
        return _early_exit([t_c2, t_sm := d_sm.tok()])
    chunks = ([("q", j) for j in range(KC)] + [("kv", 0)] + [("pj", j) for j in range(KC)])
    asm_state = dict(last=None, tr=None, ld_cons={}, lds={}, ev={})

    def emit_asm_load(ci):
        kind, j = chunks[ci]
        srcw = qw if kind == "q" else (kvw if kind == "kv" else pjw)
        asm_state["lds"][ci] = R.op("gpsimd", lambda g, ci=ci, srcw=srcw, j=j: g.dma_start(
            out=stp["wstage"][:, ci % 2, :], in_=srcw[128 * j:128 * (j + 1), :]),
            after=[t_lib] + ([asm_state["ld_cons"][ci - 2]] if ci - 2 in asm_state["ld_cons"] else []),
            inc=d_cast[ci % 2], dma=True)

    def emit_asm_chunk(ci, extra):
        kind, j = chunks[ci]
        t_ld = asm_state["lds"][ci]
        for k in range(KC):
            g4 = (2 * ci + k // 4)
            slot = 3072 + (g4 % 2) * 512 + (k % 4) * 128
            asm_state["tr"] = R.op("tensor", lambda t, ci=ci, k=k, slot=slot: t.matmul(
                psf(slot, 128), stp["wstage"][:, ci % 2, 128 * k:128 * (k + 1)], idn_b,
                start=True, stop=True),
                after=[t_ld, t_idb] + extra + ([asm_state["tr"]] if asm_state["tr"] else []) +
                      ([asm_state["ev"][g4 - 2]] if (k % 4 == 0 and g4 - 2 in asm_state["ev"]) else []),
                inc=st_)
            if k % 4 == 3:
                if kind == "q":
                    dst = wqkvT[:, 0, k - 3:k + 1, 128 * j:128 * (j + 1)]
                elif kind == "kv":
                    dst = wqkvT[:, 0, k - 3:k + 1, D:D + 128]
                else:
                    dst = projT[:, 0, k - 3:k + 1, 128 * j:128 * (j + 1)]
                asm_state["last"] = R.op("vector", lambda v, dst=dst, g4=g4: v.tensor_copy(
                    dst, psf(3072 + (g4 % 2) * 512, 512)),
                    after=[asm_state["tr"]] + ([asm_state["last"]] if asm_state["last"] else []), inc=sv)
                asm_state["ev"][g4] = asm_state["last"]
        if kind == "kv":
            for k in range(KC):
                asm_state["last"] = R.op("vector", lambda v, k=k: v.tensor_copy(
                    wqkvT[:, 0, k, QKW:QKW + HD], wqkvT[:, 0, k, D:D + HD]),
                    after=[asm_state["last"]], inc=sv)
                asm_state["last"] = R.op("vector", lambda v, k=k: v.tensor_copy(
                    wqkvT[:, 0, k, QKW + HD:QKW2], wqkvT[:, 0, k, D:D + HD]),
                    after=[asm_state["last"]], inc=sv)
        asm_state["ld_cons"][ci] = asm_state["tr"]

    emit_asm_load(0)
    emit_asm_load(1)

    t_sil = R.op("scalar", lambda s: s.activation(stp["sil"][0:2, :], stp["c2s"][0:2, :], AF.Silu),
                 after=[t_c2], inc=ss)
    t_slT = []
    for k in range(KC):
        tt = R.op("tensor", lambda t, k=k: t.transpose(
            psf(3584, 2), stp["sil"][0:2, 128 * k:128 * (k + 1)], idn_f[0:2, 0:2]),
            after=[t_sil, t_idf] + ([t_slT[-1]] if t_slT else []), inc=st_)
        t_slT.append(R.op("vector", lambda v, k=k: v.tensor_copy(stp["silT"][:, k, :], psf(3584, 2)),
                          after=[tt], inc=sv))
    t_silT = t_slT[-1]

    t_mrow = []
    asm_next = [0]
    WQE = ("sync", "scalar", "gpsimd")
    mm_of_chunk = {}
    cast_of_chunk = {}
    NCW = 3 * D // 512
    for half in range(2):
        last_mm = None
        for k in range(KC):
            for ch in range(NCW):
                ci = (half * KC + k) * NCW + ch
                tl = R.op(WQE[ci % 3], lambda g, k=k, half=half, ch=ch, ci=ci: g.dma_start(
                    out=wm32[:, ci % 4, :],
                    in_=wmod[128 * k:128 * (k + 1),
                             half * 3 * D + ch * 512:half * 3 * D + (ch + 1) * 512]),
                    after=([cast_of_chunk[ci - 4]] if ci - 4 in cast_of_chunk else []),
                    inc=d_gw[(ci % 3) * 2 + (ci // 3) % 2], dma=True)
                tc = R.op("vector", lambda v, ci=ci: v.tensor_copy(
                    stp["wmodb"][:, ci % 4, :], wm32[:, ci % 4, :]),
                    after=[tl] + ([mm_of_chunk[ci - 4]] if ci - 4 in mm_of_chunk else []), inc=sv)
                cast_of_chunk[ci] = tc
                last_mm = R.op("tensor", lambda t, k=k, ch=ch, ci=ci: t.matmul(
                    psf(512 * ch, 512)[0:2, :], stp["silT"][:, k, :],
                    stp["wmodb"][:, ci % 4, :],
                    start=(k == 0), stop=(k == KC - 1)),
                    after=[tc, t_silT] + (t_mrow if (half == 1 and k == 0 and ch == 0) else []), inc=st_)
                mm_of_chunk[ci] = last_mm
                if ci % 6 == 5 and asm_next[0] < len(chunks):
                    ai = asm_next[0]
                    emit_asm_chunk(ai, [t_silT])
                    if ai + 2 < len(chunks):
                        emit_asm_load(ai + 2)
                    asm_next[0] += 1
        tev = R.op("scalar", lambda s, half=half: s.copy(
            stp["mrow2"][0:2, :], psf(0, 3 * D)[0:2, :]),
            after=[last_mm] + t_mrow, inc=ss)
        tsp = R.op("gpsimd", lambda g, half=half: g.dma_start(
            out=stp["mrow1"][0:1, :, half * 3 * D:(half + 1) * 3 * D],
            in_=stp["mrow2"][0:2, :]), after=[tev, t_lib], inc=d_sep, dma=True)
        if half == 0:
            tsp = R.op("sync", lambda s: s.dma_start(
                out=stp["scols"][0:1, :, :].rearrange("p a d -> p (a d)"),
                in_=stp["mrow2"][0:2, 0:2 * D]),
                after=[tev, tsp], inc=d_sc, dma=True)
        t_mrow.append(tsp)
    while asm_next[0] < len(chunks):
        ai = asm_next[0]
        emit_asm_chunk(ai, [t_silT])
        if ai + 2 < len(chunks):
            emit_asm_load(ai + 2)
        asm_next[0] += 1
    t_asm = asm_state["last"]
    t_sep = t_mrow[-1]

    if stage == -4:
        return _early_exit([t_sep, t_mrow[-1]])

    def mr(b, i):
        return stp["mrow1"][0:1, b, i * D:(i + 1) * D]

    t_clamp = None
    for b in range(2):
        for gi_ in (2, 5):
            tsg = R.op("vector", lambda v, b=b, gi_=gi_: v.tensor_scalar(
                stp["crow"][0:1, 0:D], mr(b, gi_), 0.0, 2.0, ALU.is_ge, ALU.mult),
                after=[t_sep] + ([t_clamp] if t_clamp else []), inc=sv)
            tsa0 = R.op("scalar", lambda s, b=b, gi_=gi_: s.activation(
                stp["crow"][0:1, D:2 * D], mr(b, gi_), AF.Abs),
                after=[t_sep, tsg], inc=ss)
            tsa = R.op("vector", lambda v: v.tensor_scalar(
                stp["crow"][0:1, D:2 * D], stp["crow"][0:1, D:2 * D], 0.25, None, ALU.max),
                after=[tsa0], inc=sv)
            t_clamp = R.op("vector", lambda v, b=b, gi_=gi_: v.scalar_tensor_tensor(
                mr(b, gi_), stp["crow"][0:1, 0:D], -1.0, stp["crow"][0:1, D:2 * D],
                ALU.add, ALU.mult),
                after=[tsg, tsa], inc=sv)

    t_a2 = R.op("vector", lambda v: v.tensor_scalar(stp["rbf2"][0:1, :], mr(0, 4), 1.0, None, ALU.add),
                after=[t_sep], inc=sv)

    t_cols = None
    tt = None
    for b in range(2):
        for k in range(KC):
            tt = R.op("tensor", lambda t, b=b, k=k: t.transpose(
                psf(3840 + 8 * b + k, 1)[:, :], stp["scols"][0:1, 2 * b + 1, 128 * k:128 * (k + 1)],
                idn_f[0:1, 0:1]),
                after=[t_sep, t_asm] + ([tt] if tt else []), inc=st_)
            tt = R.op("tensor", lambda t, b=b, k=k: t.transpose(
                psf(3856 + 8 * b + k, 1)[:, :], stp["scols"][0:1, 2 * b, 128 * k:128 * (k + 1)],
                idn_f[0:1, 0:1]),
                after=[tt], inc=st_)
    t_qsc = R.op("vector", lambda v: v.tensor_scalar(
        qsc.rearrange("p a b -> p (a b)"), psf(3840, 16), 1.0, None, ALU.add),
        after=[tt], inc=sv)
    t_cols = R.op("vector", lambda v: v.tensor_copy(
        sac.rearrange("p a b -> p (a b)"), psf(3856, 16)),
        after=[tt, t_qsc], inc=sv)

    def rep(row_ap, dst_ap, after):
        tm = None
        for hf in range(D // 512):
            tm = R.op("tensor", lambda t, hf=hf: t.matmul(
                psf(3072 + 512 * hf, 512), ones_b[0:1, 0:128], row_ap[:, 512 * hf:512 * (hf + 1)],
                start=True, stop=True),
                after=list(after) + [t_on, t_cols, t_asm], inc=st_)
        return R.op("scalar", lambda s: s.copy(dst_ap, psf(3072, D)), after=[tm], inc=ss)

    t_A2 = rep(stp["rbf2"][0:1, :], A2r, [t_a2])
    t_B2 = rep(mr(0, 3), B2r, [t_sep, t_A2])
    t_gb = rep(mr(0, 5), gbr, [t_clamp, t_B2])

    if stage == -3:
        return _early_exit([t_cols, t_clamp, t_a2])
    if stage == -2:
        return _early_exit([t_A2, t_B2, t_gb, t_cols, t_clamp])
    BWOF = (0, 1536)
    t_bw_last = None
    for b in range(2):
        for jj in range(QKW // 128):
            for k in range(KC):
                t_bw_last = R.op("tensor", lambda t, b=b, jj=jj, k=k: t.matmul(
                    psf(BWOF[b], QKW)[0:1, 128 * jj:128 * (jj + 1)], sac[:, b, k:k + 1],
                    wqkvT[:, 0, k, 128 * jj:128 * (jj + 1)],
                    start=(k == 0), stop=(k == KC - 1)),
                    after=[t_asm, t_cols] + ([t_bw_last] if t_bw_last else []), inc=st_)
    t_wq = []
    for k in range(KC):
        t_wq.append(R.op("vector", lambda v, k=k: v.tensor_scalar(
            wqkvT[:, 1, k, :], wqkvT[:, 0, k, :], qsc[:, 1, k:k + 1], None, ALU.mult),
            after=[t_asm, t_cols, t_bw_last], inc=sv))
        t_wq.append(R.op("vector", lambda v, k=k: v.tensor_scalar(
            wqkvT[:, 0, k, :], wqkvT[:, 0, k, :], qsc[:, 0, k:k + 1], None, ALU.mult),
            after=[t_wq[-1]], inc=sv))
    t_wqkv = t_wq[-1]
    t_Bw = []
    for b in range(2):
        te = R.op("scalar", lambda s, b=b: s.copy(Bw[0:1, b, 0:QKW], psf(BWOF[b], QKW)[0:1, :]),
                  after=[t_bw_last], inc=ss)
        te = R.op("vector", lambda v, b=b: v.tensor_copy(Bw[0:1, b, QKW:QKW + HD], Bw[0:1, b, D:D + HD]),
                  after=[te], inc=sv)
        te = R.op("vector", lambda v, b=b: v.tensor_copy(Bw[0:1, b, QKW + HD:QKW2], Bw[0:1, b, D:D + HD]),
                  after=[te], inc=sv)
        t_Bw.append(te)

    if stage == -1:
        return _early_exit([t_wqkv, t_Bw[1]])
    t_pj = []
    tge = None
    for b in (1, 0):
        tg = None
        for hf in range(D // 512):
            tg = R.op("tensor", lambda t, b=b, hf=hf: t.matmul(
                psf(3072 + 512 * hf, 512), ones_b[0:1, 0:128], mr(b, 2)[:, 512 * hf:512 * (hf + 1)],
                start=True, stop=True),
                after=[t_clamp, t_on, t_asm] + ([t_pj[-1]] if t_pj else [t_gb]), inc=st_)
        tge = R.op("scalar", lambda s: s.copy(stp["grep"], psf(3072, D)), after=[tg], inc=ss)
        for k in range(KC):
            t_pj.append(R.op("vector", lambda v, k=k, b=b: v.tensor_tensor(
                projT[:, b, k, :], projT[:, 0, k, :], stp["grep"], ALU.mult),
                after=[tge, t_asm] + ([t_pj[-1]] if t_pj else []), inc=sv))
    t_projd = t_pj[-1]
    t_R = [R.op("vector", lambda v, b=b: v.tensor_tensor(
        Rr[0:1, b, :], mr(b, 2), stp["pjbs"][0:1, :], ALU.mult),
        after=[t_clamp, t_pjb], inc=sv) for b in range(2)]

    t_setup = [t_wqkv, t_projd, t_Bw[1], t_R[1], t_A2, t_B2, t_gb, t_on,
               t_onc, t_cA, t_cB, t_b1, t_gi, t_stk, t_sw]

    d_xp = (d_x0, d_x1)
    d_op = (d_o0, d_o1)
    t_xload = {}
    for w in range(min(2, NW)):
        t_xload[w] = R.op("sync", lambda s, w=w: s.dma_start(
            out=att["xb"][:, w % 2, :, :],
            in_=x_in[W * w:W * (w + 1), :].rearrange("(t p) d -> p t d", p=128)),
            after=t_setup, inc=d_xp[w % 2], dma=True)

    def emit_ln_dve(xt, dst, ti, sel, aft, aft_sq, aft_aff):
        tb1a = R.op("vector", lambda v, xt=xt, ti=ti, sel=sel: v.bn_stats(
            mvt[:, sel, ti, 0:6], xt[:, 0:D // 2]), after=aft, inc=sv)
        tb1 = R.op("vector", lambda v, xt=xt, ti=ti, sel=sel: v.bn_stats(
            mvt[:, sel, ti, 6:12], xt[:, D // 2:D]), after=[tb1a], inc=sv)
        tb2 = R.op("vector", lambda v, ti=ti, sel=sel: v.bn_aggr(
            stt[:, sel, ti, :], mvt[:, sel, ti, 0:12]), after=[tb1], inc=sv)
        tv1 = R.op("vector", lambda v, ti=ti, sel=sel: v.tensor_scalar(
            mvt[:, sel, ti, 12:13], stt[:, sel, ti, 1:2], EPS, None, ALU.add),
            after=[tb2] + aft_sq, inc=sv)
        ts1 = R.op("scalar", lambda s, ti=ti, sel=sel: s.activation(
            mvt[:, sel, ti, 13:14], mvt[:, sel, ti, 12:13], AF.Sqrt), after=[tv1], inc=ss)
        tv2 = R.op("vector", lambda v, ti=ti, sel=sel: v.reciprocal(
            mvt[:, sel, ti, 12:13], mvt[:, sel, ti, 13:14]), after=[ts1], inc=sv)
        tv3 = R.op("vector", lambda v, ti=ti, sel=sel: v.tensor_scalar(
            mvt[:, sel, ti, 13:14], stt[:, sel, ti, 0:1], mvt[:, sel, ti, 12:13], -1.0,
            ALU.mult, ALU.mult), after=[tv2], inc=sv)
        ta = R.op("vector", lambda v, ti=ti, sel=sel, xt=xt, dst=dst: v.tensor_scalar(
            dst, xt, mvt[:, sel, ti, 12:13], mvt[:, sel, ti, 13:14], ALU.mult, ALU.add),
            after=[tv3] + aft_aff, inc=sv)
        return ta, ts1

    t_comb_prev = None
    t_ate_prev = None
    t_free_xmo = {0: [], 1: []}
    t_xmf = []
    t_h2_store = None
    t_win_last = None
    P = None
    h2_pend = None
    h2_sq, h2_rc, h2_sq2, h2_rc2 = {}, {}, {}, {}
    t_sqtok, t_rctok = {}, {}

    t_aff_n = []
    if NW > 0 and stage >= 1:
        for ti in range(TPW):
            xt0 = att["xb"][:, 0, ti, :]
            ta0, tq0 = emit_ln_dve(xt0, att["ln1"][:, ti, :], ti, 0,
                                   [t_xload[0]] + t_setup, [], t_setup)
            t_aff_n.append(ta0)
            t_sqtok[ti] = tq0

    for w in range(NW if stage >= 1 else 0):
        b = WB[w]
        ffn = w < WFN
        t_aff = t_aff_n
        t_lt = None
        ev_of_g = {}
        tt = None
        for ti in range(TPW):
            for k in range(KC):
                idx = ti * KC + k
                g4 = idx // 4
                slot = (g4 % 2) * 512 + (idx % 4) * 128
                tt = R.op("tensor", lambda t, ti=ti, k=k, slot=slot: t.matmul(
                    psf(slot, 128), att["ln1"][:, ti, 128 * k:128 * (k + 1)], idn_b,
                    start=True, stop=True),
                    after=[t_aff[ti]] +
                          (([P["expB"]] if P else t_setup) if idx == 0 else []) +
                          ([tt] if tt else []) +
                          ([ev_of_g[g4 - 2]] if (idx % 4 == 0 and g4 >= 2) else []), inc=st_)
                if idx % 4 == 3:
                    t_lt = R.op("scalar", lambda s, ti=ti, k=k, g4=g4: s.copy(
                        att["ln1T"][:, k - 3:k + 1, 128 * ti:128 * (ti + 1)],
                        psf((g4 % 2) * 512, 512)),
                        after=[tt], inc=ss)
                    ev_of_g[g4] = t_lt
        t_lnT = t_lt
        t_qkv_j = {}
        for j in range(JQK2):
            if P is None:
                jg = []
            elif j < 4:
                jg = [P["expB"]]
            elif j < 8:
                jg = [P["tze7"]]
            else:
                jg = [P["comb0"]]
            for k in range(KC):
                R.op("tensor", lambda t, j=j, k=k, b=b: t.matmul(
                    psf(j * W, W), wqkvT[:, b, k, 128 * j:128 * (j + 1)], att["ln1T"][:, k, :],
                    start=(k == 0), stop=False),
                    after=[t_lnT] + jg, inc=st_)
            t_qkv_j[j] = R.op("tensor", lambda t, j=j, b=b: t.matmul(
                psf(j * W, W), Bw[0:1, b, 128 * j:128 * (j + 1)], ones_b[0:1, 0:W],
                start=False, stop=True), after=[t_lnT], inc=st_)
        t_sqk = R.op("scalar", lambda s: s.activation(att["sqb"][0:HD, JQK - 1, :],
                                                      psf((JQK - 1) * W, W)[0:HD, :], AF.Square),
                     after=[t_qkv_j[JQK - 1], t_qkv_j[JQK]], inc=ss)
        t_ev1 = R.op("scalar", lambda s: s.copy(
            att["qkvT"][:, JQK - 1:JQK + 1, :].rearrange("p a b -> p (a b)"),
            psf((JQK - 1) * W, 2 * W)), after=[t_qkv_j[JQK - 1], t_qkv_j[JQK]], inc=ss)
        t_ev0 = R.op("scalar", lambda s: s.copy(
            att["qkvT"][:, 0:4, :].rearrange("p a b -> p (a b)"),
            psf(0, 4 * W)), after=[t_qkv_j[j] for j in range(4)], inc=ss)
        t_sqA = R.op("scalar", lambda s: s.activation(
            att["sqb"][:, 0:4, :].rearrange("p a b -> p (a b)"),
            att["qkvT"][:, 0:4, :].rearrange("p a b -> p (a b)"), AF.Square),
            after=[t_ev0], inc=ss)
        t_ev2 = R.op("scalar", lambda s: s.copy(
            att["qkvT"][:, 4:8, :].rearrange("p a b -> p (a b)"),
            psf(4 * W, 4 * W)), after=[t_qkv_j[j] for j in range(4, 8)], inc=ss)
        t_sqB = R.op("scalar", lambda s: s.activation(
            att["sqb"][:, 4:8, :].rearrange("p a b -> p (a b)"),
            att["qkvT"][:, 4:8, :].rearrange("p a b -> p (a b)"), AF.Square),
            after=[t_ev2], inc=ss)
        t_k2 = R.op("tensor", lambda t: t.matmul(psf(3072, W)[0:1, :], colA[0:HD, :],
                                                 att["sqb"][0:HD, JQK - 1, :], start=True, stop=True),
                    after=[t_sqk, t_cA] + ([P["comb1"]] if P else []), inc=st_)
        t_k2e = R.op("vector", lambda v: v.tensor_reduce(km2[0:1, 0:1], psf(3072, W)[0:1, :], AX.X, ALU.max),
                     after=[t_k2], inc=sv)

        H2 = H // 2
        t_sc = {}
        t_exp = {}
        t_zs = {}
        t_pv = {}
        tze = {}
        t_ate_l = {}
        ZSLOTS = (1024, 1536, 1280, 1792)

        def emit_scores(h):
            evg = t_ev0 if h < 8 else t_ev2
            gate = [t_ev1, evg] + ([t_exp[h - 2]] if h >= 2 else [])
            qj, i = h // 2, h % 2
            qp = i * HD
            kcj = JQK - 1 if i == 0 else JQK
            last = None
            for kc2 in range(TPW):
                last = R.op("tensor", lambda t, h=h, qj=qj, qp=qp, kc2=kc2, kcj=kcj: t.matmul(
                    psf(512 * (h % 2), TPW, W)[:, kc2, :],
                    att["qkvT"][qp:qp + HD, kcj, 128 * kc2:128 * (kc2 + 1)],
                    att["qkvT"][qp:qp + HD, qj, :],
                    start=True, stop=True),
                    after=gate, inc=st_)
            t_sc[h] = last

        def emit_exp(h):
            gate = [t_sc[h], t_uB]
            if h >= 2:
                gate += [t_pv[h - 2], t_zs[(h - 2) // 2]]
            t_exp[h] = R.op("scalar", lambda s, h=h: s.activation(
                att["E"][:, h % 2, :, :].rearrange("p a b -> p (a b)"),
                psf(512 * (h % 2), TPW * W), AF.Exp, scale=SCALE, bias=ubias),
                after=gate, inc=ss)

        def emit_pv(h):
            cj, i = h // 2, h % 2
            cp = i * HD
            gate = [t_exp[h], t_vones]
            last = None
            for kc2 in range(TPW):
                last = R.op("tensor", lambda t, h=h, cj=cj, cp=cp, kc2=kc2: t.matmul(
                    PS[cp:cp + HD, 2048 + cj * W:2048 + (cj + 1) * W],
                    vbuf[:, kc2, 0:HD],
                    att["E"][:, h % 2, kc2, :],
                    start=(kc2 == 0), stop=(kc2 == TPW - 1)),
                    after=gate, inc=st_)
            t_pv[h] = last

        def emit_zrep(cj):
            gate = [t_exp[2 * cj + 1], t_A64, t_B64] + ([tze[cj - 2]] if cj >= 2 else [])
            last = None
            for i in range(2):
                lt = A64 if i == 0 else B64
                for kc2 in range(TPW):
                    last = R.op("tensor", lambda t, cj=cj, i=i, kc2=kc2, lt=lt: t.matmul(
                        psf(ZSLOTS[cj % 4], W), lt, att["E"][:, i, kc2, :],
                        start=(i == 0 and kc2 == 0), stop=(i == 1 and kc2 == TPW - 1)),
                        after=gate, inc=st_)
            t_zs[cj] = last

        def emit_tze(cj):
            tze[cj] = R.op("vector", lambda v, cj=cj: v.reciprocal_approx_fast(
                zrs32[:, cj % 4, :], psf(ZSLOTS[cj % 4], W)), after=[t_zs[cj]], inc=sv)

        def emit_ate(cj):
            gate = [tze[cj], t_pv[2 * cj], t_pv[2 * cj + 1]]
            if cj % 2 == 0:
                gate.append(t_pv[2 * cj + 3])
            t_ate_l[cj] = R.op("vector", lambda v, cj=cj: v.tensor_tensor(
                att["attnT"][:, cj, :], PS[:, 2048 + cj * W:2048 + (cj + 1) * W],
                zrs32[:, cj % 4, :], ALU.mult),
                after=gate, inc=sv)

        emit_scores(0)
        t_q2 = None
        for k in range(KC):
            t_q2 = R.op("tensor", lambda t, k=k: t.matmul(
                psf(3328, W)[0:1, :], ones_c, att["sqb"][:, k, :],
                start=(k == 0), stop=(k == KC - 1)),
                after=[(t_sqA if k < 4 else t_sqB), t_k2e, t_onc], inc=st_)
        t_q2e = R.op("vector", lambda v: v.tensor_reduce(km2[0:1, 1:2], psf(3328, W)[0:1, :], AX.X, ALU.max),
                     after=[t_q2], inc=sv)
        tq2 = R.op("vector", lambda v: v.tensor_tensor(km2[0:1, 1:2], km2[0:1, 1:2], km2[0:1, 0:1], ALU.mult),
                   after=[t_q2e, t_k2e], inc=sv)
        tq3 = R.op("scalar", lambda s: s.activation(km2[0:1, 3:4], km2[0:1, 1:2], AF.Sqrt),
                   after=[tq2], inc=ss)
        tq4 = R.op("vector", lambda v: v.tensor_scalar(km2[0:1, 3:4], km2[0:1, 3:4], -SCALE, None, ALU.mult),
                   after=[tq3], inc=sv)
        t_uB = R.op("gpsimd", lambda g: g.partition_broadcast(ubias, km2[0:1, 3:4]),
                    after=[tq4], inc=sg)
        t_vb = None
        for kc2 in range(TPW):
            tt = R.op("tensor", lambda t, kc2=kc2: t.matmul(
                psf(2816, HD), att["qkvT"][HD:2 * HD, JQK - 1, 128 * kc2:128 * (kc2 + 1)],
                idn_b[HD:2 * HD, HD:2 * HD], start=True, stop=True),
                after=[t_ev1] + ([t_vb] if t_vb else []), inc=st_)
            t_vb = R.op("scalar", lambda s, kc2=kc2: s.copy(vbuf[:, kc2, 0:HD], psf(2816, HD)),
                        after=[tt], inc=ss)
        t_vones = R.op("vector", lambda v: v.memset(vbuf[:, :, HD:HD + 1], 1.0),
                       after=[t_vb], inc=sv)
        def emit_h2_tile(wp, combp, ti, t_h2w):
            xo = att["xmo"][:, wp % 2, ti, :]
            tah, tqh = emit_ln_dve(xo, att["h2b"][:, ti, :], ti, 1,
                                   [combp[ti]] + ([t_h2_store] if t_h2_store else []),
                                   [h2_sq[ti]] if h2_sq else [], [])
            tm1 = R.op("vector", lambda v, ti=ti: v.tensor_tensor(
                att["h2b"][:, ti, :], att["h2b"][:, ti, :], A2r, ALU.mult), after=[tah], inc=sv)
            tm2 = R.op("vector", lambda v, ti=ti: v.tensor_tensor(
                att["h2b"][:, ti, :], att["h2b"][:, ti, :], B2r, ALU.add), after=[tm1], inc=sv)
            t_h2w.append(tm2)
            h2_sq2[ti] = tqh
        t_h2w_acc = []
        emit_exp(0)
        for h in range(1, 2 * H2):
            emit_scores(h)
            if h % 2 == 0 and h >= 2:
                emit_zrep(h // 2 - 1)
                emit_tze(h // 2 - 1)
            emit_exp(h)
            emit_pv(h - 1)
            if h % 2 == 0 and h >= 4:
                emit_ate(h // 2 - 2)
            if h2_pend is not None and h in (3, 6):
                wp, combp = h2_pend
                emit_h2_tile(wp, combp, 0 if h == 3 else 1, t_h2w_acc)
                if h == 6:
                    h2_sq, h2_sq2 = dict(h2_sq2), {}
                    t_h2_store = R.op("sync", lambda s, wp=wp: s.dma_start(
                        out=h2hb[W * wp:W * (wp + 1), :].rearrange("(t p) d -> p t d", p=128),
                        in_=att["h2b"][:, :, :]), after=t_h2w_acc, inc=d_h2, dma=True)
                    t_free_xmo[wp % 2] = t_free_xmo[wp % 2] + t_h2w_acc
                    h2_pend = None
                    t_h2w_acc = []
            if h in (9, 12) and w + 1 < NW:
                if h == 9:
                    t_aff_n = []
                    sq_n = {}
                ti = 0 if h == 9 else 1
                xt1 = att["xb"][:, (w + 1) % 2, ti, :]
                tan, tqn = emit_ln_dve(xt1, att["ln1"][:, ti, :], ti, 0,
                                       [t_xload[w + 1]],
                                       [t_sqtok[ti]] if t_sqtok else [], [t_lnT])
                t_aff_n.append(tan)
                sq_n[ti] = tqn
                if h == 12:
                    t_sqtok = {ti2: sq_n[ti2] for ti2 in range(TPW)}
        emit_pv(2 * H2 - 1)
        emit_zrep(H2 - 1)
        emit_tze(H2 - 1)
        emit_ate(H2 - 2)
        emit_ate(H2 - 1)
        t_pm = {}
        for k in range(KC):
            for q in range(4):
                ti, hf = q // 2, q % 2
                gate = [t_ate_l[k]]
                if k == 0:
                    gate.append(t_ate_l[2 * q + 1])
                else:
                    gate.append(t_pm[q])
                t_pm[q] = R.op("tensor", lambda t, k=k, q=q, ti=ti, hf=hf, b=b: t.matmul(
                    psf(2048 + 512 * q, 512),
                    att["attnT"][:, k, 128 * ti:128 * (ti + 1)],
                    projT[:, b, k, 512 * hf:512 * (hf + 1)],
                    start=(k == 0), stop=False),
                    after=gate, inc=st_)
        for q in range(4):
            hf = q % 2
            t_pm[q] = R.op("tensor", lambda t, q=q, hf=hf, b=b: t.matmul(
                psf(2048 + 512 * q, 512), ones_b[0:1, 0:128],
                Rr[0:1, b, 512 * hf:512 * (hf + 1)],
                start=False, stop=True), after=[t_pm[q], t_on], inc=st_)
        t_comb = []
        for ti in range(TPW):
            t_comb.append(R.op("vector", lambda v, ti=ti, w=w: v.tensor_tensor(
                att["xmo"][:, w % 2, ti, :], psf(2048 + ti * 1024, 1024),
                att["xb"][:, w % 2, ti, :], ALU.add),
                after=[t_pm[2 * ti], t_pm[2 * ti + 1]] + t_free_xmo[w % 2], inc=sv))
        t_ate = t_ate_l[H2 - 1]
        free_xmo = []
        if not ffn:
            tso = R.op("sync", lambda s, w=w: s.dma_start(
                out=out[W * w:W * (w + 1), :].rearrange("(t p) d -> p t d", p=128),
                in_=att["xmo"][:, w % 2, :, :]), after=t_comb, inc=d_op[w % 2], dma=True)
            free_xmo = [tso]
        else:
            for ti in range(TPW):
                t_xmf.append(R.op("sync", lambda s, ti=ti, w=w: s.dma_start(
                    out=xmhb[W * w + 128 * ti:W * w + 128 * (ti + 1), :],
                    in_=att["xmo"][:, w % 2, ti, :]), after=[t_comb[ti]], inc=d_xmp[w % 2], dma=True))
            h2_pend = (w, list(t_comb))
            free_xmo = [t_xmf[-1], t_xmf[-2]]
        t_free_xmo[w % 2] = free_xmo
        if w + 2 < NW:
            t_xload[w + 2] = R.op("sync", lambda s, w=w: s.dma_start(
                out=att["xb"][:, w % 2, :, :],
                in_=x_in[W * (w + 2):W * (w + 3), :].rearrange("(t p) d -> p t d", p=128)),
                after=[t_comb[-1]], inc=d_xp[w % 2], dma=True)
        P = dict(expA=t_exp[2 * H2 - 2], expB=t_exp[2 * H2 - 1], tze7=tze[H2 - 1],
                 comb0=t_comb[0], comb1=t_comb[1])
        t_comb_prev = t_comb[-1]
        t_ate_prev = t_ate
        t_win_last = t_comb[-1]

    att_done = ([t_win_last, t_ate_prev, t_h2_store, d_o0.tok(), d_o1.tok(),
                 d_h2.tok(), d_xmp[0].tok(), d_xmp[1].tok()] if stage >= 1 else
                [x for x in t_setup])
    aw_w1 = [t_qkv_j[JQK]] if stage >= 1 else att_done
    aw_w2 = ([t_qkv_j[JQK], t_pm[3]] if stage >= 1 else att_done)
    aw_gath = ([t_qkv_j[JQK], t_h2_store, d_h2.tok()] if stage >= 1 else att_done)
    if stage < 2:
        R.op("sync", lambda s: s.dma_start(out=out[0:128, :], in_=gbr),
             after=att_done + [t_gb], inc=d_o0, dma=True)
        R.op("sync", lambda s: s.nop(), after=[s_.tok() for s_ in Sem.REGISTRY if s_.n > 0])
        with nc.Block() as block:
            R.emit(block)
        nc.finalize()
        return nc, c

    octs = [(e, oc) for e in range(E) for oc in range(OC)]
    t_w1l, t_w2l = {}, {}
    t_w1_cons = {}
    t_w2_cons = {}
    t_gath = {}
    t_gath_cons = {}
    t_b2l = {}
    t_b2_cons = {}

    def load_oct(i):
        e, oc = octs[i]
        aft_w1 = aw_w1 if i < 2 else [t_w1_cons[i - 2]]
        aft_w2 = aw_w2 if i < 2 else [t_w2_cons[i - 2]]
        t_w1l[i] = R.op("gpsimd", lambda g, e=e, oc=oc: g.dma_start(
            out=moe["w1s"][:, oc % 2, :, :],
            in_=w1[e, :, 512 * oc:512 * (oc + 1)].rearrange("(k p) f -> p k f", p=128)),
            after=[t_lib] + aft_w1, inc=d_w1p[oc % 2], dma=True)
        t_w2l[i] = R.op("gpsimd", lambda g, e=e, oc=oc: g.dma_start(
            out=moe["w2s"][:, oc % 2, :, :],
            in_=w2[e, 512 * oc:512 * (oc + 1), :].rearrange("(m p) d -> p m d", p=128)),
            after=[t_lib] + aft_w2, inc=d_w2p[oc % 2], dma=True)

    def load_gath(e):
        aft = aw_gath if e < 1 else [t_gath_cons[e - 1]]
        t_gath[e] = R.op("gpsimd", lambda g, e=e: g.dma_gather(
            moe["gath"][:, :, :], h2hb, gix[:, e, :], REG, REG, D,
            elem_step=D, transpose=False),
            after=[t_lib, t_gi] + aft, inc=d_gtp[e % 2], dma=True)

    def load_b2(e):
        aft = att_done if e < 2 else [t_b2_cons[e - 2]]
        t_b2l[e] = R.op("gpsimd", lambda g, e=e: g.dma_start(
            out=moe["b2e"][0:1, e % 2, :], in_=b2[e:e + 1, :]),
            after=[t_lib] + aft, inc=d_b2p[e % 2], dma=True)

    NE_PRE = E if stage >= 3 else 1
    for i in range(min(2, NE_PRE * OC)):
        load_oct(i)
    load_gath(0)
    for e in range(min(2, NE_PRE)):
        load_b2(e)
    t_memz = R.op("vector", lambda v: v.memset(moe["out2"].rearrange("p a b -> p (a b)"), 0.0),
                  after=att_done, inc=sv)

    t_o2e = {}
    t_hb_cons = {}
    t_gtr_done = {}
    t_last_gel = None
    NE_RUN = E if stage >= 3 else 1
    for e in range(NE_RUN):
        t_gtr = None
        for nt in range(NT):
            for k in range(KC):
                bank = 6 + (k % 2)
                ttr = R.op("tensor", lambda t, nt=nt, k=k, bank=bank: t.matmul(
                    psf(bank * 512, 128), moe["gath"][:, nt, 128 * k:128 * (k + 1)], idn_b,
                    start=True, stop=True),
                    after=[t_gath[e]] + ([t_gtr] if t_gtr else []) +
                          ([t_win_last] if (e == 0 and nt == 0 and k == 0 and stage >= 1) else []) +
                          ([t_gtr_done[e - 2]] if (e - 2 in t_gtr_done and nt == 0 and k == 0) else []) +
                          ([t_last_gel] if (t_last_gel and nt == 0 and k == 0) else []), inc=st_)
                t_gtr = R.op("vector", lambda v, e=e, nt=nt, k=k, bank=bank: v.tensor_copy(
                    moe["gathT"][:, e % 2, k, 128 * nt:128 * (nt + 1)], psf(bank * 512, 128)),
                    after=[ttr], inc=sv)
        t_gtr_done[e] = t_gtr
        if e + 1 < NE_RUN:
            t_gath_cons[e] = t_gtr
            load_gath(e + 1)
        t_w2mm = None
        gel_of_oct = {}
        for oc in range(OC):
            i = e * OC + oc
            tw1, tw2 = t_w1l[i], t_w2l[i]
            t_gel = []
            for m in range(4):
                bank = 6 + (m % 2)
                mm = None
                gate = [tw1, t_gtr_done[e]]
                if len(t_gel) >= 2:
                    gate.append(t_gel[-2])
                elif oc >= 1 and len(t_gel) < 2:
                    gate.append(gel_of_oct[oc - 1][-2 + len(t_gel)])
                if (i % 2) in t_hb_cons and m == 0:
                    gate.append(t_hb_cons[i % 2])
                for k in range(KC):
                    mm = R.op("tensor", lambda t, e=e, oc=oc, m=m, k=k, bank=bank: t.matmul(
                        psf(bank * 512, CAP), moe["w1s"][:, oc % 2, k, 128 * m:128 * (m + 1)],
                        moe["gathT"][:, e % 2, k, 0:CAP],
                        start=(k == 0), stop=(k == KC - 1)),
                        after=gate, inc=st_)
                ffg = oc * 512 + m * 128
                t_gel.append(R.op("scalar", lambda s, e=e, oc=oc, m=m, bank=bank, ffg=ffg: s.activation(
                    moe["hb"][:, oc % 2, m, :], psf(bank * 512, CAP),
                    AF.Gelu_apprx_tanh, bias=b1c[:, ffg // 128, e:e + 1]),
                    after=[mm, t_b1], inc=ss))
            gel_of_oct[oc] = t_gel
            t_w1_cons[i] = t_gel[-1]
            for tix in range(NT):
                tsz = TSZ[tix]
                for m in range(4):
                    for hf in range(2):
                        t_w2mm = R.op("tensor", lambda t, oc=oc, m=m, hf=hf, tix=tix, tsz=tsz: t.matmul(
                            psf(tix * 1024, 2, 512)[0:tsz, hf, :],
                            moe["hb"][:, oc % 2, m, 128 * tix:128 * tix + tsz],
                            moe["w2s"][:, oc % 2, m, 512 * hf:512 * (hf + 1)],
                            start=(oc == 0 and m == 0), stop=False),
                            after=[tw2, t_gel[m]], inc=st_)
            t_w2_cons[i] = t_w2mm
            t_hb_cons[i % 2] = t_w2mm
            t_last_gel = t_gel[-1]
            if i + 2 < NE_RUN * OC:
                load_oct(i + 2)
        for tix in range(NT):
            tsz = TSZ[tix]
            for hf in range(2):
                t_w2mm = R.op("tensor", lambda t, e=e, tix=tix, hf=hf, tsz=tsz: t.matmul(
                    psf(tix * 1024, 2, 512)[0:tsz, hf, :], ones_b[0:1, 0:tsz],
                    moe["b2e"][0:1, e % 2, 512 * hf:512 * (hf + 1)],
                    start=False, stop=True),
                    after=[t_b2l[e], t_on], inc=st_)
        t_b2_cons[e] = t_w2mm
        evs = []
        for tix in range(NT):
            tsz = TSZ[tix]
            evs.append(R.op("scalar", lambda s, e=e, tix=tix, tsz=tsz: s.copy(
                moe["out2"][0:tsz, e * NT + tix, :],
                psf(tix * 1024, 1024)[0:tsz, :]),
                after=[t_w2mm, t_memz], inc=ss))
        t_o2e[e] = evs[-1]
        if e + 2 < NE_RUN:
            load_b2(e + 2)

    if stage < 4:
        R.op("sync", lambda s: s.dma_start(out=out[0:128, :], in_=gbr),
             after=[t_o2e[NE_RUN - 1], t_gb], inc=d_o0, dma=True)
        R.op("sync", lambda s: s.nop(), after=[s_.tok() for s_ in Sem.REGISTRY if s_.n > 0])
        with nc.Block() as block:
            R.emit(block)
        nc.finalize()
        return nc, c
    t_st_final = []
    t_mt_cons = {}
    prev_pass_last = None
    for pas in range(2):
        rcs = list(range(pas * (RC - RC // 2) if False else pas * (RC // 2), (pas + 1) * (RC // 2)))
        if RC == 1:
            rcs = [0] if pas == 0 else []
        t_acc = {}
        for sc in range(SC if rcs else 0):
            e_of = sc // NT
            gate = [t_stk, t_sw, t_iof, t_o2e[e_of]]
            if (sc % 2) in t_mt_cons:
                gate.append(t_mt_cons[sc % 2])
            if sc == 0 and pas == 0:
                gate.append(t_o2e[E - 1])
            if sc == 0 and prev_pass_last is not None:
                gate.append(prev_pass_last)
            tb = R.op("vector", lambda v, sc=sc: v.tensor_scalar(
                moe["Mt"][:, sc % 2, :], iota_f, stokc[:, sc:sc + 1], swc[:, sc:sc + 1],
                ALU.is_equal, ALU.mult), after=gate, inc=sv)
            last = None
            for r in rcs:
                for hf in range(2):
                    last = R.op("tensor", lambda t, sc=sc, r=r, hf=hf: t.matmul(
                        psf((r % max(1, RC // 2)) * 1024, 2, 512)[:, hf, :],
                        moe["Mt"][:, sc % 2, 128 * r:128 * (r + 1)],
                        moe["out2"][:, sc, 512 * hf:512 * (hf + 1)],
                        start=(sc == 0), stop=(sc == SC - 1)),
                        after=[tb], inc=st_)
                    t_acc[r] = last
            t_mt_cons[sc % 2] = last
        for r in rcs:
            txi = R.op("sync", lambda s, r=r: s.dma_start(
                out=moe["xmi"][:, r % 2, :], in_=xmhb[128 * r:128 * (r + 1), :]),
                after=[d_xmp[0].tok(), d_xmp[1].tok()] + ([t_st_final[-2][0]] if len(t_st_final) >= 2 else []),
                inc=d_xi[r % 2], dma=True)
            tv = R.op("vector", lambda v, r=r: v.tensor_tensor(
                moe["xo"][:, r % 2, :], psf((r % max(1, RC // 2)) * 1024, 1024), gbr, ALU.mult),
                after=[t_acc[r], t_gb] + ([t_st_final[-2][1]] if len(t_st_final) >= 2 else []),
                inc=sv)
            tv2 = R.op("vector", lambda v, r=r: v.tensor_tensor(
                moe["xo"][:, r % 2, :], moe["xo"][:, r % 2, :], moe["xmi"][:, r % 2, :], ALU.add),
                after=[tv, txi], inc=sv)
            tso = R.op("sync", lambda s, r=r: s.dma_start(
                out=out[128 * r:128 * (r + 1), :], in_=moe["xo"][:, r % 2, :]),
                after=[tv2], inc=d_op[r % 2], dma=True)
            t_st_final.append((tv2, tso))
        if rcs:
            prev_pass_last = t_acc[rcs[-1]]

    R.op("sync", lambda s: s.nop(), after=[s_.tok() for s_ in Sem.REGISTRY if s_.n > 0])

    with nc.Block() as block:
        R.emit(block)
    nc.finalize()
    return nc, c


_PROG_CACHE = {}


def _get_program(key, cfg):
    if key not in _PROG_CACHE:
        _PROG_CACHE[key] = build_program(cfg)
    return _PROG_CACHE[key]


def host_prep_core(cfg, top_idx, combine, row_base):
    c = _derived(cfg)
    E, REG, CAP, SC, S, NROW = c["E"], c["REG"], c["CAP"], c["SC"], c["S"], c["NROW"]
    gidx = np.zeros((16, E * (REG // 16)), np.int16)
    stok = np.full(S, -1.0, np.float32)
    sw = np.zeros(S, np.float32)
    for e in range(E):
        col = np.asarray(top_idx[:, e])
        locs = np.flatnonzero((col >= row_base) & (col < row_base + NROW))
        toks = (col[locs] - row_base).astype(np.int64)
        n = len(toks)
        if n > CAP:
            raise RuntimeError(f"slot capacity overflow e={e}: {n} > {CAP}")
        base = e * REG
        for i in range(n):
            j = base + i
            gidx[j % 16, j // 16] = np.int16(toks[i])
            stok[j] = np.float32(toks[i])
            sw[j] = np.float32(combine[int(col[locs[i]]), e])
    return (np.tile(gidx, (8, 1)),
            np.ascontiguousarray(stok.reshape(SC, 128).T),
            np.ascontiguousarray(sw.reshape(SC, 128).T))


def kernel(L0_attn_x, L0_attn_q_w, L0_attn_kv_w, L0_attn_proj_w, L0_attn_proj_b,
           L0_ffn_combine_weights, L0_ffn_top_idx, L0_ffn_W1, L0_ffn_b1,
           L0_ffn_W2, L0_ffn_b2, L0_c, L0_wmod):
    cfg = FULL_CFG
    c = _derived(cfg)
    ncores = 8
    L = 16384
    x = np.ascontiguousarray(np.asarray(L0_attn_x, dtype=np.float32)).reshape(-1, c["D"])
    top_idx = np.asarray(L0_ffn_top_idx)
    combine = np.asarray(L0_ffn_combine_weights, dtype=np.float32)
    nc, _ = _get_program("full", cfg)

    shared = dict(
        qw=np.ascontiguousarray(np.asarray(L0_attn_q_w, np.float32)),
        kvw=np.ascontiguousarray(np.asarray(L0_attn_kv_w, np.float32)),
        pjw=np.ascontiguousarray(np.asarray(L0_attn_proj_w, np.float32)),
        pjb=np.ascontiguousarray(np.asarray(L0_attn_proj_b, np.float32)).reshape(1, -1),
        c2=np.ascontiguousarray(np.asarray(L0_c, np.float32)),
        wmod=np.ascontiguousarray(np.asarray(L0_wmod, np.float32)),
        w1=np.ascontiguousarray(np.asarray(L0_ffn_W1, np.float32)),
        b1t=np.ascontiguousarray(np.asarray(L0_ffn_b1, np.float32).T),
        w2=np.ascontiguousarray(np.asarray(L0_ffn_W2, np.float32)),
        b2=np.ascontiguousarray(np.asarray(L0_ffn_b2, np.float32)),
    )
    in_maps, row_maps = [], []
    for core in range(ncores):
        rows = np.concatenate([
            np.arange(1024 * core, 1024 * core + 1024),
            np.arange(8192 + 1024 * core, 8192 + 1024 * core + 1024),
            np.arange(L + 2048 * core, L + 2048 * core + 2048)])
        row_maps.append(rows)
        gi, stk, sw = host_prep_core(cfg, top_idx, combine, 1024 * core)
        m = dict(shared)
        m["x_rows"] = np.ascontiguousarray(x[rows])
        m["gidx"] = gi
        m["stok"] = stk
        m["swt"] = sw
        in_maps.append(m)

    res = run_bass_kernel_spmd(nc, in_maps, core_ids=list(range(ncores)))
    outp = np.empty((2 * L, c["D"]), np.float32)
    for core in range(ncores):
        outp[row_maps[core]] = res.results[core]["out_rows"]
    return outp.reshape(2, L, c["D"])


# version 19
# speedup vs baseline: 1.1561x; 1.0036x over previous
import numpy as np
import ml_dtypes
import concourse.bass as bass
import concourse.bacc as bacc
import concourse.mybir as mybir
from concourse import library_config
from concourse.bass_utils import run_bass_kernel_spmd

F32 = mybir.dt.float32
BF16 = mybir.dt.bfloat16
I16 = mybir.dt.int16
U8 = mybir.dt.uint8
AF = mybir.ActivationFunctionType
ALU = mybir.AluOpType
AX = mybir.AxisListType

FULL_CFG = dict(NW=16, WFN=4, WB=[0] * 8 + [1] * 8, W=256, D=1024, H=16, HD=64,
                E=8, FF=4096, CAP=352, REG=384, EPS=1e-5)
SMALL_CFG = dict(NW=4, WFN=2, WB=[0, 0, 1, 1], W=256, D=1024, H=16, HD=64,
                 E=8, FF=1024, CAP=96, REG=128, EPS=1e-5)


def _derived(cfg):
    d = dict(cfg)
    d["KC"] = cfg["D"] // 128
    d["TPW"] = cfg["W"] // 128
    d["OC"] = cfg["FF"] // 512
    d["NT"] = cfg["REG"] // 128
    d["S"] = cfg["E"] * cfg["REG"]
    d["SC"] = d["S"] // 128
    d["NROW"] = cfg["WFN"] * cfg["W"]
    d["RC"] = d["NROW"] // 128
    d["NTOK"] = cfg["NW"] * cfg["W"]
    d["HG"] = cfg["H"] // 4
    sizes, left = [], cfg["CAP"]
    for _ in range(d["NT"]):
        sizes.append(min(128, left))
        left -= sizes[-1]
    d["TSZ"] = sizes
    return d


class Sem:
    REGISTRY = None

    def __init__(self, nc, name):
        self.h = nc.alloc_semaphore(name)
        self.n = 0
        self.name = name
        if Sem.REGISTRY is not None:
            Sem.REGISTRY.append(self)

    def tok(self):
        return (self, self.n)


class Rec:
    ENGS = ("sync", "gpsimd", "vector", "scalar", "tensor")

    def __init__(self, nc):
        self.nc = nc
        self.ops = {e: [] for e in self.ENGS}

    def op(self, eng, build, after=(), inc=None, dma=False):
        waits = [(s, v) for (s, v) in after if v > 0]
        if inc is not None:
            inc.n += 16 if dma else 1
        self.ops[eng].append((waits, build, inc, 16 if dma else 1))
        return inc.tok() if inc is not None else None

    def emit(self, block):
        for eng_name in self.ENGS:
            ops = self.ops[eng_name]

            def body(eng, ops=ops):
                seen = {}
                for waits, build, inc, k in ops:
                    for s, v in waits:
                        if seen.get(s.name, 0) < v:
                            eng.wait_ge(s.h, v)
                            seen[s.name] = v
                    ins = build(eng)
                    if inc is not None:
                        ins.then_inc(inc.h, k)
            getattr(block, eng_name)(body)


class Arena:

    def __init__(self, nc, name, dtype, nelem):
        self.t = nc.alloc_sbuf_tensor(name, [128, nelem], dtype)
        self.dtype = dtype
        self.nelem = nelem
        self.off = 0
        self.marks = []
        self.peak = 0

    def alloc(self, *shape):
        n = int(np.prod(shape))
        off = self.off
        self.off += (n + 15) & ~15
        self.peak = max(self.peak, self.off)
        assert self.off <= self.nelem, f"arena overflow {self.off} > {self.nelem}"
        ap = self.t[:, off:off + n]
        if len(shape) > 1:
            names = [chr(97 + i) for i in range(len(shape))]
            ap = ap.rearrange("p (" + " ".join(names) + ") -> p " + " ".join(names),
                              **{nm: s for nm, s in zip(names, shape)})
        return ap

    def mark(self):
        self.marks.append(self.off)

    def reset(self):
        self.off = self.marks.pop()


def build_program(cfg, stage=99):
    import os
    KDROP = set(os.environ.get('KDROP', '').split(','))
    c = _derived(cfg)
    NW, WFN, WB, W, D, H, HD, E, FF = (c["NW"], c["WFN"], c["WB"], c["W"], c["D"],
                                       c["H"], c["HD"], c["E"], c["FF"])
    KC, CAP, REG, NT, S, SC = c["KC"], c["CAP"], c["REG"], c["NT"], c["S"], c["SC"]
    NROW, RC, NTOK, TPW, OC, HG, TSZ, EPS = (c["NROW"], c["RC"], c["NTOK"], c["TPW"],
                                             c["OC"], c["HG"], c["TSZ"], c["EPS"])
    QKW = D + 2 * HD
    JQK = QKW // 128
    QKW2 = QKW + 128
    JQK2 = JQK + 1
    SCALE = float(HD) ** -0.5

    nc = bacc.Bacc("TRN2", debug=False)
    Sem.REGISTRY = []

    x_in = nc.dram_tensor("x_rows", [NTOK, D], F32, kind="ExternalInput").ap()
    qw = nc.dram_tensor("qw", [D, D], F32, kind="ExternalInput").ap()
    kvw = nc.dram_tensor("kvw", [2 * HD, D], F32, kind="ExternalInput").ap()
    pjw = nc.dram_tensor("pjw", [D, D], F32, kind="ExternalInput").ap()
    pjb = nc.dram_tensor("pjb", [1, D], F32, kind="ExternalInput").ap()
    c2 = nc.dram_tensor("c2", [2, D], F32, kind="ExternalInput").ap()
    wmod = nc.dram_tensor("wmod", [D, 6 * D], F32, kind="ExternalInput").ap()
    w1 = nc.dram_tensor("w1", [E, D, FF], F32, kind="ExternalInput").ap()
    b1t = nc.dram_tensor("b1t", [FF, E], F32, kind="ExternalInput").ap()
    w2 = nc.dram_tensor("w2", [E, FF, D], F32, kind="ExternalInput").ap()
    b2 = nc.dram_tensor("b2", [E, D], F32, kind="ExternalInput").ap()
    gidx = nc.dram_tensor("gidx", [128, E * (REG // 16)], I16, kind="ExternalInput").ap()
    stok_d = nc.dram_tensor("stok", [128, SC], F32, kind="ExternalInput").ap()
    swt_d = nc.dram_tensor("swt", [128, SC], F32, kind="ExternalInput").ap()
    out = nc.dram_tensor("out_rows", [NTOK, D], F32, kind="ExternalOutput").ap()
    h2hb = nc.dram_tensor("h2hb", [NROW, D], BF16, kind="Internal").ap()
    xmhb = nc.dram_tensor("xmhb", [NROW, D], F32, kind="Internal").ap()

    AB = Arena(nc, "ab", BF16, 71 * 1024)
    FA = Arena(nc, "af", F32, 14 * 1024)
    gix = nc.alloc_sbuf_tensor("gix", [128, E * (REG // 16)], I16).ap().rearrange("p (e r) -> p e r", e=E)
    wm32 = nc.alloc_sbuf_tensor("wm32", [128, 4, 512], F32).ap()

    iota_f = FA.alloc(D)
    iota_c = FA.alloc(1)
    idn_f = FA.alloc(128)
    gbr = FA.alloc(D)
    qsc = FA.alloc(2, KC)
    km2 = FA.alloc(4)
    stokc = FA.alloc(SC)
    swc = FA.alloc(SC)
    b1c = FA.alloc(FF // 128, E)
    mvt = FA.alloc(2, TPW, 16)
    stt = FA.alloc(2, TPW, 2)
    idn_b = AB.alloc(128)
    ones_b = AB.alloc(256)
    colA = AB.alloc(1)
    colB = AB.alloc(1)
    A64 = AB.alloc(128)
    B64 = AB.alloc(128)
    ones_c = AB.alloc(1)
    Bw = AB.alloc(2, QKW2)
    Rr = AB.alloc(2, D)
    A2r = AB.alloc(D)
    B2r = AB.alloc(D)
    sac = AB.alloc(2, KC)

    ovb = AB.off
    wqkvT = AB.alloc(2, KC, QKW2)
    projT = AB.alloc(2, KC, D)
    qb2 = AB.off
    AB.mark(); FA.mark()
    att = {}
    att["ln1T"] = AB.alloc(KC, W)
    att["qkvT"] = AB.alloc(JQK2, W)
    att["E"] = AB.alloc(2, TPW, W)
    att["sqb"] = AB.alloc(JQK, W)
    att["attnT"] = AB.alloc(KC, W)
    att["h2b"] = AB.alloc(TPW, D)
    att["ln1"] = AB.alloc(TPW, D)
    vbuf = AB.alloc(TPW, HD + 1)
    att["xb"] = FA.alloc(2, TPW, D)
    att["xmo"] = FA.alloc(2, TPW, D)
    ubias = FA.alloc(1)
    zrs32 = FA.alloc(4, 256)
    AB.reset(); FA.reset()
    AB.mark(); FA.mark()
    stp = {}
    stp["wstage"] = AB.alloc(2, D)
    stp["wmodb"] = AB.alloc(4, 512)
    stp["silT"] = AB.alloc(KC, 2)
    stp["mrow1"] = AB.alloc(2, 6 * D)
    stp["pjbs"] = AB.alloc(D)
    stp["grep"] = AB.alloc(D)
    stp["rbf2"] = AB.alloc(D)
    stp["sil"] = FA.alloc(D)
    stp["mrow2"] = FA.alloc(3 * D)
    stp["crow"] = FA.alloc(2 * D)
    stp["c2s"] = FA.alloc(D)
    stp["scols"] = FA.alloc(4, D)
    AB.reset(); FA.reset()
    AB.off = ovb
    moe = {}
    moe["gath"] = AB.alloc(NT, D)
    moe["gathT"] = AB.alloc(2, KC, REG)
    moe["w1s"] = AB.alloc(2, KC, 512)
    moe["w2s"] = AB.alloc(2, 4, D)
    moe["hb"] = AB.alloc(2, 4, CAP)
    moe["out2"] = AB.alloc(E * NT, D)
    moe["b2e"] = AB.alloc(2, D)
    moe["Mt"] = AB.alloc(2, D)
    moe["xo"] = FA.alloc(2, D)
    moe["xmi"] = FA.alloc(2, D)

    PS = nc.alloc_psum_tensor("PS", [128, 4096], F32)

    def psf(lo, *shape):
        n = int(np.prod(shape))
        ap = PS[:, lo:lo + n]
        if len(shape) > 1:
            names = [chr(97 + i) for i in range(len(shape))]
            ap = ap.rearrange("p (" + " ".join(names) + ") -> p " + " ".join(names),
                              **{nm: s for nm, s in zip(names, shape)})
        return ap

    R = Rec(nc)
    sv = Sem(nc, "sv")
    ss = Sem(nc, "ss")
    st_ = Sem(nc, "st")
    sg = Sem(nc, "sg")
    d_set = Sem(nc, "d_set")
    d_g = Sem(nc, "d_g")
    d_x0 = Sem(nc, "d_x0")
    d_x1 = Sem(nc, "d_x1")
    d_o0 = Sem(nc, "d_o0")
    d_o1 = Sem(nc, "d_o1")
    d_h2 = Sem(nc, "d_h2")
    d_w1 = Sem(nc, "d_w1")
    d_w2 = Sem(nc, "d_w2")
    d_gt = Sem(nc, "d_gt")
    d_b2 = Sem(nc, "d_b2")
    d_sep = Sem(nc, "d_sep")
    d_c2 = Sem(nc, "d_c2")
    d_xmp = [Sem(nc, "d_xm0"), Sem(nc, "d_xm1")]
    d_sc = Sem(nc, "d_sc")
    d_xi = [Sem(nc, "d_xia"), Sem(nc, "d_xib")]
    d_sm = Sem(nc, "d_sm")
    d_cast = [Sem(nc, "d_cast0"), Sem(nc, "d_cast1"), Sem(nc, "d_cast2")]
    d_stg = [Sem(nc, "d_stg0"), Sem(nc, "d_stg1"), Sem(nc, "d_stg2")]
    d_tt = [Sem(nc, "d_tt0"), Sem(nc, "d_tt1")]
    d_gw = [Sem(nc, "d_gw%d" % i) for i in range(6)]
    d_w1p = [Sem(nc, "d_w1a"), Sem(nc, "d_w1b")]
    d_w2p = [Sem(nc, "d_w2a"), Sem(nc, "d_w2b")]
    d_gtp = [Sem(nc, "d_gta"), Sem(nc, "d_gtb")]
    d_b2p = [Sem(nc, "d_b2a"), Sem(nc, "d_b2b")]


    def _early_exit(after_toks):
        R.op("sync", lambda s: s.dma_start(out=out[0:2, 0:D], in_=stp["c2s"][0:2, :]),
             after=list(after_toks), inc=d_o0, dma=True)
        if os.environ.get("KEXIT") == "min":
            R.op("sync", lambda s: s.nop(), after=[d_o0.tok()])
        else:
            R.op("sync", lambda s: s.nop(), after=[s_.tok() for s_ in Sem.REGISTRY if s_.n > 0])
        with nc.Block() as block:
            R.emit(block)
        nc.finalize()
        return nc, c

    if stage == -9:
        R.op("vector", lambda v: v.memset(gbr, 3.0), inc=sv)
        R.op("sync", lambda s: s.dma_start(out=out[0:128, 0:D], in_=gbr), after=[sv.tok()], inc=d_o0, dma=True)
        R.op("sync", lambda s: s.nop(), after=[d_o0.tok()])
        with nc.Block() as block:
            R.emit(block)
        nc.finalize()
        return nc, c
    t_iof = R.op("gpsimd", lambda g: g.iota(iota_f, pattern=[[1, D]], channel_multiplier=0,
                                            allow_small_or_imprecise_dtypes=True), inc=sg)
    t_ioc = R.op("gpsimd", lambda g: g.iota(iota_c, pattern=[[0, 1]], channel_multiplier=1,
                                            allow_small_or_imprecise_dtypes=True), after=[t_iof], inc=sg)
    R.op("gpsimd", lambda g: g.load_library(library_config.mlp), after=[t_ioc], inc=None)
    t_lib = t_ioc
    t_idb = R.op("vector", lambda v: v.tensor_scalar(idn_b, iota_f[:, 0:128], iota_c, None, ALU.is_equal),
                 after=[t_iof, t_ioc], inc=sv)
    t_idf = R.op("vector", lambda v: v.tensor_copy(idn_f, idn_b), after=[t_idb], inc=sv)
    t_on = R.op("vector", lambda v: v.memset(ones_b, 1.0), inc=sv)
    t_onc = R.op("vector", lambda v: v.memset(ones_c, 1.0), inc=sv)
    t_cA = R.op("vector", lambda v: v.tensor_scalar(colA, iota_c, 64.0, None, ALU.is_lt),
                after=[t_ioc], inc=sv)
    t_cB = R.op("vector", lambda v: v.tensor_scalar(colB, iota_c, 64.0, None, ALU.is_ge),
                after=[t_ioc], inc=sv)
    t_A64 = R.op("vector", lambda v: v.tensor_scalar(A64, iota_f[:, 0:128], 64.0, None, ALU.is_lt),
                 after=[t_iof], inc=sv)
    t_B64 = R.op("vector", lambda v: v.tensor_scalar(B64, iota_f[:, 0:128], 64.0, None, ALU.is_ge),
                 after=[t_iof], inc=sv)

    if stage == -8:
        R.op("sync", lambda s: s.dma_start(out=out[0:128, 0:128], in_=idn_f), after=[t_idf], inc=d_o0, dma=True)
        R.op("sync", lambda s: s.nop(), after=[d_o0.tok()])
        with nc.Block() as block:
            R.emit(block)
        nc.finalize()
        return nc, c
    t_c2 = R.op("sync", lambda s: s.dma_start(out=stp["c2s"][0:2, :], in_=c2), inc=d_c2, dma=True)
    if stage == -7:
        R.op("sync", lambda s: s.dma_start(out=out[0:2, 0:D], in_=stp["c2s"][0:2, :]),
             after=[t_c2, t_idf, t_cB], inc=d_o0, dma=True)
        R.op("sync", lambda s: s.nop(), after=[d_o0.tok()])
        with nc.Block() as block:
            R.emit(block)
        nc.finalize()
        return nc, c
    if "pjb" in KDROP:
        t_pjb = t_lib
    else:
        t_pjb = R.op("gpsimd", lambda g: g.dma_start(out=stp["pjbs"][0:1, :], in_=pjb),
                     after=[t_lib], inc=d_g, dma=True)
    if "smloads" not in KDROP:
        R.op("sync", lambda s: s.dma_start(out=b1c, in_=b1t.rearrange("(m p) e -> p m e", p=128)),
             inc=d_sm, dma=True)
        R.op("sync", lambda s: s.dma_start(out=gix, in_=gidx.rearrange("p (e r) -> p e r", e=E)),
             inc=d_sm, dma=True)
        R.op("sync", lambda s: s.dma_start(out=stokc, in_=stok_d), inc=d_sm, dma=True)
        R.op("sync", lambda s: s.dma_start(out=swc, in_=swt_d), inc=d_sm, dma=True)
    t_b1 = t_gi = t_stk = t_sw = d_sm.tok()

    if stage == -6:
        return _early_exit([t_c2, t_sm := d_sm.tok()])
    chunks = ([("q", j) for j in range(KC)] + [("kv", 0)] + [("pj", j) for j in range(KC)])
    asm_state = dict(last=None, tr=None, ld_cons={}, lds={}, ev={})

    def emit_asm_load(ci):
        kind, j = chunks[ci]
        srcw = qw if kind == "q" else (kvw if kind == "kv" else pjw)
        asm_state["lds"][ci] = R.op("gpsimd", lambda g, ci=ci, srcw=srcw, j=j: g.dma_start(
            out=stp["wstage"][:, ci % 2, :], in_=srcw[128 * j:128 * (j + 1), :]),
            after=[t_lib] + ([asm_state["ld_cons"][ci - 2]] if ci - 2 in asm_state["ld_cons"] else []),
            inc=d_cast[ci % 2], dma=True)

    def emit_asm_chunk(ci, extra):
        kind, j = chunks[ci]
        t_ld = asm_state["lds"][ci]
        for k in range(KC):
            g4 = (2 * ci + k // 4)
            slot = 3072 + (g4 % 2) * 512 + (k % 4) * 128
            asm_state["tr"] = R.op("tensor", lambda t, ci=ci, k=k, slot=slot: t.matmul(
                psf(slot, 128), stp["wstage"][:, ci % 2, 128 * k:128 * (k + 1)], idn_b,
                start=True, stop=True),
                after=[t_ld, t_idb] + extra + ([asm_state["tr"]] if asm_state["tr"] else []) +
                      ([asm_state["ev"][g4 - 2]] if (k % 4 == 0 and g4 - 2 in asm_state["ev"]) else []),
                inc=st_)
            if k % 4 == 3:
                if kind == "q":
                    dst = wqkvT[:, 0, k - 3:k + 1, 128 * j:128 * (j + 1)]
                elif kind == "kv":
                    dst = wqkvT[:, 0, k - 3:k + 1, D:D + 128]
                else:
                    dst = projT[:, 0, k - 3:k + 1, 128 * j:128 * (j + 1)]
                asm_state["last"] = R.op("vector", lambda v, dst=dst, g4=g4: v.tensor_copy(
                    dst, psf(3072 + (g4 % 2) * 512, 512)),
                    after=[asm_state["tr"]] + ([asm_state["last"]] if asm_state["last"] else []), inc=sv)
                asm_state["ev"][g4] = asm_state["last"]
        if kind == "kv":
            for k in range(KC):
                asm_state["last"] = R.op("vector", lambda v, k=k: v.tensor_copy(
                    wqkvT[:, 0, k, QKW:QKW + HD], wqkvT[:, 0, k, D:D + HD]),
                    after=[asm_state["last"]], inc=sv)
                asm_state["last"] = R.op("vector", lambda v, k=k: v.tensor_copy(
                    wqkvT[:, 0, k, QKW + HD:QKW2], wqkvT[:, 0, k, D:D + HD]),
                    after=[asm_state["last"]], inc=sv)
        asm_state["ld_cons"][ci] = asm_state["tr"]

    emit_asm_load(0)
    emit_asm_load(1)

    t_sil = R.op("scalar", lambda s: s.activation(stp["sil"][0:2, :], stp["c2s"][0:2, :], AF.Silu),
                 after=[t_c2], inc=ss)
    t_slT = []
    for k in range(KC):
        tt = R.op("tensor", lambda t, k=k: t.transpose(
            psf(3584, 2), stp["sil"][0:2, 128 * k:128 * (k + 1)], idn_f[0:2, 0:2]),
            after=[t_sil, t_idf] + ([t_slT[-1]] if t_slT else []), inc=st_)
        t_slT.append(R.op("vector", lambda v, k=k: v.tensor_copy(stp["silT"][:, k, :], psf(3584, 2)),
                          after=[tt], inc=sv))
    t_silT = t_slT[-1]

    t_mrow = []
    asm_next = [0]
    WQE = ("sync", "scalar", "gpsimd")
    mm_of_chunk = {}
    cast_of_chunk = {}
    NCW = 3 * D // 512
    for half in range(2):
        last_mm = None
        for k in range(KC):
            for ch in range(NCW):
                ci = (half * KC + k) * NCW + ch
                tl = R.op(WQE[ci % 3], lambda g, k=k, half=half, ch=ch, ci=ci: g.dma_start(
                    out=wm32[:, ci % 4, :],
                    in_=wmod[128 * k:128 * (k + 1),
                             half * 3 * D + ch * 512:half * 3 * D + (ch + 1) * 512]),
                    after=([cast_of_chunk[ci - 4]] if ci - 4 in cast_of_chunk else []),
                    inc=d_gw[(ci % 3) * 2 + (ci // 3) % 2], dma=True)
                tc = R.op("vector", lambda v, ci=ci: v.tensor_copy(
                    stp["wmodb"][:, ci % 4, :], wm32[:, ci % 4, :]),
                    after=[tl] + ([mm_of_chunk[ci - 4]] if ci - 4 in mm_of_chunk else []), inc=sv)
                cast_of_chunk[ci] = tc
                last_mm = R.op("tensor", lambda t, k=k, ch=ch, ci=ci: t.matmul(
                    psf(512 * ch, 512)[0:2, :], stp["silT"][:, k, :],
                    stp["wmodb"][:, ci % 4, :],
                    start=(k == 0), stop=(k == KC - 1)),
                    after=[tc, t_silT] + (t_mrow if (half == 1 and k == 0 and ch == 0) else []), inc=st_)
                mm_of_chunk[ci] = last_mm
                if ci % 6 == 5 and asm_next[0] < len(chunks):
                    ai = asm_next[0]
                    emit_asm_chunk(ai, [t_silT])
                    if ai + 2 < len(chunks):
                        emit_asm_load(ai + 2)
                    asm_next[0] += 1
        tev = R.op("scalar", lambda s, half=half: s.copy(
            stp["mrow2"][0:2, :], psf(0, 3 * D)[0:2, :]),
            after=[last_mm] + t_mrow, inc=ss)
        tsp = R.op("gpsimd", lambda g, half=half: g.dma_start(
            out=stp["mrow1"][0:1, :, half * 3 * D:(half + 1) * 3 * D],
            in_=stp["mrow2"][0:2, :]), after=[tev, t_lib], inc=d_sep, dma=True)
        if half == 0:
            tsp = R.op("sync", lambda s: s.dma_start(
                out=stp["scols"][0:1, :, :].rearrange("p a d -> p (a d)"),
                in_=stp["mrow2"][0:2, 0:2 * D]),
                after=[tev, tsp], inc=d_sc, dma=True)
        t_mrow.append(tsp)
    while asm_next[0] < len(chunks):
        ai = asm_next[0]
        emit_asm_chunk(ai, [t_silT])
        if ai + 2 < len(chunks):
            emit_asm_load(ai + 2)
        asm_next[0] += 1
    t_asm = asm_state["last"]
    t_sep = t_mrow[-1]

    if stage == -4:
        return _early_exit([t_sep, t_mrow[-1]])

    def mr(b, i):
        return stp["mrow1"][0:1, b, i * D:(i + 1) * D]

    t_clamp = None
    for b in range(2):
        for gi_ in (2, 5):
            tsg = R.op("vector", lambda v, b=b, gi_=gi_: v.tensor_scalar(
                stp["crow"][0:1, 0:D], mr(b, gi_), 0.0, 2.0, ALU.is_ge, ALU.mult),
                after=[t_sep] + ([t_clamp] if t_clamp else []), inc=sv)
            tsa0 = R.op("scalar", lambda s, b=b, gi_=gi_: s.activation(
                stp["crow"][0:1, D:2 * D], mr(b, gi_), AF.Abs),
                after=[t_sep, tsg], inc=ss)
            tsa = R.op("vector", lambda v: v.tensor_scalar(
                stp["crow"][0:1, D:2 * D], stp["crow"][0:1, D:2 * D], 0.25, None, ALU.max),
                after=[tsa0], inc=sv)
            t_clamp = R.op("vector", lambda v, b=b, gi_=gi_: v.scalar_tensor_tensor(
                mr(b, gi_), stp["crow"][0:1, 0:D], -1.0, stp["crow"][0:1, D:2 * D],
                ALU.add, ALU.mult),
                after=[tsg, tsa], inc=sv)

    t_a2 = R.op("vector", lambda v: v.tensor_scalar(stp["rbf2"][0:1, :], mr(0, 4), 1.0, None, ALU.add),
                after=[t_sep], inc=sv)

    t_cols = None
    tt = None
    for b in range(2):
        for k in range(KC):
            tt = R.op("tensor", lambda t, b=b, k=k: t.transpose(
                psf(3840 + 8 * b + k, 1)[:, :], stp["scols"][0:1, 2 * b + 1, 128 * k:128 * (k + 1)],
                idn_f[0:1, 0:1]),
                after=[t_sep, t_asm] + ([tt] if tt else []), inc=st_)
            tt = R.op("tensor", lambda t, b=b, k=k: t.transpose(
                psf(3856 + 8 * b + k, 1)[:, :], stp["scols"][0:1, 2 * b, 128 * k:128 * (k + 1)],
                idn_f[0:1, 0:1]),
                after=[tt], inc=st_)
    t_qsc = R.op("vector", lambda v: v.tensor_scalar(
        qsc.rearrange("p a b -> p (a b)"), psf(3840, 16), 1.0, None, ALU.add),
        after=[tt], inc=sv)
    t_cols = R.op("vector", lambda v: v.tensor_copy(
        sac.rearrange("p a b -> p (a b)"), psf(3856, 16)),
        after=[tt, t_qsc], inc=sv)

    def rep(row_ap, dst_ap, after):
        tm = None
        for hf in range(D // 512):
            tm = R.op("tensor", lambda t, hf=hf: t.matmul(
                psf(3072 + 512 * hf, 512), ones_b[0:1, 0:128], row_ap[:, 512 * hf:512 * (hf + 1)],
                start=True, stop=True),
                after=list(after) + [t_on, t_cols, t_asm], inc=st_)
        return R.op("scalar", lambda s: s.copy(dst_ap, psf(3072, D)), after=[tm], inc=ss)

    t_A2 = rep(stp["rbf2"][0:1, :], A2r, [t_a2])
    t_B2 = rep(mr(0, 3), B2r, [t_sep, t_A2])
    t_gb = rep(mr(0, 5), gbr, [t_clamp, t_B2])

    if stage == -3:
        return _early_exit([t_cols, t_clamp, t_a2])
    if stage == -2:
        return _early_exit([t_A2, t_B2, t_gb, t_cols, t_clamp])
    BWOF = (0, 1536)
    t_bw_last = None
    for b in range(2):
        for jj in range(QKW // 128):
            for k in range(KC):
                t_bw_last = R.op("tensor", lambda t, b=b, jj=jj, k=k: t.matmul(
                    psf(BWOF[b], QKW)[0:1, 128 * jj:128 * (jj + 1)], sac[:, b, k:k + 1],
                    wqkvT[:, 0, k, 128 * jj:128 * (jj + 1)],
                    start=(k == 0), stop=(k == KC - 1)),
                    after=[t_asm, t_cols] + ([t_bw_last] if t_bw_last else []), inc=st_)
    t_wq = []
    for k in range(KC):
        t_wq.append(R.op("vector", lambda v, k=k: v.tensor_scalar(
            wqkvT[:, 1, k, :], wqkvT[:, 0, k, :], qsc[:, 1, k:k + 1], None, ALU.mult),
            after=[t_asm, t_cols, t_bw_last], inc=sv))
        t_wq.append(R.op("vector", lambda v, k=k: v.tensor_scalar(
            wqkvT[:, 0, k, :], wqkvT[:, 0, k, :], qsc[:, 0, k:k + 1], None, ALU.mult),
            after=[t_wq[-1]], inc=sv))
    t_wqkv = t_wq[-1]
    t_Bw = []
    for b in range(2):
        te = R.op("scalar", lambda s, b=b: s.copy(Bw[0:1, b, 0:QKW], psf(BWOF[b], QKW)[0:1, :]),
                  after=[t_bw_last], inc=ss)
        te = R.op("vector", lambda v, b=b: v.tensor_copy(Bw[0:1, b, QKW:QKW + HD], Bw[0:1, b, D:D + HD]),
                  after=[te], inc=sv)
        te = R.op("vector", lambda v, b=b: v.tensor_copy(Bw[0:1, b, QKW + HD:QKW2], Bw[0:1, b, D:D + HD]),
                  after=[te], inc=sv)
        t_Bw.append(te)

    if stage == -1:
        return _early_exit([t_wqkv, t_Bw[1]])
    t_pj = []
    tge = None
    for b in (1, 0):
        tg = None
        for hf in range(D // 512):
            tg = R.op("tensor", lambda t, b=b, hf=hf: t.matmul(
                psf(3072 + 512 * hf, 512), ones_b[0:1, 0:128], mr(b, 2)[:, 512 * hf:512 * (hf + 1)],
                start=True, stop=True),
                after=[t_clamp, t_on, t_asm] + ([t_pj[-1]] if t_pj else [t_gb]), inc=st_)
        tge = R.op("scalar", lambda s: s.copy(stp["grep"], psf(3072, D)), after=[tg], inc=ss)
        for k in range(KC):
            t_pj.append(R.op("vector", lambda v, k=k, b=b: v.tensor_tensor(
                projT[:, b, k, :], projT[:, 0, k, :], stp["grep"], ALU.mult),
                after=[tge, t_asm] + ([t_pj[-1]] if t_pj else []), inc=sv))
    t_projd = t_pj[-1]
    t_R = [R.op("vector", lambda v, b=b: v.tensor_tensor(
        Rr[0:1, b, :], mr(b, 2), stp["pjbs"][0:1, :], ALU.mult),
        after=[t_clamp, t_pjb], inc=sv) for b in range(2)]

    t_setup = [t_wqkv, t_projd, t_Bw[1], t_R[1], t_A2, t_B2, t_gb, t_on,
               t_onc, t_cA, t_cB, t_b1, t_gi, t_stk, t_sw]

    d_xp = (d_x0, d_x1)
    d_op = (d_o0, d_o1)
    t_xload = {}
    for w in range(min(2, NW)):
        t_xload[w] = R.op("sync", lambda s, w=w: s.dma_start(
            out=att["xb"][:, w % 2, :, :],
            in_=x_in[W * w:W * (w + 1), :].rearrange("(t p) d -> p t d", p=128)),
            after=t_setup, inc=d_xp[w % 2], dma=True)

    def emit_ln_dve(xt, dst, ti, sel, aft, aft_sq, aft_aff):
        tb1a = R.op("vector", lambda v, xt=xt, ti=ti, sel=sel: v.bn_stats(
            mvt[:, sel, ti, 0:6], xt[:, 0:D // 2]), after=aft, inc=sv)
        tb1 = R.op("vector", lambda v, xt=xt, ti=ti, sel=sel: v.bn_stats(
            mvt[:, sel, ti, 6:12], xt[:, D // 2:D]), after=[tb1a], inc=sv)
        tb2 = R.op("vector", lambda v, ti=ti, sel=sel: v.bn_aggr(
            stt[:, sel, ti, :], mvt[:, sel, ti, 0:12]), after=[tb1], inc=sv)
        tv1 = R.op("vector", lambda v, ti=ti, sel=sel: v.tensor_scalar(
            mvt[:, sel, ti, 12:13], stt[:, sel, ti, 1:2], EPS, None, ALU.add),
            after=[tb2] + aft_sq, inc=sv)
        ts1 = R.op("scalar", lambda s, ti=ti, sel=sel: s.activation(
            mvt[:, sel, ti, 13:14], mvt[:, sel, ti, 12:13], AF.Sqrt), after=[tv1], inc=ss)
        tv2 = R.op("vector", lambda v, ti=ti, sel=sel: v.reciprocal(
            mvt[:, sel, ti, 12:13], mvt[:, sel, ti, 13:14]), after=[ts1], inc=sv)
        tv3 = R.op("vector", lambda v, ti=ti, sel=sel: v.tensor_scalar(
            mvt[:, sel, ti, 13:14], stt[:, sel, ti, 0:1], mvt[:, sel, ti, 12:13], -1.0,
            ALU.mult, ALU.mult), after=[tv2], inc=sv)
        ta = R.op("vector", lambda v, ti=ti, sel=sel, xt=xt, dst=dst: v.tensor_scalar(
            dst, xt, mvt[:, sel, ti, 12:13], mvt[:, sel, ti, 13:14], ALU.mult, ALU.add),
            after=[tv3] + aft_aff, inc=sv)
        return ta, ts1

    t_comb_prev = None
    t_ate_prev = None
    t_free_xmo = {0: [], 1: []}
    t_xmf = []
    t_h2_store = None
    t_win_last = None
    P = None
    h2_pend = None
    h2_sq, h2_rc, h2_sq2, h2_rc2 = {}, {}, {}, {}
    t_sqtok, t_rctok = {}, {}

    t_aff_n = []
    if NW > 0 and stage >= 1:
        for ti in range(TPW):
            xt0 = att["xb"][:, 0, ti, :]
            ta0, tq0 = emit_ln_dve(xt0, att["ln1"][:, ti, :], ti, 0,
                                   [t_xload[0]] + t_setup, [], t_setup)
            t_aff_n.append(ta0)
            t_sqtok[ti] = tq0

    for w in range(NW if stage >= 1 else 0):
        b = WB[w]
        ffn = w < WFN
        t_aff = t_aff_n
        t_lt = None
        ev_of_g = {}
        tt = None
        for ti in range(TPW):
            for k in range(KC):
                idx = ti * KC + k
                g4 = idx // 4
                slot = (g4 % 2) * 512 + (idx % 4) * 128
                tt = R.op("tensor", lambda t, ti=ti, k=k, slot=slot: t.matmul(
                    psf(slot, 128), att["ln1"][:, ti, 128 * k:128 * (k + 1)], idn_b,
                    start=True, stop=True),
                    after=[t_aff[ti]] +
                          (([P["expB"]] if P else t_setup) if idx == 0 else []) +
                          ([tt] if tt else []) +
                          ([ev_of_g[g4 - 2]] if (idx % 4 == 0 and g4 >= 2) else []), inc=st_)
                if idx % 4 == 3:
                    t_lt = R.op("scalar", lambda s, ti=ti, k=k, g4=g4: s.copy(
                        att["ln1T"][:, k - 3:k + 1, 128 * ti:128 * (ti + 1)],
                        psf((g4 % 2) * 512, 512)),
                        after=[tt], inc=ss)
                    ev_of_g[g4] = t_lt
        t_lnT = t_lt
        t_qkv_j = {}
        for j in range(JQK2):
            if P is None:
                jg = []
            elif j < 4:
                jg = [P["expB"]]
            elif j < 8:
                jg = [P["tze7"]]
            else:
                jg = [P["comb0"]]
            for k in range(KC):
                R.op("tensor", lambda t, j=j, k=k, b=b: t.matmul(
                    psf(j * W, W), wqkvT[:, b, k, 128 * j:128 * (j + 1)], att["ln1T"][:, k, :],
                    start=(k == 0), stop=False),
                    after=[t_lnT] + jg, inc=st_)
            t_qkv_j[j] = R.op("tensor", lambda t, j=j, b=b: t.matmul(
                psf(j * W, W), Bw[0:1, b, 128 * j:128 * (j + 1)], ones_b[0:1, 0:W],
                start=False, stop=True), after=[t_lnT], inc=st_)
        t_sqk = R.op("scalar", lambda s: s.activation(att["sqb"][0:HD, JQK - 1, :],
                                                      psf((JQK - 1) * W, W)[0:HD, :], AF.Square),
                     after=[t_qkv_j[JQK - 1], t_qkv_j[JQK]], inc=ss)
        t_ev1 = R.op("scalar", lambda s: s.copy(
            att["qkvT"][:, JQK - 1:JQK + 1, :].rearrange("p a b -> p (a b)"),
            psf((JQK - 1) * W, 2 * W)), after=[t_qkv_j[JQK - 1], t_qkv_j[JQK]], inc=ss)
        t_ev0 = R.op("scalar", lambda s: s.copy(
            att["qkvT"][:, 0:4, :].rearrange("p a b -> p (a b)"),
            psf(0, 4 * W)), after=[t_qkv_j[j] for j in range(4)], inc=ss)
        t_sqA = R.op("scalar", lambda s: s.activation(
            att["sqb"][:, 0:4, :].rearrange("p a b -> p (a b)"),
            att["qkvT"][:, 0:4, :].rearrange("p a b -> p (a b)"), AF.Square),
            after=[t_ev0], inc=ss)
        t_ev2 = R.op("scalar", lambda s: s.copy(
            att["qkvT"][:, 4:8, :].rearrange("p a b -> p (a b)"),
            psf(4 * W, 4 * W)), after=[t_qkv_j[j] for j in range(4, 8)], inc=ss)
        t_sqB = R.op("scalar", lambda s: s.activation(
            att["sqb"][:, 4:8, :].rearrange("p a b -> p (a b)"),
            att["qkvT"][:, 4:8, :].rearrange("p a b -> p (a b)"), AF.Square),
            after=[t_ev2], inc=ss)
        t_k2 = R.op("tensor", lambda t: t.matmul(psf(3072, W)[0:1, :], colA[0:HD, :],
                                                 att["sqb"][0:HD, JQK - 1, :], start=True, stop=True),
                    after=[t_sqk, t_cA] + ([P["comb1"]] if P else []), inc=st_)
        t_k2e = R.op("vector", lambda v: v.tensor_reduce(km2[0:1, 0:1], psf(3072, W)[0:1, :], AX.X, ALU.max),
                     after=[t_k2], inc=sv)

        H2 = H // 2
        t_sc = {}
        t_exp = {}
        t_zs = {}
        t_pv = {}
        tze = {}
        t_ate_l = {}
        ZSLOTS = (1024, 1536, 1280, 1792)

        def emit_scores(h):
            evg = t_ev0 if h < 8 else t_ev2
            gate = [t_ev1, evg] + ([t_exp[h - 2]] if h >= 2 else [])
            qj, i = h // 2, h % 2
            qp = i * HD
            kcj = JQK - 1 if i == 0 else JQK
            last = None
            for kc2 in range(TPW):
                last = R.op("tensor", lambda t, h=h, qj=qj, qp=qp, kc2=kc2, kcj=kcj: t.matmul(
                    psf(512 * (h % 2), TPW, W)[:, kc2, :],
                    att["qkvT"][qp:qp + HD, kcj, 128 * kc2:128 * (kc2 + 1)],
                    att["qkvT"][qp:qp + HD, qj, :],
                    start=True, stop=True),
                    after=gate, inc=st_)
            t_sc[h] = last

        def emit_exp(h):
            gate = [t_sc[h], t_uB]
            if h >= 2:
                gate += [t_pv[h - 2], t_zs[(h - 2) // 2]]
            t_exp[h] = R.op("scalar", lambda s, h=h: s.activation(
                att["E"][:, h % 2, :, :].rearrange("p a b -> p (a b)"),
                psf(512 * (h % 2), TPW * W), AF.Exp, scale=SCALE, bias=ubias),
                after=gate, inc=ss)

        def emit_pv(h):
            cj, i = h // 2, h % 2
            cp = i * HD
            gate = [t_exp[h], t_vones]
            last = None
            for kc2 in range(TPW):
                last = R.op("tensor", lambda t, h=h, cj=cj, cp=cp, kc2=kc2: t.matmul(
                    PS[cp:cp + HD, 2048 + cj * W:2048 + (cj + 1) * W],
                    vbuf[:, kc2, 0:HD],
                    att["E"][:, h % 2, kc2, :],
                    start=(kc2 == 0), stop=(kc2 == TPW - 1)),
                    after=gate, inc=st_)
            t_pv[h] = last

        def emit_zrep(cj):
            gate = [t_exp[2 * cj + 1], t_A64, t_B64] + ([tze[cj - 2]] if cj >= 2 else [])
            last = None
            for i in range(2):
                lt = A64 if i == 0 else B64
                for kc2 in range(TPW):
                    last = R.op("tensor", lambda t, cj=cj, i=i, kc2=kc2, lt=lt: t.matmul(
                        psf(ZSLOTS[cj % 4], W), lt, att["E"][:, i, kc2, :],
                        start=(i == 0 and kc2 == 0), stop=(i == 1 and kc2 == TPW - 1)),
                        after=gate, inc=st_)
            t_zs[cj] = last

        def emit_tze(cj):
            tze[cj] = R.op("vector", lambda v, cj=cj: v.reciprocal_approx_fast(
                zrs32[:, cj % 4, :], psf(ZSLOTS[cj % 4], W)), after=[t_zs[cj]], inc=sv)

        def emit_ate(cj):
            gate = [tze[cj], t_pv[2 * cj], t_pv[2 * cj + 1]]
            if cj % 2 == 0:
                gate.append(t_pv[2 * cj + 3])
            t_ate_l[cj] = R.op("vector", lambda v, cj=cj: v.tensor_tensor(
                att["attnT"][:, cj, :], PS[:, 2048 + cj * W:2048 + (cj + 1) * W],
                zrs32[:, cj % 4, :], ALU.mult),
                after=gate, inc=sv)

        emit_scores(0)
        t_q2 = None
        for k in range(KC):
            t_q2 = R.op("tensor", lambda t, k=k: t.matmul(
                psf(3328, W)[0:1, :], ones_c, att["sqb"][:, k, :],
                start=(k == 0), stop=(k == KC - 1)),
                after=[(t_sqA if k < 4 else t_sqB), t_k2e, t_onc], inc=st_)
        t_q2e = R.op("vector", lambda v: v.tensor_reduce(km2[0:1, 1:2], psf(3328, W)[0:1, :], AX.X, ALU.max),
                     after=[t_q2], inc=sv)
        tq2 = R.op("vector", lambda v: v.tensor_tensor(km2[0:1, 1:2], km2[0:1, 1:2], km2[0:1, 0:1], ALU.mult),
                   after=[t_q2e, t_k2e], inc=sv)
        tq3 = R.op("scalar", lambda s: s.activation(km2[0:1, 3:4], km2[0:1, 1:2], AF.Sqrt),
                   after=[tq2], inc=ss)
        tq4 = R.op("vector", lambda v: v.tensor_scalar(km2[0:1, 3:4], km2[0:1, 3:4], -SCALE, None, ALU.mult),
                   after=[tq3], inc=sv)
        t_uB = R.op("gpsimd", lambda g: g.partition_broadcast(ubias, km2[0:1, 3:4]),
                    after=[tq4], inc=sg)
        t_vb = None
        for kc2 in range(TPW):
            tt = R.op("tensor", lambda t, kc2=kc2: t.matmul(
                psf(2816, HD), att["qkvT"][HD:2 * HD, JQK - 1, 128 * kc2:128 * (kc2 + 1)],
                idn_b[HD:2 * HD, HD:2 * HD], start=True, stop=True),
                after=[t_ev1] + ([t_vb] if t_vb else []), inc=st_)
            t_vb = R.op("scalar", lambda s, kc2=kc2: s.copy(vbuf[:, kc2, 0:HD], psf(2816, HD)),
                        after=[tt], inc=ss)
        t_vones = R.op("vector", lambda v: v.memset(vbuf[:, :, HD:HD + 1], 1.0),
                       after=[t_vb], inc=sv)
        def emit_h2_tile(wp, combp, ti, t_h2w):
            xo = att["xmo"][:, wp % 2, ti, :]
            tah, tqh = emit_ln_dve(xo, att["h2b"][:, ti, :], ti, 1,
                                   [combp[ti]] + ([t_h2_store] if t_h2_store else []),
                                   [h2_sq[ti]] if h2_sq else [], [])
            tm1 = R.op("vector", lambda v, ti=ti: v.tensor_tensor(
                att["h2b"][:, ti, :], att["h2b"][:, ti, :], A2r, ALU.mult), after=[tah], inc=sv)
            tm2 = R.op("vector", lambda v, ti=ti: v.tensor_tensor(
                att["h2b"][:, ti, :], att["h2b"][:, ti, :], B2r, ALU.add), after=[tm1], inc=sv)
            t_h2w.append(tm2)
            h2_sq2[ti] = tqh
        t_h2w_acc = []
        emit_exp(0)
        for h in range(1, 2 * H2):
            emit_scores(h)
            if h % 2 == 0 and h >= 2:
                emit_zrep(h // 2 - 1)
                emit_tze(h // 2 - 1)
            emit_exp(h)
            emit_pv(h - 1)
            if h % 2 == 0 and h >= 4:
                emit_ate(h // 2 - 2)
            if h2_pend is not None and h in (3, 6):
                wp, combp = h2_pend
                emit_h2_tile(wp, combp, 0 if h == 3 else 1, t_h2w_acc)
                if h == 6:
                    h2_sq, h2_sq2 = dict(h2_sq2), {}
                    t_h2_store = R.op("sync", lambda s, wp=wp: s.dma_start(
                        out=h2hb[W * wp:W * (wp + 1), :].rearrange("(t p) d -> p t d", p=128),
                        in_=att["h2b"][:, :, :]), after=t_h2w_acc, inc=d_h2, dma=True)
                    t_free_xmo[wp % 2] = t_free_xmo[wp % 2] + t_h2w_acc
                    h2_pend = None
                    t_h2w_acc = []
            if h in (9, 12) and w + 1 < NW:
                if h == 9:
                    t_aff_n = []
                    sq_n = {}
                ti = 0 if h == 9 else 1
                xt1 = att["xb"][:, (w + 1) % 2, ti, :]
                tan, tqn = emit_ln_dve(xt1, att["ln1"][:, ti, :], ti, 0,
                                       [t_xload[w + 1]],
                                       [t_sqtok[ti]] if t_sqtok else [], [t_lnT])
                t_aff_n.append(tan)
                sq_n[ti] = tqn
                if h == 12:
                    t_sqtok = {ti2: sq_n[ti2] for ti2 in range(TPW)}
        emit_pv(2 * H2 - 1)
        emit_zrep(H2 - 1)
        emit_tze(H2 - 1)
        emit_ate(H2 - 2)
        emit_ate(H2 - 1)
        t_pm = {}
        for k in range(KC):
            for q in range(4):
                ti, hf = q // 2, q % 2
                gate = [t_ate_l[k]]
                if k == 0:
                    gate.append(t_ate_l[2 * q + 1])
                else:
                    gate.append(t_pm[q])
                t_pm[q] = R.op("tensor", lambda t, k=k, q=q, ti=ti, hf=hf, b=b: t.matmul(
                    psf(2048 + 512 * q, 512),
                    att["attnT"][:, k, 128 * ti:128 * (ti + 1)],
                    projT[:, b, k, 512 * hf:512 * (hf + 1)],
                    start=(k == 0), stop=False),
                    after=gate, inc=st_)
        for q in range(4):
            hf = q % 2
            t_pm[q] = R.op("tensor", lambda t, q=q, hf=hf, b=b: t.matmul(
                psf(2048 + 512 * q, 512), ones_b[0:1, 0:128],
                Rr[0:1, b, 512 * hf:512 * (hf + 1)],
                start=False, stop=True), after=[t_pm[q], t_on], inc=st_)
        t_comb = []
        for ti in range(TPW):
            t_comb.append(R.op("vector", lambda v, ti=ti, w=w: v.tensor_tensor(
                att["xmo"][:, w % 2, ti, :], psf(2048 + ti * 1024, 1024),
                att["xb"][:, w % 2, ti, :], ALU.add),
                after=[t_pm[2 * ti], t_pm[2 * ti + 1]] + t_free_xmo[w % 2], inc=sv))
        t_ate = t_ate_l[H2 - 1]
        free_xmo = []
        if not ffn:
            tso = R.op("sync", lambda s, w=w: s.dma_start(
                out=out[W * w:W * (w + 1), :].rearrange("(t p) d -> p t d", p=128),
                in_=att["xmo"][:, w % 2, :, :]), after=t_comb, inc=d_op[w % 2], dma=True)
            free_xmo = [tso]
        else:
            for ti in range(TPW):
                t_xmf.append(R.op("sync", lambda s, ti=ti, w=w: s.dma_start(
                    out=xmhb[W * w + 128 * ti:W * w + 128 * (ti + 1), :],
                    in_=att["xmo"][:, w % 2, ti, :]), after=[t_comb[ti]], inc=d_xmp[w % 2], dma=True))
            h2_pend = (w, list(t_comb))
            free_xmo = [t_xmf[-1], t_xmf[-2]]
        t_free_xmo[w % 2] = free_xmo
        if w + 2 < NW:
            t_xload[w + 2] = R.op("sync", lambda s, w=w: s.dma_start(
                out=att["xb"][:, w % 2, :, :],
                in_=x_in[W * (w + 2):W * (w + 3), :].rearrange("(t p) d -> p t d", p=128)),
                after=[t_comb[-1]], inc=d_xp[w % 2], dma=True)
        P = dict(expA=t_exp[2 * H2 - 2], expB=t_exp[2 * H2 - 1], tze7=tze[H2 - 1],
                 comb0=t_comb[0], comb1=t_comb[1])
        t_comb_prev = t_comb[-1]
        t_ate_prev = t_ate
        t_win_last = t_comb[-1]

    att_done = ([t_win_last, t_ate_prev, t_h2_store, d_o0.tok(), d_o1.tok(),
                 d_h2.tok(), d_xmp[0].tok(), d_xmp[1].tok()] if stage >= 1 else
                [x for x in t_setup])
    aw_w1 = [t_qkv_j[JQK]] if stage >= 1 else att_done
    aw_w2 = ([t_qkv_j[JQK], t_pm[3]] if stage >= 1 else att_done)
    aw_gath = ([t_qkv_j[JQK], t_h2_store, d_h2.tok()] if stage >= 1 else att_done)
    if stage < 2:
        R.op("sync", lambda s: s.dma_start(out=out[0:128, :], in_=gbr),
             after=att_done + [t_gb], inc=d_o0, dma=True)
        R.op("sync", lambda s: s.nop(), after=[s_.tok() for s_ in Sem.REGISTRY if s_.n > 0])
        with nc.Block() as block:
            R.emit(block)
        nc.finalize()
        return nc, c

    octs = [(e, oc) for e in range(E) for oc in range(OC)]
    t_w1l, t_w2l = {}, {}
    t_w1_cons = {}
    t_w2_cons = {}
    t_gath = {}
    t_gath_cons = {}
    t_b2l = {}
    t_b2_cons = {}

    def load_oct(i):
        e, oc = octs[i]
        aft_w1 = aw_w1 if i < 2 else [t_w1_cons[i - 2]]
        aft_w2 = aw_w2 if i < 2 else [t_w2_cons[i - 2]]
        t_w1l[i] = R.op("gpsimd", lambda g, e=e, oc=oc: g.dma_start(
            out=moe["w1s"][:, oc % 2, :, :],
            in_=w1[e, :, 512 * oc:512 * (oc + 1)].rearrange("(k p) f -> p k f", p=128)),
            after=[t_lib] + aft_w1, inc=d_w1p[oc % 2], dma=True)
        t_w2l[i] = R.op("gpsimd", lambda g, e=e, oc=oc: g.dma_start(
            out=moe["w2s"][:, oc % 2, :, :],
            in_=w2[e, 512 * oc:512 * (oc + 1), :].rearrange("(m p) d -> p m d", p=128)),
            after=[t_lib] + aft_w2, inc=d_w2p[oc % 2], dma=True)

    def load_gath(e):
        aft = aw_gath if e < 1 else [t_gath_cons[e - 1]]
        t_gath[e] = R.op("gpsimd", lambda g, e=e: g.dma_gather(
            moe["gath"][:, :, :], h2hb, gix[:, e, :], REG, REG, D,
            elem_step=D, transpose=False),
            after=[t_lib, t_gi] + aft, inc=d_gtp[e % 2], dma=True)

    def load_b2(e):
        aft = att_done if e < 2 else [t_b2_cons[e - 2]]
        t_b2l[e] = R.op("gpsimd", lambda g, e=e: g.dma_start(
            out=moe["b2e"][0:1, e % 2, :], in_=b2[e:e + 1, :]),
            after=[t_lib] + aft, inc=d_b2p[e % 2], dma=True)

    NE_PRE = E if stage >= 3 else 1
    for i in range(min(2, NE_PRE * OC)):
        load_oct(i)
    load_gath(0)
    for e in range(min(2, NE_PRE)):
        load_b2(e)
    t_memz = R.op("vector", lambda v: v.memset(moe["out2"].rearrange("p a b -> p (a b)"), 0.0),
                  after=att_done, inc=sv)

    t_o2e = {}
    t_hb_cons = {}
    t_gtr_done = {}
    t_last_gel = None
    NE_RUN = E if stage >= 3 else 1
    for e in range(NE_RUN):
        t_gtr = None
        gev = {}
        ttr = None
        for nt in range(NT):
            for k in range(KC):
                gi4 = nt * 2 + k // 4
                slot = 3072 + (gi4 % 2) * 512 + (k % 4) * 128
                ttr = R.op("tensor", lambda t, nt=nt, k=k, slot=slot: t.matmul(
                    psf(slot, 128), moe["gath"][:, nt, 128 * k:128 * (k + 1)], idn_b,
                    start=True, stop=True),
                    after=[t_gath[e]] + ([ttr] if ttr else []) +
                          ([gev[gi4 - 2]] if (k % 4 == 0 and gi4 - 2 in gev) else []) +
                          ([t_win_last] if (e == 0 and nt == 0 and k == 0 and stage >= 1) else []) +
                          ([t_gtr_done[e - 2]] if (e - 2 in t_gtr_done and nt == 0 and k == 0) else []) +
                          ([t_last_gel] if (t_last_gel and nt == 0 and k == 0) else []), inc=st_)
                if k % 4 == 3:
                    t_gtr = R.op("vector", lambda v, e=e, nt=nt, k=k, gi4=gi4: v.tensor_copy(
                        moe["gathT"][:, e % 2, k - 3:k + 1, 128 * nt:128 * (nt + 1)],
                        psf(3072 + (gi4 % 2) * 512, 512)),
                        after=[ttr], inc=sv)
                    gev[gi4] = t_gtr
        t_gtr_done[e] = t_gtr
        if e + 1 < NE_RUN:
            t_gath_cons[e] = t_gtr
            load_gath(e + 1)
        t_w2mm = None
        gel_of_oct = {}
        for oc in range(OC):
            i = e * OC + oc
            tw1, tw2 = t_w1l[i], t_w2l[i]
            t_gel = []
            for m in range(4):
                bank = 6 + (m % 2)
                mm = None
                gate = [tw1, t_gtr_done[e]]
                if len(t_gel) >= 2:
                    gate.append(t_gel[-2])
                elif oc >= 1 and len(t_gel) < 2:
                    gate.append(gel_of_oct[oc - 1][-2 + len(t_gel)])
                if (i % 2) in t_hb_cons and m == 0:
                    gate.append(t_hb_cons[i % 2])
                for k in range(KC):
                    mm = R.op("tensor", lambda t, e=e, oc=oc, m=m, k=k, bank=bank: t.matmul(
                        psf(bank * 512, CAP), moe["w1s"][:, oc % 2, k, 128 * m:128 * (m + 1)],
                        moe["gathT"][:, e % 2, k, 0:CAP],
                        start=(k == 0), stop=(k == KC - 1)),
                        after=gate, inc=st_)
                ffg = oc * 512 + m * 128
                t_gel.append(R.op("scalar", lambda s, e=e, oc=oc, m=m, bank=bank, ffg=ffg: s.activation(
                    moe["hb"][:, oc % 2, m, :], psf(bank * 512, CAP),
                    AF.Gelu_apprx_tanh, bias=b1c[:, ffg // 128, e:e + 1]),
                    after=[mm, t_b1], inc=ss))
            gel_of_oct[oc] = t_gel
            t_w1_cons[i] = t_gel[-1]
            for tix in range(NT):
                tsz = TSZ[tix]
                for m in range(4):
                    for hf in range(2):
                        t_w2mm = R.op("tensor", lambda t, oc=oc, m=m, hf=hf, tix=tix, tsz=tsz: t.matmul(
                            psf(tix * 1024, 2, 512)[0:tsz, hf, :],
                            moe["hb"][:, oc % 2, m, 128 * tix:128 * tix + tsz],
                            moe["w2s"][:, oc % 2, m, 512 * hf:512 * (hf + 1)],
                            start=(oc == 0 and m == 0), stop=False),
                            after=[tw2, t_gel[m]], inc=st_)
            t_w2_cons[i] = t_w2mm
            t_hb_cons[i % 2] = t_w2mm
            t_last_gel = t_gel[-1]
            if i + 2 < NE_RUN * OC:
                load_oct(i + 2)
        for tix in range(NT):
            tsz = TSZ[tix]
            for hf in range(2):
                t_w2mm = R.op("tensor", lambda t, e=e, tix=tix, hf=hf, tsz=tsz: t.matmul(
                    psf(tix * 1024, 2, 512)[0:tsz, hf, :], ones_b[0:1, 0:tsz],
                    moe["b2e"][0:1, e % 2, 512 * hf:512 * (hf + 1)],
                    start=False, stop=True),
                    after=[t_b2l[e], t_on], inc=st_)
        t_b2_cons[e] = t_w2mm
        evs = []
        for tix in range(NT):
            tsz = TSZ[tix]
            evs.append(R.op("scalar", lambda s, e=e, tix=tix, tsz=tsz: s.copy(
                moe["out2"][0:tsz, e * NT + tix, :],
                psf(tix * 1024, 1024)[0:tsz, :]),
                after=[t_w2mm, t_memz], inc=ss))
        t_o2e[e] = evs[-1]
        if e + 2 < NE_RUN:
            load_b2(e + 2)

    if stage < 4:
        R.op("sync", lambda s: s.dma_start(out=out[0:128, :], in_=gbr),
             after=[t_o2e[NE_RUN - 1], t_gb], inc=d_o0, dma=True)
        R.op("sync", lambda s: s.nop(), after=[s_.tok() for s_ in Sem.REGISTRY if s_.n > 0])
        with nc.Block() as block:
            R.emit(block)
        nc.finalize()
        return nc, c
    t_st_final = []
    t_mt_cons = {}
    prev_pass_last = None
    for pas in range(2):
        rcs = list(range(pas * (RC - RC // 2) if False else pas * (RC // 2), (pas + 1) * (RC // 2)))
        if RC == 1:
            rcs = [0] if pas == 0 else []
        t_acc = {}
        for sc in range(SC if rcs else 0):
            e_of = sc // NT
            gate = [t_stk, t_sw, t_iof, t_o2e[e_of]]
            if (sc % 2) in t_mt_cons:
                gate.append(t_mt_cons[sc % 2])
            if sc == 0 and pas == 0:
                gate.append(t_o2e[E - 1])
            if sc == 0 and prev_pass_last is not None:
                gate.append(prev_pass_last)
            tb = R.op("vector", lambda v, sc=sc: v.tensor_scalar(
                moe["Mt"][:, sc % 2, :], iota_f, stokc[:, sc:sc + 1], swc[:, sc:sc + 1],
                ALU.is_equal, ALU.mult), after=gate, inc=sv)
            last = None
            for r in rcs:
                for hf in range(2):
                    last = R.op("tensor", lambda t, sc=sc, r=r, hf=hf: t.matmul(
                        psf((r % max(1, RC // 2)) * 1024, 2, 512)[:, hf, :],
                        moe["Mt"][:, sc % 2, 128 * r:128 * (r + 1)],
                        moe["out2"][:, sc, 512 * hf:512 * (hf + 1)],
                        start=(sc == 0), stop=(sc == SC - 1)),
                        after=[tb], inc=st_)
                    t_acc[r] = last
            t_mt_cons[sc % 2] = last
        for r in rcs:
            txi = R.op("sync", lambda s, r=r: s.dma_start(
                out=moe["xmi"][:, r % 2, :], in_=xmhb[128 * r:128 * (r + 1), :]),
                after=[d_xmp[0].tok(), d_xmp[1].tok()] + ([t_st_final[-2][0]] if len(t_st_final) >= 2 else []),
                inc=d_xi[r % 2], dma=True)
            tv = R.op("vector", lambda v, r=r: v.tensor_tensor(
                moe["xo"][:, r % 2, :], psf((r % max(1, RC // 2)) * 1024, 1024), gbr, ALU.mult),
                after=[t_acc[r], t_gb] + ([t_st_final[-2][1]] if len(t_st_final) >= 2 else []),
                inc=sv)
            tv2 = R.op("vector", lambda v, r=r: v.tensor_tensor(
                moe["xo"][:, r % 2, :], moe["xo"][:, r % 2, :], moe["xmi"][:, r % 2, :], ALU.add),
                after=[tv, txi], inc=sv)
            tso = R.op("sync", lambda s, r=r: s.dma_start(
                out=out[128 * r:128 * (r + 1), :], in_=moe["xo"][:, r % 2, :]),
                after=[tv2], inc=d_op[r % 2], dma=True)
            t_st_final.append((tv2, tso))
        if rcs:
            prev_pass_last = t_acc[rcs[-1]]

    R.op("sync", lambda s: s.nop(), after=[s_.tok() for s_ in Sem.REGISTRY if s_.n > 0])

    with nc.Block() as block:
        R.emit(block)
    nc.finalize()
    return nc, c


_PROG_CACHE = {}


def _get_program(key, cfg):
    if key not in _PROG_CACHE:
        _PROG_CACHE[key] = build_program(cfg)
    return _PROG_CACHE[key]


def host_prep_core(cfg, top_idx, combine, row_base):
    c = _derived(cfg)
    E, REG, CAP, SC, S, NROW = c["E"], c["REG"], c["CAP"], c["SC"], c["S"], c["NROW"]
    gidx = np.zeros((16, E * (REG // 16)), np.int16)
    stok = np.full(S, -1.0, np.float32)
    sw = np.zeros(S, np.float32)
    for e in range(E):
        col = np.asarray(top_idx[:, e])
        locs = np.flatnonzero((col >= row_base) & (col < row_base + NROW))
        toks = (col[locs] - row_base).astype(np.int64)
        n = len(toks)
        if n > CAP:
            raise RuntimeError(f"slot capacity overflow e={e}: {n} > {CAP}")
        base = e * REG
        for i in range(n):
            j = base + i
            gidx[j % 16, j // 16] = np.int16(toks[i])
            stok[j] = np.float32(toks[i])
            sw[j] = np.float32(combine[int(col[locs[i]]), e])
    return (np.tile(gidx, (8, 1)),
            np.ascontiguousarray(stok.reshape(SC, 128).T),
            np.ascontiguousarray(sw.reshape(SC, 128).T))


def kernel(L0_attn_x, L0_attn_q_w, L0_attn_kv_w, L0_attn_proj_w, L0_attn_proj_b,
           L0_ffn_combine_weights, L0_ffn_top_idx, L0_ffn_W1, L0_ffn_b1,
           L0_ffn_W2, L0_ffn_b2, L0_c, L0_wmod):
    cfg = FULL_CFG
    c = _derived(cfg)
    ncores = 8
    L = 16384
    x = np.ascontiguousarray(np.asarray(L0_attn_x, dtype=np.float32)).reshape(-1, c["D"])
    top_idx = np.asarray(L0_ffn_top_idx)
    combine = np.asarray(L0_ffn_combine_weights, dtype=np.float32)
    nc, _ = _get_program("full", cfg)

    shared = dict(
        qw=np.ascontiguousarray(np.asarray(L0_attn_q_w, np.float32)),
        kvw=np.ascontiguousarray(np.asarray(L0_attn_kv_w, np.float32)),
        pjw=np.ascontiguousarray(np.asarray(L0_attn_proj_w, np.float32)),
        pjb=np.ascontiguousarray(np.asarray(L0_attn_proj_b, np.float32)).reshape(1, -1),
        c2=np.ascontiguousarray(np.asarray(L0_c, np.float32)),
        wmod=np.ascontiguousarray(np.asarray(L0_wmod, np.float32)),
        w1=np.ascontiguousarray(np.asarray(L0_ffn_W1, np.float32)),
        b1t=np.ascontiguousarray(np.asarray(L0_ffn_b1, np.float32).T),
        w2=np.ascontiguousarray(np.asarray(L0_ffn_W2, np.float32)),
        b2=np.ascontiguousarray(np.asarray(L0_ffn_b2, np.float32)),
    )
    in_maps, row_maps = [], []
    for core in range(ncores):
        rows = np.concatenate([
            np.arange(1024 * core, 1024 * core + 1024),
            np.arange(8192 + 1024 * core, 8192 + 1024 * core + 1024),
            np.arange(L + 2048 * core, L + 2048 * core + 2048)])
        row_maps.append(rows)
        gi, stk, sw = host_prep_core(cfg, top_idx, combine, 1024 * core)
        m = dict(shared)
        m["x_rows"] = np.ascontiguousarray(x[rows])
        m["gidx"] = gi
        m["stok"] = stk
        m["swt"] = sw
        in_maps.append(m)

    res = run_bass_kernel_spmd(nc, in_maps, core_ids=list(range(ncores)))
    outp = np.empty((2 * L, c["D"]), np.float32)
    for core in range(ncores):
        outp[row_maps[core]] = res.results[core]["out_rows"]
    return outp.reshape(2, L, c["D"])
